# version 10
# baseline (speedup 1.0000x reference)
from contextlib import ExitStack
import numpy as np
import concourse.bass as bass
import concourse.mybir as mybir
from concourse.bass_utils import run_bass_kernel_spmd

F32 = mybir.dt.float32
BF16 = mybir.dt.bfloat16
I32 = mybir.dt.int32
U8 = mybir.dt.uint8
AF = mybir.ActivationFunctionType
ALU = mybir.AluOpType
AX = mybir.AxisListType

NCORES = 8
ROWS = 4096
D = 1024
LN_EPS = 1e-5


class Chan:
    def __init__(self, es, nc, name):
        self.sem = es.enter_context(nc.semaphore(name))
        self.n = 0

    def inc(self, instr, amt=1):
        instr.then_inc(self.sem, amt)
        self.n += amt
        return instr

    def wt(self, eng, n=None):
        eng.wait_ge(self.sem, self.n if n is None else n)


class B:
    def __init__(self):
        self.nc = bass.Bass("TRN2", target_bir_lowering=False)
        self.es = ExitStack()
        self._ch = {}
        self._uid = 0
        self.fused = False
        self.prefix = ""
        self.block = None
        self._scopes = []
        self._dreg = {}
        self._input_names = set()

    def _nm(self, pfx):
        self._uid += 1
        return f"{self.prefix}{pfx}{self._uid}"

    def ch(self, name):
        name = self.prefix + name
        if name not in self._ch:
            self._ch[name] = Chan(self.es, self.nc, name)
        return self._ch[name]

    def _alloc_ctx(self):
        return self._scopes[-1] if self._scopes else self.es

    def sb(self, shape, dt=F32, name=None):
        return self._alloc_ctx().enter_context(
            self.nc.sbuf_tensor(name or self._nm("sb"), shape, dt))

    def psum(self, shape, dt=F32, name=None):
        return self._alloc_ctx().enter_context(
            self.nc.psum_tensor(name or self._nm("ps"), shape, dt))

    def din(self, name, shape, dt=F32):
        if self.fused:
            key = self.prefix + name
            if key in self._dreg:
                return self._dreg[key]
            self._input_names.add(key)
            ap = self.nc.dram_tensor(key, shape, dt, kind="ExternalInput").ap()
            self._dreg[key] = ap
            return ap
        return self.nc.dram_tensor(name, shape, dt, kind="ExternalInput").ap()

    def reg(self, key, shape, dt=F32, kind="Internal"):
        ap = self.nc.dram_tensor(key, shape, dt, kind=kind).ap()
        self._dreg[key] = ap
        return ap

    def dout(self, name, shape, dt=F32):
        if self.fused:
            key = self.prefix + name
            if key in self._dreg:
                return self._dreg[key]
            ap = self.nc.dram_tensor(key, shape, dt, kind="Internal").ap()
            self._dreg[key] = ap
            return ap
        return self.nc.dram_tensor(name, shape, dt, kind="ExternalOutput").ap()

    def dtmp(self, name, shape, dt=F32):
        if self.fused:
            name = self.prefix + name
        return self.nc.dram_tensor(name, shape, dt, kind="Internal").ap()

    def seg(self):
        if not self.fused:
            return self.es
        scope = ExitStack()
        self._scopes.append(scope)
        outer = self

        class _Seg:
            def __enter__(self):
                return scope.__enter__()

            def __exit__(self, *a):
                outer._scopes.pop()
                return scope.__exit__(*a)
        return _Seg()

    def block_open(self):
        if self.fused:
            return self.block
        self.block = self.es.enter_context(self.nc.Block())
        return self.block


def emit_ln_subtile(b, block, xr_s, xn_out, sq, stat, ld_ch, ln_ch, eps_ap=None,
                    gamma=None):
    Dd = xr_s.shape[-1]

    @block.scalar
    def _(scalar):
        ld_ch.wt(scalar)
        ln_ch.wt(scalar)
        ln_ch.inc(scalar.activation(sq[:, 0:Dd], xr_s, AF.Square,
                                    accum_out=stat[:, 0:1]))

    @block.vector
    def _(vector):
        ld_ch.wt(vector)
        ln_ch.wt(vector)
        ln_ch.inc(vector.reduce_sum(stat[:, 1:2], xr_s, axis=AX.X))
        ln_ch.wt(vector)
        ln_ch.inc(vector.tensor_scalar(stat[:, 2:3], stat[:, 1:2], 1.0 / Dd, None, ALU.mult))
        ln_ch.wt(vector)
        ln_ch.inc(vector.tensor_scalar(stat[:, 3:4], stat[:, 0:1], 1.0 / Dd, None, ALU.mult))
        ln_ch.wt(vector)
        ln_ch.inc(vector.tensor_tensor(stat[:, 4:5], stat[:, 2:3], stat[:, 2:3], ALU.mult))
        ln_ch.wt(vector)
        ln_ch.inc(vector.tensor_tensor(stat[:, 4:5], stat[:, 3:4], stat[:, 4:5], ALU.subtract))

    @block.scalar
    def _(scalar):
        ln_ch.wt(scalar)
        ln_ch.inc(scalar.activation(stat[:, 5:6], stat[:, 4:5], AF.Ln, bias=eps_ap))
        ln_ch.wt(scalar)
        ln_ch.inc(scalar.activation(stat[:, 5:6], stat[:, 5:6], AF.Exp, scale=-0.5))

    @block.vector
    def _(vector):
        ln_ch.wt(vector)
        ln_ch.inc(vector.tensor_scalar(xn_out, xr_s, stat[:, 2:3], stat[:, 5:6],
                                       ALU.subtract, ALU.mult))


def emit_tm_to_fm(b, block, xn_tm_bf, fm_dst_cols, identb, ps_t, mm_ch, ev_ch,
                  kd=8, evac_engine="vector", ready_ch=None):
    slots = [ps_t[0][:, 0:128], ps_t[1][:, 0:128]]
    for c in range(kd):
        slot = slots[c % 2]

        @block.tensor
        def _(tensor, c=c, slot=slot):
            if c == 0:
                if ready_ch is not None:
                    ready_ch.wt(tensor)
                if ev_ch.n > 0:
                    ev_ch.wt(tensor)
            elif c >= 2:
                ev_ch.wt(tensor, ev_ch.n - 1)
            mm_ch.inc(tensor.transpose(slot, xn_tm_bf[:, c * 128:(c + 1) * 128], identb))

        eng = getattr(block, evac_engine)

        @eng
        def _(e, c=c, slot=slot):
            mm_ch.wt(e)
            ev_ch.inc(e.tensor_copy(fm_dst_cols(c), slot))


def simrun(b, in_map, out_names, fp64=False):
    from concourse.bass_interp import MultiCoreSim
    sim = MultiCoreSim(b.nc, num_cores=1, num_workers=1, trace=False)
    for k, v in in_map.items():
        sim.cores[0].tensor(k)[:] = v
    sim.simulate(check_with_hw=False)
    return {k: np.asarray(sim.cores[0].tensor(k)) for k in out_names}


"""Per-sub-block launch builders. Each build_X() returns (b, meta) where the
program computes resid_out = resid + block(LN(resid)) for its shard.

Row sharding (all launches): core c owns token range t in [c*2048,(c+1)*2048)
of BOTH batches: rows = concat(b0[tr], b1[tr]) -> [4096, 1024].
"""
import numpy as np

TT = 512
NT = ROWS // TT
KD = D // 128
KF = 4096 // 128


class Skel:

    def __init__(self, b, block, resid, out):
        self.b = b
        self.block = block
        self.resid = resid
        self.out = out
        self.ldx = b.ch("ldx"); self.ln = b.ch("ln"); self.tr = b.ch("tr")
        self.fmc = b.ch("fmc"); self.ad = b.ch("ad"); self.sto = b.ch("sto")
        self.aux = b.ch("aux"); self.ld0 = b.ch("ld0")
        self.idf = b.sb([128, 128])
        self.identb = b.sb([128, 128], BF16)
        self.epsb = b.sb([128, 1])
        self.ones128 = b.sb([128, 1])
        self.ones_row = b.sb([1, 128])
        self.xr4 = b.sb([128, 4 * D])
        self.sq = b.sb([128, D])
        self.stat = b.sb([128, 16])
        self.xnb = b.sb([128, D], BF16)
        self.vsc = b.sb([32, 128])
        self.xfm = b.sb([128, KD * TT], BF16)
        self.ps_t = (b.psum([128, 128], BF16), b.psum([128, 128], BF16))
        self.ps_tf = (b.psum([128, 128]), b.psum([128, 128]))

    def consts(self, identf_ap):
        b, block = self.b, self.block
        ld0, aux, ln = self.ld0, self.aux, self.ln

        @block.sync
        def _(sync):
            ld0.inc(sync.dma_start(out=self.idf[:], in_=identf_ap), 16)

        @block.vector
        def _(vector):
            ld0.wt(vector)
            aux.inc(vector.tensor_copy(self.identb[:], self.idf[:]))
            aux.inc(vector.memset(self.epsb[:], LN_EPS))
            aux.inc(vector.memset(self.ones128[:], 1.0))
            aux.inc(vector.memset(self.ones_row[:], 1.0))

        @block.scalar
        def _(scalar):
            aux.wt(scalar)
            ln.inc(scalar.nop())

        @block.vector
        def _(vector):
            aux.wt(vector)
            ln.inc(vector.engine_nop())

    def load_rows(self, t, war_chs=()):
        b, block = self.b, self.block
        skel = self

        @block.sync
        def _(sync, t=t):
            for ch in war_chs:
                ch.wt(sync)
            if t >= 1:
                skel.sto.wt(sync)
            rows = skel.resid[t * TT:(t + 1) * TT, :].rearrange("(s p) d -> p s d", p=128)
            skel.ldx.inc(sync.dma_start(
                out=skel.xr4.rearrange("p (s d) -> p s d", s=4), in_=rows), 16)

    def ln_fmize(self, t, mm_war_ch=None):
        b, block, skel = self.b, self.block, self
        for s in range(4):
            if s == 0 and t >= 1 and mm_war_ch is not None:
                @block.vector
                def _(vector):
                    mm_war_ch.wt(vector)
                    skel.ln.inc(vector.engine_nop())
            emit_ln_subtile(b, block, skel.xr4[:, s * D:(s + 1) * D], skel.xnb[:],
                            skel.sq, skel.stat, skel.ldx, skel.ln, eps_ap=skel.epsb[:])
            emit_tm_to_fm(b, block, skel.xnb[:],
                          lambda c, s=s: skel.xfm[:, c * TT + s * 128: c * TT + (s + 1) * 128],
                          self.identb[:], self.ps_t, skel.tr, skel.fmc, ready_ch=skel.ln)

    def epilogue(self, t, ofm, ready_ch):
        b, block, skel = self.b, self.block, self
        for o in range(KD):
            for s in range(4):
                i = o * 4 + s
                pstf = self.ps_tf[i % 2]

                @block.tensor
                def _(tensor, o=o, s=s, i=i, pstf=pstf, t=t):
                    if i == 0:
                        ready_ch.wt(tensor)
                        if t >= 1:
                            skel.ad.wt(tensor)
                    elif i >= 2:
                        skel.ad.wt(tensor, skel.ad.n - 1)
                    skel.tr.inc(tensor.transpose(
                        pstf[:], ofm[:, o * TT + s * 128: o * TT + (s + 1) * 128], skel.idf[:]))

                @block.vector
                def _(vector, o=o, s=s, i=i, pstf=pstf):
                    if i == 0:
                        skel.ldx.wt(vector)
                    skel.tr.wt(vector)
                    dst = skel.xr4[:, s * D + o * 128: s * D + (o + 1) * 128]
                    skel.ad.inc(vector.tensor_tensor(dst, dst, pstf[:], ALU.add))

        @block.sync
        def _(sync, t=t):
            skel.ad.wt(sync)
            orow = skel.out[t * TT:(t + 1) * TT, :].rearrange("(s p) d -> p s d", p=128)
            skel.sto.inc(sync.dma_start(
                out=orow, in_=skel.xr4.rearrange("p (s d) -> p s d", s=4)), 16)

    def finish(self):
        block, skel = self.block, self

        @block.sync
        def _(sync):
            skel.sto.wt(sync)


def emit_vec_load(b, block, dst_pc, src_vec_ap, n, skel, ps_small, ch):
    nch = n // 128

    @block.sync
    def _(sync):
        ch.wt(sync)
        skel.tr.wt(sync)
        ch.inc(sync.dma_start(out=skel.vsc[0:nch, 0:128], in_=src_vec_ap), 16)

    @block.tensor
    def _(tensor):
        ch.wt(tensor)
        skel.tr.inc(tensor.transpose(ps_small[:, 0:nch], skel.vsc[0:nch, 0:128],
                                     skel.idf[0:nch, 0:nch]))

    @block.vector
    def _(vector):
        skel.tr.wt(vector)
        ch.inc(vector.tensor_copy(dst_pc[:, 0:nch], ps_small[:, 0:nch]))


def emit_stream_matmul(b, block, skel, *, src_fm, src_tt, kc, fc, wtiles, wbufs,
                       psums, chs, evac, t, first_tile, wait_src_ch, wready_ch=None):
    wld = [chs["wldA"], chs["wldB"]]
    for f in range(fc):
        wb = wbufs[f % 2]
        pst = psums[f % 2]
        lch = wld[f % 2]

        @block.sync
        def _(sync, f=f, wb=wb, lch=lch):
            if not (first_tile and f < 2):
                chs["mm"].wt(sync, max(0, chs["mm"].n - 1))
            if wready_ch is not None and first_tile and f < 2:
                wready_ch.wt(sync)
            lch.inc(sync.dma_start(
                out=wb[:, 0:kc * 128].rearrange("p (c q) -> p c q", q=128),
                in_=wtiles[:, f].rearrange("a p q -> p a q")), 16)

        @block.tensor
        def _(tensor, f=f, wb=wb, pst=pst, lch=lch, t=t):
            lch.wt(tensor)
            if f == 0:
                wait_src_ch.wt(tensor)
                if not first_tile:
                    chs["ev"].wt(tensor)
            elif f >= 2:
                chs["ev"].wt(tensor, chs["ev"].n - 1)
            for c in range(kc):
                ins = tensor.matmul(pst[:], wb[:, c * 128:(c + 1) * 128],
                                    src_fm[:, c * src_tt:(c + 1) * src_tt],
                                    start=(c == 0), stop=(c == kc - 1))
            chs["mm"].inc(ins)

        evac(f, pst)


def shard_rows(x2d):
    a = x2d.reshape(2, 16384, -1)
    return [np.ascontiguousarray(
        np.concatenate([a[0, c * 2048:(c + 1) * 2048], a[1, c * 2048:(c + 1) * 2048]], 0))
        for c in range(NCORES)]


def unshard_rows(parts):
    b0 = np.concatenate([p[:2048] for p in parts], 0)
    b1 = np.concatenate([p[2048:] for p in parts], 0)
    return np.stack([b0, b1], 0)


def tiles4(w, kc, fc):
    return np.ascontiguousarray(
        w.reshape(kc, 128, fc, 128).transpose(0, 2, 1, 3))


def build_l2ffn(b=None):
    if b is None:
        b = B()
    b.prefix = "l2ffn_"
    nc = b.nc
    resid = b.din("resid", [ROWS, D])
    wt_in = {}
    for nm, kc, fc in (("w1", 8, 32), ("w2", 8, 32), ("h1", 32, 32),
                       ("h2", 32, 32), ("w3", 32, 8)):
        wt_in[nm] = b.din(nm + "t", [kc, fc, 128, 128])
    sv_in = {nm: b.din(nm + "s", [n // 128, 128]) for nm, n in
             (("w1", 4096), ("w2", 4096), ("h1", 4096), ("h2", 4096), ("w3", 1024))}
    identf = b.din("identf", [128, 128])
    out = b.dout("out", [ROWS, D])
    wbt = {nm: b.dtmp(nm + "b", [kc, fc, 128, 128], BF16) for nm, kc, fc in
           (("w1", 8, 32), ("w2", 8, 32), ("h1", 32, 32), ("h2", 32, 32), ("w3", 32, 8))}

    with b.seg():
        block = b.block_open()
        skel = Skel(b, block, resid, out)
        wp = b.ch("wp"); lds = b.ch("lds")
        chs = {k: b.ch(k) for k in ("wldA", "wldB", "mm", "ev")}
        skel.consts(identf)

        @block.gpsimd
        def _(g):
            for nm in ("w1", "w2", "h1", "h2", "w3"):
                wp.inc(g.dma_start(out=wbt[nm].rearrange("a b p q -> (a b p) q"),
                                   in_=wt_in[nm].rearrange("a b p q -> (a b p) q")), 16)

        svs = {nm: b.sb([128, 32]) for nm in ("w1", "w2", "h1", "h2")}
        svs["w3"] = b.sb([128, 8])
        ps_small = b.psum([128, 32])
        for nm in ("w1", "w2", "h1", "h2", "w3"):
            emit_vec_load(b, block, svs[nm], sv_in[nm], svs[nm].shape[1] * 128,
                          skel, ps_small, lds)

        hbuf1 = b.sb([128, 32 * TT], BF16)
        hbuf2 = b.sb([128, 32 * TT], BF16)
        ofm = b.sb([128, 8 * TT])
        wbufs = (b.sb([128, 32 * 128], BF16), b.sb([128, 32 * 128], BF16))
        psums = (b.psum([128, TT]), b.psum([128, TT]))

        for t in range(NT):
            skel.load_rows(t, war_chs=(skel.ad,) if t else (chs["mm"], wp))
            skel.ln_fmize(t, mm_war_ch=chs["mm"])

            def ev_s1a(f, pst, t=t):
                @block.scalar
                def _(scalar, f=f, pst=pst):
                    chs["mm"].wt(scalar)
                    if f == 0:
                        lds.wt(scalar)
                    chs["ev"].inc(scalar.activation(
                        hbuf1[:, f * TT:(f + 1) * TT], pst[:], AF.Silu,
                        scale=svs["w1"][:, f:f + 1]))
            emit_stream_matmul(b, block, skel, src_fm=skel.xfm, src_tt=TT, kc=8, fc=32,
                               wtiles=wbt["w1"], wbufs=wbufs, psums=psums, chs=chs,
                               evac=ev_s1a, t=t, first_tile=(t == 0), wait_src_ch=skel.fmc,
                               wready_ch=wp)

            def ev_s1b(f, pst, t=t):
                @block.vector
                def _(vector, f=f, pst=pst):
                    chs["mm"].wt(vector)
                    chs["ev"].inc(vector.scalar_tensor_tensor(
                        hbuf1[:, f * TT:(f + 1) * TT], pst[:], svs["w2"][:, f:f + 1],
                        hbuf1[:, f * TT:(f + 1) * TT], ALU.mult, ALU.mult))
            emit_stream_matmul(b, block, skel, src_fm=skel.xfm, src_tt=TT, kc=8, fc=32,
                               wtiles=wbt["w2"], wbufs=wbufs, psums=psums, chs=chs,
                               evac=ev_s1b, t=t, first_tile=False, wait_src_ch=chs["ev"])

            def ev_s2(f, pst, t=t):
                @block.scalar
                def _(scalar, f=f, pst=pst):
                    chs["mm"].wt(scalar)
                    chs["ev"].inc(scalar.activation(
                        hbuf2[:, f * TT:(f + 1) * TT], pst[:], AF.Gelu,
                        scale=svs["h1"][:, f:f + 1]))
            emit_stream_matmul(b, block, skel, src_fm=hbuf1, src_tt=TT, kc=32, fc=32,
                               wtiles=wbt["h1"], wbufs=wbufs, psums=psums, chs=chs,
                               evac=ev_s2, t=t, first_tile=False, wait_src_ch=chs["ev"])

            def ev_s3(f, pst, t=t):
                @block.scalar
                def _(scalar, f=f, pst=pst):
                    chs["mm"].wt(scalar)
                    chs["ev"].inc(scalar.activation(
                        hbuf1[:, f * TT:(f + 1) * TT], pst[:], AF.Gelu,
                        scale=svs["h2"][:, f:f + 1]))
            emit_stream_matmul(b, block, skel, src_fm=hbuf2, src_tt=TT, kc=32, fc=32,
                               wtiles=wbt["h2"], wbufs=wbufs, psums=psums, chs=chs,
                               evac=ev_s3, t=t, first_tile=False, wait_src_ch=chs["ev"])

            def ev_s4(f, pst, t=t):
                @block.vector
                def _(vector, f=f, pst=pst):
                    chs["mm"].wt(vector)
                    chs["ev"].inc(vector.tensor_scalar(
                        ofm[:, f * TT:(f + 1) * TT], pst[:], svs["w3"][:, f:f + 1],
                        None, ALU.mult))
            emit_stream_matmul(b, block, skel, src_fm=hbuf1, src_tt=TT, kc=32, fc=8,
                               wtiles=wbt["w3"], wbufs=wbufs, psums=psums, chs=chs,
                               evac=ev_s4, t=t, first_tile=False, wait_src_ch=chs["ev"])

            skel.epilogue(t, ofm, chs["ev"])
        skel.finish()
    return b


def inmaps_l2ffn(P, resid_full):
    parts = shard_rows(resid_full)
    base = {
        "w1t": tiles4(np.ascontiguousarray(P["L2_ffn_w1_w"].T), 8, 32),
        "w2t": tiles4(np.ascontiguousarray(P["L2_ffn_w2_w"].T), 8, 32),
        "h1t": tiles4(np.ascontiguousarray(P["L2_ffn_h1_w"].T), 32, 32),
        "h2t": tiles4(np.ascontiguousarray(P["L2_ffn_h2_w"].T), 32, 32),
        "w3t": tiles4(np.ascontiguousarray(P["L2_ffn_w3_w"].T), 32, 8),
        "w1s": P["L2_ffn_w1_s"].reshape(32, 128),
        "w2s": P["L2_ffn_w2_s"].reshape(32, 128),
        "h1s": P["L2_ffn_h1_s"].reshape(32, 128),
        "h2s": P["L2_ffn_h2_s"].reshape(32, 128),
        "w3s": P["L2_ffn_w3_s"].reshape(8, 128),
        "identf": np.eye(128, dtype=np.float32),
    }
    return [dict(base, resid=parts[c]) for c in range(NCORES)]


def build_l3ffn(b=None):
    if b is None:
        b = B()
    b.prefix = "l3ffn_"
    nc = b.nc
    resid = b.din("resid", [ROWS, D])
    k0top = b.din("k0top", [8, 32, 128, 128], U8)
    k0bot = b.din("k0bot", [8, 32, 128, 128], U8)
    k1t = b.din("k1t", [32, 32, 128, 128], U8)
    k2t = b.din("k2t", [32, 8, 128, 128], U8)
    s0 = b.din("s0", [1, 1]); s1 = b.din("s1", [1, 1]); s2 = b.din("s2", [1, 1])
    identf = b.din("identf", [128, 128])
    out = b.dout("out", [ROWS, D])
    w0b = b.dtmp("w0b", [8, 32, 128, 128], BF16)
    w1b = b.dtmp("w1b", [32, 32, 128, 128], BF16)
    w2b = b.dtmp("w2b", [32, 8, 128, 128], BF16)

    with b.seg():
        block = b.block_open()
        skel = Skel(b, block, resid, out)
        wp = b.ch("wp"); wl = b.ch("wl"); lds = b.ch("lds"); wst = b.ch("wst")
        chs = {k: b.ch(k) for k in ("wldA", "wldB", "mm", "ev")}
        skel.consts(identf)

        svs = {}
        ps_small = b.psum([128, 4])
        sctmp = b.sb([1, 4])
        for i, (nm, sap) in enumerate((("s0", s0), ("s1", s1), ("s2", s2))):
            svs[nm] = b.sb([128, 1])

            @block.sync
            def _(sync, sap=sap, i=i):
                lds.inc(sync.dma_start(out=sctmp[0:1, i:i + 1], in_=sap), 16)

        @block.vector
        def _(vector):
            lds.wt(vector)
            skel.aux.inc(vector.tensor_scalar(sctmp[0:1, 0:3], sctmp[0:1, 0:3],
                                              2.0, None, ALU.mult))

        @block.tensor
        def _(tensor):
            skel.aux.wt(tensor)
            for i in range(3):
                ins = tensor.matmul(ps_small[:, i:i + 1], skel.ones_row[:],
                                    sctmp[0:1, i:i + 1], start=True, stop=True)
            skel.tr.inc(ins)

        @block.vector
        def _(vector):
            skel.tr.wt(vector)
            for i, nm in enumerate(("s0", "s1", "s2")):
                skel.aux.inc(vector.tensor_copy(svs[nm][:], ps_small[:, i:i + 1]))

        u8a = b.sb([128, 32 * 128], U8)
        u8b = b.sb([128, 32 * 128], U8)
        wcv = b.sb([128, 32 * 128], BF16)
        for kc in range(8):
            @block.sync
            def _(sync, kc=kc):
                wp.wt(sync)
                wl.wt(sync)
                wl.inc(sync.dma_start(out=u8a.rearrange("p (c q) -> p c q", q=128),
                                      in_=k0top[kc]), 16)
                wl.inc(sync.dma_start(out=u8b.rearrange("p (c q) -> p c q", q=128),
                                      in_=k0bot[kc]), 16)

            @block.vector
            def _(vector, kc=kc):
                wl.wt(vector)
                wst.wt(vector)
                wp.inc(vector.tensor_tensor(wcv[:], u8a[:], u8b[:], ALU.subtract))

            @block.gpsimd
            def _(g, kc=kc):
                wp.wt(g)
                wst.inc(g.dma_start(out=w0b[kc].rearrange("c p q -> p c q"),
                                   in_=wcv.rearrange("p (c q) -> p c q", q=128)), 16)

        for (src, dst, nkc, nfc) in ((k1t, w1b, 32, 32), (k2t, w2b, 32, 8)):
            for kc in range(nkc):
                @block.sync
                def _(sync, src=src, kc=kc, nfc=nfc):
                    wp.wt(sync)
                    wl.wt(sync)
                    wl.inc(sync.dma_start(
                        out=u8a[:, 0:nfc * 128].rearrange("p (c q) -> p c q", q=128),
                        in_=src[kc]), 16)

                @block.vector
                def _(vector, kc=kc, nfc=nfc):
                    wl.wt(vector)
                    wst.wt(vector)
                    wp.inc(vector.tensor_scalar(wcv[:, 0:nfc * 128], u8a[:, 0:nfc * 128],
                                                2.0, -1.0, ALU.mult, ALU.add))

                @block.gpsimd
                def _(g, dst=dst, kc=kc, nfc=nfc):
                    wp.wt(g)
                    wst.inc(g.dma_start(
                        out=dst[kc].rearrange("c p q -> p c q"),
                        in_=wcv[:, 0:nfc * 128].rearrange("p (c q) -> p c q", q=128)), 16)

        hbuf1 = b.sb([128, 32 * TT], BF16)
        hbuf2 = b.sb([128, 32 * TT], BF16)
        ofm = b.sb([128, 8 * TT])
        wbufs = (b.sb([128, 32 * 128], BF16), b.sb([128, 32 * 128], BF16))
        psums = (b.psum([128, TT]), b.psum([128, TT]))

        for t in range(NT):
            skel.load_rows(t, war_chs=(skel.ad,) if t else (chs["mm"], wp))
            skel.ln_fmize(t, mm_war_ch=chs["mm"])

            def ev_s0(f, pst, t=t):
                @block.scalar
                def _(scalar, f=f, pst=pst):
                    chs["mm"].wt(scalar)
                    if f == 0:
                        skel.aux.wt(scalar)
                    chs["ev"].inc(scalar.activation(
                        hbuf1[:, f * TT:(f + 1) * TT], pst[:], AF.Relu,
                        scale=svs["s0"][:, 0:1]))
            emit_stream_matmul(b, block, skel, src_fm=skel.xfm, src_tt=TT, kc=8, fc=32,
                               wtiles=w0b, wbufs=wbufs, psums=psums, chs=chs,
                               evac=ev_s0, t=t, first_tile=(t == 0), wait_src_ch=skel.fmc,
                               wready_ch=wst)

            def ev_s1(f, pst, t=t):
                @block.scalar
                def _(scalar, f=f, pst=pst):
                    chs["mm"].wt(scalar)
                    chs["ev"].inc(scalar.activation(
                        hbuf2[:, f * TT:(f + 1) * TT], pst[:], AF.Relu,
                        scale=svs["s1"][:, 0:1]))
            emit_stream_matmul(b, block, skel, src_fm=hbuf1, src_tt=TT, kc=32, fc=32,
                               wtiles=w1b, wbufs=wbufs, psums=psums, chs=chs,
                               evac=ev_s1, t=t, first_tile=False, wait_src_ch=chs["ev"])

            def ev_s2(f, pst, t=t):
                @block.vector
                def _(vector, f=f, pst=pst):
                    chs["mm"].wt(vector)
                    chs["ev"].inc(vector.tensor_scalar(
                        ofm[:, f * TT:(f + 1) * TT], pst[:], svs["s2"][:, 0:1],
                        None, ALU.mult))
            emit_stream_matmul(b, block, skel, src_fm=hbuf2, src_tt=TT, kc=32, fc=8,
                               wtiles=w2b, wbufs=wbufs, psums=psums, chs=chs,
                               evac=ev_s2, t=t, first_tile=False, wait_src_ch=chs["ev"])

            skel.epilogue(t, ofm, chs["ev"])
        skel.finish()
    return b


def inmaps_l3ffn(P, resid_full):
    parts = shard_rows(resid_full)
    k0 = P["L3_ffn_k0"].view(np.uint8)
    base = {
        "k0top": tiles4(k0[0:1024], 8, 32),
        "k0bot": tiles4(k0[1024:2048], 8, 32),
        "k1t": tiles4(P["L3_ffn_k1"].view(np.uint8), 32, 32),
        "k2t": tiles4(P["L3_ffn_k2"].view(np.uint8), 32, 8),
        "s0": P["L3_ffn_s0"].reshape(1, 1),
        "s1": P["L3_ffn_s1"].reshape(1, 1),
        "s2": P["L3_ffn_s2"].reshape(1, 1),
        "identf": np.eye(128, dtype=np.float32),
    }
    return [dict(base, resid=parts[c]) for c in range(NCORES)]


def build_l0ffn(b=None):
    if b is None:
        b = B()
    b.prefix = "l0ffn_"
    nc = b.nc
    resid = b.din("resid", [ROWS, D])
    gate_w = b.din("gate_w", [D, 8])
    w0t = b.din("w0t", [D, D])
    w1t = b.din("w1t", [D, D])
    b0v = b.din("b0v", [8, 128])
    b1v = b.din("b1v", [8, 128])
    identf = b.din("identf", [128, 128])
    out = b.dout("out", [ROWS, D])

    with b.seg():
        block = b.block_open()
        skel = Skel(b, block, resid, out)
        lds = b.ch("lds"); mm = b.ch("mm"); ev = b.ch("ev"); gt = b.ch("gt")
        ldg = b.ch("ldg")
        skel.consts(identf)

        gw = b.sb([128, 8 * 8], BF16)
        w0 = b.sb([128, 8 * D], BF16)
        w1 = b.sb([128, 8 * D], BF16)
        b0s = b.sb([128, 8])
        b1s = b.sb([128, 8])

        @block.gpsimd
        def _(g):
            for c in range(8):
                ldg.inc(g.dma_start(out=gw[:, c * 8:(c + 1) * 8],
                                    in_=gate_w[c * 128:(c + 1) * 128, :]), 16)
            for c in range(8):
                ldg.inc(g.dma_start(out=w0[:, c * D:(c + 1) * D],
                                    in_=w0t[c * 128:(c + 1) * 128, :]), 16)
                ldg.inc(g.dma_start(out=w1[:, c * D:(c + 1) * D],
                                    in_=w1t[c * 128:(c + 1) * 128, :]), 16)

        emit_vec_load(b, block, b0s, b0v, 1024, skel, skel.ps_tf[0], lds)
        emit_vec_load(b, block, b1s, b1v, 1024, skel, skel.ps_tf[0], lds)

        e0fm = b.sb([128, KD * TT], BF16)
        e1fm = b.sb([128, KD * TT], BF16)
        ofm = b.sb([128, KD * TT])
        lg = b.sb([8, TT])
        g_tm = b.sb([128, 4 * 8])
        gs = b.sb([128, 12])
        c01 = b.sb([128, 4 * 2])
        c0r = b.sb([1, TT])
        c1r = b.sb([1, TT])
        neginf = b.sb([128, 8])
        msk = b.sb([128, 8], U8)
        ps_a = b.psum([128, TT])
        ps_b = b.psum([128, TT])
        ps_c0 = b.psum([128, TT])
        ps_c1 = b.psum([128, TT])

        @block.vector
        def _(vector):
            skel.aux.inc(vector.memset(neginf[:], -1e30))

        for t in range(NT):
            skel.load_rows(t, war_chs=(skel.ad,) if t else (mm, lds))
            skel.ln_fmize(t, mm_war_ch=mm)

            @block.tensor
            def _(tensor, t=t):
                skel.fmc.wt(tensor)
                if t == 0:
                    lds.wt(tensor)
                    ldg.wt(tensor)
                else:
                    ev.wt(tensor)
                for c in range(KD):
                    ins = tensor.matmul(ps_a[0:8, :], gw[:, c * 8:(c + 1) * 8],
                                        skel.xfm[:, c * TT:(c + 1) * TT],
                                        start=(c == 0), stop=(c == KD - 1))
                gt.inc(ins)

            @block.vector
            def _(vector):
                gt.wt(vector)
                gt.inc(vector.tensor_copy(lg[:], ps_a[0:8, :]))

            for s in range(4):
                @block.tensor
                def _(tensor, s=s):
                    gt.wt(tensor)
                    if s >= 2:
                        gt.wt(tensor, gt.n - 1)
                    skel.tr.inc(tensor.transpose(
                        skel.ps_tf[s % 2][:, 0:8], lg[0:8, s * 128:(s + 1) * 128],
                        skel.idf[0:8, 0:8]))

                @block.vector
                def _(vector, s=s):
                    skel.tr.wt(vector)
                    gt.inc(vector.tensor_copy(g_tm[:, s * 8:(s + 1) * 8],
                                              skel.ps_tf[s % 2][:, 0:8]))

            for s in range(4):
                gsl = g_tm[:, s * 8:(s + 1) * 8]

                @block.vector
                def _(vector, s=s, gsl=gsl):
                    gt.wt(vector)
                    gt.inc(vector.reduce_max(gs[:, 0:1], gsl, axis=AX.X))
                    gt.wt(vector)
                    gt.inc(vector.tensor_scalar(gs[:, 1:2], gs[:, 0:1], -1.0, None, ALU.mult))

                @block.scalar
                def _(scalar, s=s, gsl=gsl):
                    gt.wt(scalar)
                    gt.inc(scalar.activation(skel.sq[:, 0:8], gsl, AF.Exp,
                                             bias=gs[:, 1:2]))

                @block.vector
                def _(vector, s=s):
                    gt.wt(vector)
                    gt.inc(vector.reduce_sum(gs[:, 2:3], skel.sq[:, 0:8], axis=AX.X))
                    gt.wt(vector)
                    gt.inc(vector.reciprocal(gs[:, 3:4], gs[:, 2:3]))
                    gt.wt(vector)
                    gt.inc(vector.tensor_scalar(skel.sq[:, 0:8], skel.sq[:, 0:8],
                                                gs[:, 3:4], None, ALU.mult))
                    gt.wt(vector)
                    gt.inc(vector.reduce_max(gs[:, 4:5], skel.sq[:, 0:8], axis=AX.X))
                    gt.wt(vector)
                    gt.inc(vector.tensor_scalar(c01[:, s * 2:s * 2 + 1], skel.sq[:, 0:1],
                                                gs[:, 4:5], gs[:, 4:5], ALU.is_equal, ALU.mult))
                    gt.wt(vector)
                    gt.inc(vector.tensor_scalar(skel.sq[:, 8:16], skel.sq[:, 0:8], gs[:, 4:5],
                                                1e30, ALU.is_ge, ALU.mult))
                    gt.wt(vector)
                    gt.inc(vector.tensor_tensor(skel.sq[:, 8:16], skel.sq[:, 0:8],
                                                skel.sq[:, 8:16], ALU.subtract))
                    gt.wt(vector)
                    gt.inc(vector.reduce_max(gs[:, 5:6], skel.sq[:, 8:16], axis=AX.X))
                    gt.wt(vector)
                    gt.inc(vector.tensor_scalar(c01[:, s * 2 + 1:s * 2 + 2], skel.sq[:, 9:10],
                                                gs[:, 5:6], gs[:, 5:6], ALU.is_equal, ALU.mult))

            for o in range(KD):
                pst = ps_a if o % 2 == 0 else ps_b

                @block.tensor
                def _(tensor, o=o, pst=pst, t=t):
                    if o < 2:
                        gt.wt(tensor)
                        if t >= 1:
                            ev.wt(tensor)
                    else:
                        ev.wt(tensor, ev.n - 2)
                    for c in range(KD):
                        ins = tensor.matmul(pst[:], w0[:, c * D + o * 128: c * D + (o + 1) * 128],
                                            skel.xfm[:, c * TT:(c + 1) * TT],
                                            start=(c == 0), stop=(c == KD - 1))
                    mm.inc(ins)

                @block.scalar
                def _(scalar, o=o, pst=pst):
                    mm.wt(scalar)
                    ev.inc(scalar.activation(e0fm[:, o * TT:(o + 1) * TT], pst[:],
                                             AF.Silu, bias=b0s[:, o:o + 1]))

                @block.tensor
                def _(tensor, o=o, pst=pst):
                    ev.wt(tensor)
                    for c in range(KD):
                        ins = tensor.matmul(pst[:], w1[:, c * D + o * 128: c * D + (o + 1) * 128],
                                            skel.xfm[:, c * TT:(c + 1) * TT],
                                            start=(c == 0), stop=(c == KD - 1))
                    mm.inc(ins)

                @block.scalar
                def _(scalar, o=o, pst=pst):
                    mm.wt(scalar)
                    ev.inc(scalar.activation(e1fm[:, o * TT:(o + 1) * TT], pst[:],
                                             AF.Silu, bias=b1s[:, o:o + 1]))

            for s in range(4):
                for j in range(2):
                    i = s * 2 + j
                    dstrow = (c0r if j == 0 else c1r)[0:1, s * 128:(s + 1) * 128]

                    @block.tensor
                    def _(tensor, s=s, j=j, i=i):
                        gt.wt(tensor) if i == 0 else gt.wt(tensor, gt.n - 1)
                        skel.tr.inc(tensor.transpose(skel.ps_tf[i % 2][0:1, 0:128],
                                                     c01[:, i:i + 1], skel.idf[:]))

                    @block.vector
                    def _(vector, i=i, dstrow=dstrow):
                        skel.tr.wt(vector)
                        gt.inc(vector.tensor_copy(dstrow, skel.ps_tf[i % 2][0:1, 0:128]))

            @block.tensor
            def _(tensor):
                gt.wt(tensor)
                ev.wt(tensor)
                ins = tensor.matmul(ps_c0[:], skel.ones_row[:], c0r[0:1, :],
                                    start=True, stop=True)
                ins = tensor.matmul(ps_c1[:], skel.ones_row[:], c1r[0:1, :],
                                    start=True, stop=True)
                mm.inc(ins)

            for o in range(KD):
                @block.vector
                def _(vector, o=o):
                    mm.wt(vector)
                    ev.inc(vector.tensor_tensor(ofm[:, o * TT:(o + 1) * TT],
                                                e0fm[:, o * TT:(o + 1) * TT],
                                                ps_c0[:], ALU.mult))
                    ev.wt(vector)
                    ev.inc(vector.tensor_tensor(skel.sq[:, 0:TT],
                                                e1fm[:, o * TT:(o + 1) * TT],
                                                ps_c1[:], ALU.mult))
                    ev.wt(vector)
                    ev.inc(vector.tensor_tensor(ofm[:, o * TT:(o + 1) * TT],
                                                ofm[:, o * TT:(o + 1) * TT],
                                                skel.sq[:, 0:TT], ALU.add))

            skel.epilogue(t, ofm, ev)
        skel.finish()
    return b


def inmaps_l0ffn(P, resid_full):
    parts = shard_rows(resid_full)
    base = {
        "gate_w": P["L0_ffn_gate_w"],
        "w0t": np.ascontiguousarray(P["L0_ffn_w0"].T),
        "w1t": np.ascontiguousarray(P["L0_ffn_w1"].T),
        "b0v": P["L0_ffn_b0"].reshape(8, 128),
        "b1v": P["L0_ffn_b1"].reshape(8, 128),
        "identf": np.eye(128, dtype=np.float32),
    }
    return [dict(base, resid=parts[c]) for c in range(NCORES)]


def build_l3attn_a(b=None):
    if b is None:
        b = B()
    b.prefix = "l3attn_a_"
    nc = b.nc
    resid = b.din("resid", [ROWS, D])
    wembt = b.din("wembt", [D, 2048])
    wat = b.din("wat", [64, 64])
    identf = b.din("identf", [128, 128])
    attnT = b.dout("attnT", [NT * 4, 128, 1024], BF16)
    av_out = b.dout("av_out", [128, 520])

    SCALE = 64.0 ** -0.5
    with b.seg():
        block = b.block_open()
        skel = Skel(b, block, resid, None)
        ldg = b.ch("ldg"); mm = b.ch("mm"); ev = b.ch("ev"); sc = b.ch("sc")
        av = b.ch("av"); sv = b.ch("sv"); pst_ch = b.ch("pst")
        skel.consts(identf)

        wemb = b.sb([128, KD * 2048], BF16)
        wa = b.sb([128, 64], BF16)

        @block.gpsimd
        def _(g):
            for c in range(KD):
                ldg.inc(g.dma_start(out=wemb[:, c * 2048:(c + 1) * 2048],
                                    in_=wembt[c * 128:(c + 1) * 128, :]), 16)
            ldg.inc(g.dma_start(out=wa[0:64, :], in_=wat), 16)
            ldg.inc(g.dma_start(out=wa[64:128, :], in_=wat), 16)

        emb = b.sb([128, 16 * TT], BF16)
        v_tm = b.sb([128, 4 * 1024], BF16)
        p_tm = b.sb([128, 64])
        pstats = b.sb([128, 8])
        att_sb = b.sb([128, 8 * 128], BF16)
        av_sb = b.sb([128, 512])
        s_sb = b.sb([128, 8])
        p_all = b.sb([128, 64 * 64], BF16)
        ps_pool = (b.psum([128, TT]), b.psum([128, TT]))
        ps_av = (b.psum([128, 64]), b.psum([128, 64]))

        @block.vector
        def _(vector):
            skel.aux.wt(vector)
            av.inc(vector.memset(av_sb[:], 0.0))
            sv.inc(vector.memset(s_sb[:], 0.0))

        for t in range(NT):
            skel.load_rows(t, war_chs=() if t else (mm, ldg))
            skel.ln_fmize(t, mm_war_ch=mm)

            for o in range(16):
                pp = ps_pool[o % 2]

                @block.tensor
                def _(tensor, o=o, pp=pp, t=t):
                    if o < 2:
                        skel.fmc.wt(tensor)
                        if t == 0:
                            ldg.wt(tensor)
                        else:
                            ev.wt(tensor)
                            sc.wt(tensor)
                    else:
                        ev.wt(tensor, ev.n - 1)
                    for c in range(KD):
                        ins = tensor.matmul(pp[:], wemb[:, c * 2048 + o * 128: c * 2048 + (o + 1) * 128],
                                            skel.xfm[:, c * TT:(c + 1) * TT],
                                            start=(c == 0), stop=(c == KD - 1))
                    mm.inc(ins)

                @block.vector
                def _(vector, o=o, pp=pp):
                    mm.wt(vector)
                    ev.inc(vector.tensor_copy(emb[:, o * TT:(o + 1) * TT], pp[:]))

            for s in range(4):
                for c in range(8):
                    i = s * 8 + c
                    pt = skel.ps_t[i % 2]

                    @block.tensor
                    def _(tensor, s=s, c=c, i=i, pt=pt, t=t):
                        if i == 0:
                            ev.wt(tensor)
                            if t >= 1:
                                pst_ch.wt(tensor)
                        else:
                            pst_ch.wt(tensor, pst_ch.n - 1)
                        mm.inc(tensor.transpose(
                            pt[:], emb[:, (8 + c) * TT + s * 128:(8 + c) * TT + (s + 1) * 128],
                            skel.identb[:]))

                    @block.vector
                    def _(vector, s=s, c=c, pt=pt):
                        mm.wt(vector)
                        pst_ch.inc(vector.tensor_copy(v_tm[:, s * 1024 + c * 128: s * 1024 + (c + 1) * 128], pt[:]))

            for s in range(4):
                for h in range(16):
                    j = h // 2; hh = h % 2
                    c = h // 2; r0 = (h % 2) * 64
                    pp = ps_pool[(s * 16 + h) % 2]

                    @block.tensor
                    def _(tensor, s=s, h=h, c=c, r0=r0, pp=pp):
                        pst_ch.wt(tensor)
                        sc.wt(tensor)
                        mm.inc(tensor.matmul(
                            pp[0:128, 0:64],
                            emb[r0:r0 + 64, c * TT + s * 128: c * TT + (s + 1) * 128],
                            wa[r0:r0 + 64, :], start=True, stop=True))

                    @block.scalar
                    def _(scalar, pp=pp):
                        mm.wt(scalar)
                        sc.inc(scalar.activation(p_tm[:], pp[0:128, 0:64], AF.Copy,
                                                 scale=SCALE))

                    @block.vector
                    def _(vector):
                        sc.wt(vector)
                        sc.inc(vector.reduce_max(pstats[:, 0:1], p_tm[:], axis=AX.X))
                        sc.wt(vector)
                        sc.inc(vector.tensor_scalar(pstats[:, 1:2], pstats[:, 0:1], -1.0, None, ALU.mult))

                    @block.scalar
                    def _(scalar):
                        sc.wt(scalar)
                        sc.inc(scalar.activation(p_tm[:], p_tm[:], AF.Exp, bias=pstats[:, 1:2]))

                    @block.vector
                    def _(vector, s=s, h=h):
                        sc.wt(vector)
                        sc.inc(vector.reduce_sum(pstats[:, 2:3], p_tm[:], axis=AX.X))
                        sc.wt(vector)
                        sc.inc(vector.reciprocal(pstats[:, 3:4], pstats[:, 2:3]))
                        sc.wt(vector)
                        sc.inc(vector.tensor_scalar(
                            p_all[:, (s * 16 + h) * 64:(s * 16 + h) * 64 + 64],
                            p_tm[:], pstats[:, 3:4], None, ALU.mult))

                    @block.tensor
                    def _(tensor, s=s, h=h, hh=hh, j=j):
                        sc.wt(tensor)
                        pst_ch.wt(tensor, max(0, pst_ch.n - 1))
                        mm.inc(tensor.transpose(
                            skel.ps_t[j % 2][hh * 64:hh * 64 + 64, 0:128],
                            p_all[:, (s * 16 + h) * 64:(s * 16 + h) * 64 + 64],
                            skel.identb[:]))

                    if hh == 1:
                        @block.vector
                        def _(vector, s=s, j=j):
                            mm.wt(vector)
                            sv.wt(vector)
                            pst_ch.inc(vector.tensor_scalar(
                                att_sb[:, j * 128:(j + 1) * 128],
                                skel.ps_t[j % 2][:, 0:128], 0.0, 0.0, ALU.add,
                                ALU.add, accum_out=pstats[:, 4:5]))
                            sv.inc(vector.tensor_tensor(
                                s_sb[:, j:j + 1], s_sb[:, j:j + 1], pstats[:, 4:5], ALU.add))

                @block.sync
                def _(sync, s=s, t=t):
                    pst_ch.wt(sync)
                    skel.sto.inc(sync.dma_start(out=attnT[t * 4 + s], in_=att_sb[:]), 16)

            for j in range(8):
                for hh in range(2):
                    @block.tensor
                    def _(tensor, j=j, hh=hh, t=t):
                        if hh == 0 and j < 2:
                            sc.wt(tensor)
                            if t >= 1 or j >= 2:
                                av.wt(tensor)
                        elif hh == 0:
                            av.wt(tensor, av.n - 1)
                        for s in range(4):
                            h = 2 * j + hh
                            ins = tensor.matmul(
                                ps_av[j % 2][hh * 64: hh * 64 + 64, 0:64],
                                p_all[:, (s * 16 + h) * 64:(s * 16 + h) * 64 + 64],
                                v_tm[:, s * 1024 + h * 64: s * 1024 + (h + 1) * 64],
                                start=(s == 0), stop=(s == 3))
                        mm.inc(ins)

                @block.vector
                def _(vector, j=j):
                    mm.wt(vector)
                    av.wt(vector)
                    av.inc(vector.tensor_tensor(av_sb[:, j * 64:(j + 1) * 64],
                                                av_sb[:, j * 64:(j + 1) * 64],
                                                ps_av[j % 2][:, 0:64], ALU.add))

        @block.sync
        def _(sync):
            av.wt(sync)
            sv.wt(sync)
            skel.sto.inc(sync.dma_start(out=av_out[:, 0:512], in_=av_sb[:]), 16)
            skel.sto.inc(sync.dma_start(out=av_out[:, 512:520], in_=s_sb[:]), 16)
        skel.finish()
    return b


def build_l3attn_b(b=None):
    if b is None:
        b = B()
    b.prefix = "l3attn_b_"
    nc = b.nc
    resid = b.din("resid", [ROWS, D])
    av_all = b.din("av_all", [NCORES, 128, 520])
    attnT = b.din("attnT", [NT * 4, 128, 1024], BF16)
    woutt = b.din("woutt", [D, D])
    boutv = b.din("boutv", [8, 128])
    identf = b.din("identf", [128, 128])
    out = b.dout("out", [ROWS, D])

    with b.seg():
        block = b.block_open()
        skel = Skel(b, block, resid, out)
        ldg = b.ch("ldg"); lds = b.ch("lds"); mm = b.ch("mm"); ev = b.ch("ev")
        ya = b.ch("ya"); lda = b.ch("lda")
        skel.consts(identf)

        wout = b.sb([128, KD * D], BF16)
        bouts = b.sb([128, 8])
        avg = b.sb([128, 520])
        avt = b.sb([128, 520])
        outg = b.sb([128, 512], BF16)
        rst = b.sb([128, 16], )
        att_t = b.sb([128, 4 * 1024], BF16)
        y_fm = b.sb([128, KD * TT], BF16)
        ofm = b.sb([128, KD * TT])
        ps_y = (b.psum([128, 128]), b.psum([128, 128]))
        ps_o = (b.psum([128, TT]), b.psum([128, TT]))

        @block.gpsimd
        def _(g):
            for c in range(KD):
                ldg.inc(g.dma_start(out=wout[:, c * D:(c + 1) * D],
                                    in_=woutt[c * 128:(c + 1) * 128, :]), 16)

        emit_vec_load(b, block, bouts, boutv, 1024, skel, skel.ps_tf[0], lds)

        @block.sync
        def _(sync):
            lds.wt(sync)
            lds.inc(sync.dma_start(out=avg[:], in_=av_all[0]), 16)
            lds.inc(sync.dma_start(out=avt[:], in_=av_all[1]), 16)

        @block.vector
        def _(vector):
            lds.wt(vector)
            ya.inc(vector.tensor_add(avg[:], avg[:], avt[:]))
        for cc in range(2, NCORES):
            @block.sync
            def _(sync, cc=cc):
                ya.wt(sync)
                lds.wt(sync)
                lds.inc(sync.dma_start(out=avt[:], in_=av_all[cc]), 16)

            @block.vector
            def _(vector):
                lds.wt(vector)
                ya.wt(vector)
                ya.inc(vector.tensor_add(avg[:], avg[:], avt[:]))

        @block.vector
        def _(vector):
            ya.wt(vector)
            ya.inc(vector.reciprocal(rst[:, 0:8], avg[:, 512:520]))
            ya.wt(vector)
            ya.inc(vector.tensor_scalar(rst[:, 8:16], avg[:, 512:520], 0.0, 1e30,
                                        ALU.is_equal, ALU.mult))
            ya.wt(vector)
            ya.inc(vector.tensor_tensor(rst[:, 0:8], rst[:, 0:8], rst[:, 8:16], ALU.subtract))
            for j in range(8):
                ya.wt(vector)
                ya.inc(vector.tensor_scalar(outg[:, j * 64:(j + 1) * 64],
                                            avg[:, j * 64:(j + 1) * 64],
                                            rst[:, j:j + 1], None, ALU.mult))

        for t in range(NT):
            skel.load_rows(t, war_chs=(skel.ad,) if t else (mm, ldg))
            skel.ln_fmize(t, mm_war_ch=mm)

            @block.sync
            def _(sync, t=t):
                if t >= 1:
                    mm.wt(sync)
                for s in range(4):
                    lda.inc(sync.dma_start(out=att_t[:, s * 1024:(s + 1) * 1024],
                                           in_=attnT[t * 4 + s]), 16)

            for s in range(4):
                for h in range(16):
                    j = h // 2; hh = h % 2; i = s * 16 + h
                    pp = ps_y[i % 2]

                    @block.tensor
                    def _(tensor, s=s, h=h, j=j, hh=hh, i=i, pp=pp, t=t):
                        if i == 0:
                            lda.wt(tensor)
                            ya.wt(tensor)
                            if t >= 1:
                                ev.wt(tensor)
                        else:
                            ev.wt(tensor, ev.n - 1)
                        mm.inc(tensor.matmul(
                            pp[0:64, 0:128],
                            outg[hh * 64:hh * 64 + 64, j * 64:(j + 1) * 64],
                            att_t[hh * 64:hh * 64 + 64, s * 1024 + j * 128: s * 1024 + (j + 1) * 128],
                            start=True, stop=True))

                    @block.vector
                    def _(vector, s=s, h=h, pp=pp):
                        mm.wt(vector)
                        c = h // 2; r0 = (h % 2) * 64
                        ev.inc(vector.tensor_copy(
                            y_fm[r0:r0 + 64, c * TT + s * 128: c * TT + (s + 1) * 128],
                            pp[0:64, 0:128]))

            for o in range(KD):
                pp = ps_o[o % 2]

                @block.tensor
                def _(tensor, o=o, pp=pp, t=t):
                    if o < 2:
                        ev.wt(tensor)
                        if t >= 1:
                            ya.wt(tensor)
                    else:
                        ya.wt(tensor, ya.n - 1)
                    for c in range(KD):
                        ins = tensor.matmul(pp[:], wout[:, c * D + o * 128: c * D + (o + 1) * 128],
                                            y_fm[:, c * TT:(c + 1) * TT],
                                            start=(c == 0), stop=(c == KD - 1))
                    mm.inc(ins)

                @block.vector
                def _(vector, o=o, pp=pp):
                    mm.wt(vector)
                    ya.inc(vector.tensor_scalar(ofm[:, o * TT:(o + 1) * TT], pp[:],
                                                bouts[:, o:o + 1], None, ALU.add))

            skel.epilogue(t, ofm, ya)
        skel.finish()
    return b


def inmaps_l3attn_a(P, resid_full):
    parts = shard_rows(resid_full)
    base = {
        "wembt": np.ascontiguousarray(P["L3_attn_W_emb"].T),
        "wat": np.ascontiguousarray(P["L3_attn_W_anchor"].T),
        "identf": np.eye(128, dtype=np.float32),
    }
    return [dict(base, resid=parts[c]) for c in range(NCORES)]


def inmaps_l3attn_b(P, resid_full, av_all, attnT_parts):
    parts = shard_rows(resid_full)
    base = {
        "woutt": np.ascontiguousarray(P["L3_attn_W_out"].T),
        "boutv": P["L3_attn_b_out"].reshape(8, 128),
        "av_all": av_all,
        "identf": np.eye(128, dtype=np.float32),
    }
    return [dict(base, resid=parts[c], attnT=attnT_parts[c]) for c in range(NCORES)]


def build_l1attn(b=None):
    if b is None:
        b = B()
    b.prefix = "l1attn_"
    nc = b.nc
    resid = b.din("resid", [ROWS, D])
    wint = b.din("wint", [D, 3072])
    binv = b.din("binv", [24, 128])
    woutt = b.din("woutt", [D, D])
    boutv = b.din("boutv", [8, 128])
    kd0 = b.din("kd0", [128, 1])
    kd1 = b.din("kd1", [128, 1])
    vd0 = b.din("vd0", [128, 1])
    vd1 = b.din("vd1", [128, 1])
    patr = b.din("patr", [8, 128, 16])
    patb = b.din("patb", [8, 16, 128])
    identf = b.din("identf", [128, 128])
    out = b.dout("out", [ROWS, D])

    PTT = 512
    PH = 256
    with b.seg():
        block = b.block_open()
        skel = Skel(b, block, resid, out)
        ldg = b.ch("ldg"); lds = b.ch("lds"); mm = b.ch("mm"); ev = b.ch("ev")
        at = b.ch("at"); ya = b.ch("ya")
        skel.consts(identf)

        win = b.sb([128, KD * 3072], BF16)
        wout = b.sb([128, KD * D], BF16)
        bins = b.sb([128, 24])
        bouts = b.sb([128, 8])
        kdt = b.sb([128, 4])
        patr_sb = b.sb([128, 8 * 16], BF16)
        patb_sb = b.sb([16, 8 * 128], BF16)

        @block.gpsimd
        def _(g):
            for c in range(KD):
                ldg.inc(g.dma_start(out=win[:, c * 3072:(c + 1) * 3072],
                                    in_=wint[c * 128:(c + 1) * 128, :]), 16)
                ldg.inc(g.dma_start(out=wout[:, c * D:(c + 1) * D],
                                    in_=woutt[c * 128:(c + 1) * 128, :]), 16)
            ldg.inc(g.dma_start(out=patr_sb.rearrange("p (c x) -> p c x", x=16),
                                in_=patr.rearrange("c p x -> p c x")), 16)
            ldg.inc(g.dma_start(out=patb_sb.rearrange("p (c x) -> p c x", x=128),
                                in_=patb.rearrange("c p x -> p c x")), 16)

        @block.sync
        def _(sync):
            lds.wt(sync)
            lds.inc(sync.dma_start(out=kdt[:, 0:1], in_=kd0), 16)
            lds.inc(sync.dma_start(out=kdt[:, 1:2], in_=kd1), 16)
            lds.inc(sync.dma_start(out=kdt[:, 2:3], in_=vd0), 16)
            lds.inc(sync.dma_start(out=kdt[:, 3:4], in_=vd1), 16)

        emit_vec_load(b, block, bins, binv, 3072, skel, skel.ps_tf[0], lds)
        emit_vec_load(b, block, bouts, boutv, 1024, skel, skel.ps_tf[0], lds)

        xr8 = b.sb([128, 4 * D])
        xfm_ = b.sb([128, KD * PTT], BF16)
        qkv = b.sb([128, 24 * PTT], BF16)
        A00 = b.sb([16, PH]); A01 = b.sb([16, PH])
        A10 = b.sb([16, PH]); A11 = b.sb([16, PH])
        pstats = b.sb([16, 6 * PH])
        kmod = b.sb([128, PH], BF16)
        prod = b.sb([128, PH], BF16)
        Pa = b.sb([16, PH], BF16)
        Pb = b.sb([16, PH], BF16)
        o_fm = b.sb([128, KD * PTT], BF16)
        ofm = b.sb([128, KD * PTT])
        ps_mm = (b.psum([128, PH]), b.psum([128, PH]))
        ps_att = (b.psum([16, PH]), b.psum([16, PH]))

        NPT = ROWS // 2 // PH
        for t in range(NPT):
            @block.sync
            def _(sync, t=t):
                if t == 0:
                    mm.wt(sync); ldg.wt(sync); lds.wt(sync)
                else:
                    skel.sto.wt(sync)
                r0 = resid[t * PH:(t + 1) * PH, :].rearrange("(s p) d -> p s d", p=128)
                r1 = resid[2048 + t * PH: 2048 + (t + 1) * PH, :].rearrange("(s p) d -> p s d", p=128)
                skel.ldx.inc(sync.dma_start(
                    out=xr8.rearrange("p (s d) -> p s d", s=4)[:, 0:2], in_=r0), 16)
                skel.ldx.inc(sync.dma_start(
                    out=xr8.rearrange("p (s d) -> p s d", s=4)[:, 2:4], in_=r1), 16)

            for s in range(4):
                if s == 0 and t >= 1:
                    @block.vector
                    def _(vector):
                        mm.wt(vector)
                        skel.ln.inc(vector.engine_nop())
                emit_ln_subtile(b, block, xr8[:, s * D:(s + 1) * D], skel.xnb[:],
                                skel.sq, skel.stat, skel.ldx, skel.ln, eps_ap=skel.epsb[:])
                emit_tm_to_fm(b, block, skel.xnb[:],
                              lambda c, s=s: xfm_[:, c * PTT + s * 128: c * PTT + (s + 1) * 128],
                              skel.identb[:], skel.ps_t, skel.tr, skel.fmc, ready_ch=skel.ln)

            for o in range(24):
                for half in range(2):
                    pp = ps_mm[(o * 2 + half) % 2]

                    @block.tensor
                    def _(tensor, o=o, half=half, pp=pp, t=t):
                        i = o * 2 + half
                        if i < 2:
                            skel.fmc.wt(tensor)
                            if t >= 1:
                                ev.wt(tensor)
                                at.wt(tensor)
                                ya.wt(tensor)
                        else:
                            ev.wt(tensor, ev.n - 1)
                        for c in range(KD):
                            ins = tensor.matmul(
                                pp[:], win[:, c * 3072 + o * 128: c * 3072 + (o + 1) * 128],
                                xfm_[:, c * PTT + half * PH: c * PTT + (half + 1) * PH],
                                start=(c == 0), stop=(c == KD - 1))
                        mm.inc(ins)

                    @block.vector
                    def _(vector, o=o, half=half, pp=pp):
                        mm.wt(vector)
                        ev.inc(vector.tensor_scalar(
                            qkv[:, o * PTT + half * PH: o * PTT + (half + 1) * PH],
                            pp[:], bins[:, o:o + 1], None, ALU.add))

            for (s_, t_, Adst) in ((0, 0, A00), (0, 1, A01), (1, 0, A10), (1, 1, A11)):
                kdcol = 0 if s_ == t_ else 1
                pa = ps_att[(s_ * 2 + t_) % 2]

                for c in range(KD):
                    @block.vector
                    def _(vector, c=c, s_=s_, t_=t_, kdcol=kdcol):
                        ev.wt(vector)
                        at.wt(vector)
                        mm.wt(vector)
                        at.inc(vector.tensor_scalar(
                            kmod[:], qkv[:, (8 + c) * PTT + t_ * PH:(8 + c) * PTT + (t_ + 1) * PH],
                            kdt[:, kdcol:kdcol + 1], None, ALU.add))
                        at.wt(vector)
                        at.inc(vector.tensor_tensor(
                            prod[:], qkv[:, c * PTT + s_ * PH: c * PTT + (s_ + 1) * PH],
                            kmod[:], ALU.mult))

                    @block.tensor
                    def _(tensor, c=c, pa=pa):
                        at.wt(tensor)
                        if c == 0:
                            at.wt(tensor)
                        mm.inc(tensor.matmul(pa[:], patr_sb[:, c * 16:(c + 1) * 16],
                                             prod[:], start=(c == 0), stop=(c == KD - 1)))

                @block.vector
                def _(vector, Adst=Adst, pa=pa):
                    mm.wt(vector)
                    at.inc(vector.tensor_scalar(Adst[:], pa[:], 0.125, None, ALU.mult))

            for s_ in range(2):
                Aa = A00 if s_ == 0 else A10
                Ab = A01 if s_ == 0 else A11

                @block.vector
                def _(vector, Aa=Aa, Ab=Ab):
                    at.wt(vector)
                    at.inc(vector.tensor_tensor(pstats[:, 0:PH], Aa[:], Ab[:], ALU.max))
                    at.wt(vector)
                    at.inc(vector.tensor_tensor(pstats[:, PH:2*PH], Aa[:], pstats[:, 0:PH], ALU.subtract))
                    at.wt(vector)
                    at.inc(vector.tensor_tensor(pstats[:, 2*PH:3*PH], Ab[:], pstats[:, 0:PH], ALU.subtract))

                @block.scalar
                def _(scalar):
                    at.wt(scalar)
                    at.inc(scalar.activation(pstats[:, PH:2*PH], pstats[:, PH:2*PH], AF.Exp))
                    at.wt(scalar)
                    at.inc(scalar.activation(pstats[:, 2*PH:3*PH], pstats[:, 2*PH:3*PH], AF.Exp))

                @block.vector
                def _(vector, Aa=Aa, Ab=Ab):
                    mm.wt(vector)
                    at.wt(vector)
                    at.inc(vector.tensor_tensor(pstats[:, 3*PH:4*PH], pstats[:, PH:2*PH],
                                                pstats[:, 2*PH:3*PH], ALU.add))
                    at.wt(vector)
                    at.inc(vector.reciprocal(pstats[:, 4*PH:5*PH], pstats[:, 3*PH:4*PH]))
                    at.wt(vector)
                    at.inc(vector.tensor_tensor(Pa[:], pstats[:, PH:2*PH],
                                                pstats[:, 4*PH:5*PH], ALU.mult))
                    at.wt(vector)
                    at.inc(vector.tensor_tensor(Pb[:], pstats[:, 2*PH:3*PH],
                                                pstats[:, 4*PH:5*PH], ALU.mult))

                for c in range(KD):
                    for t_ in range(2):
                        p_st = Pa if t_ == 0 else Pb
                        vdcol = 2 + (0 if s_ == t_ else 1)
                        pp = ps_mm[t_]

                        @block.tensor
                        def _(tensor, c=c, t_=t_, p_st=p_st, pp=pp):
                            at.wt(tensor)
                            ya.wt(tensor)
                            mm.inc(tensor.matmul(pp[:], patb_sb[:, c * 128:(c + 1) * 128],
                                                 p_st[:], start=True, stop=True))

                        @block.vector
                        def _(vector, c=c, s_=s_, t_=t_, vdcol=vdcol, pp=pp):
                            mm.wt(vector)
                            ya.wt(vector)
                            ya.inc(vector.tensor_scalar(
                                kmod[:], qkv[:, (16 + c) * PTT + t_ * PH:(16 + c) * PTT + (t_ + 1) * PH],
                                kdt[:, vdcol:vdcol + 1], None, ALU.add))
                            ya.wt(vector)
                            if t_ == 0:
                                ya.inc(vector.tensor_tensor(
                                    o_fm[:, c * PTT + s_ * PH: c * PTT + (s_ + 1) * PH],
                                    kmod[:], pp[:], ALU.mult))
                            else:
                                ya.inc(vector.tensor_tensor(prod[:], kmod[:], pp[:], ALU.mult))
                                ya.wt(vector)
                                ya.inc(vector.tensor_tensor(
                                    o_fm[:, c * PTT + s_ * PH: c * PTT + (s_ + 1) * PH],
                                    o_fm[:, c * PTT + s_ * PH: c * PTT + (s_ + 1) * PH],
                                    prod[:], ALU.add))

            for o in range(KD):
                for half in range(2):
                    pp = ps_mm[(o * 2 + half) % 2]

                    @block.tensor
                    def _(tensor, o=o, half=half, pp=pp):
                        i = o * 2 + half
                        ya.wt(tensor)
                        if i >= 2:
                            ya.wt(tensor, ya.n - 1)
                        for c in range(KD):
                            ins = tensor.matmul(
                                pp[:], wout[:, c * D + o * 128: c * D + (o + 1) * 128],
                                o_fm[:, c * PTT + half * PH: c * PTT + (half + 1) * PH],
                                start=(c == 0), stop=(c == KD - 1))
                        mm.inc(ins)

                    @block.vector
                    def _(vector, o=o, half=half, pp=pp):
                        mm.wt(vector)
                        ya.inc(vector.tensor_scalar(
                            ofm[:, o * PTT + half * PH: o * PTT + (half + 1) * PH],
                            pp[:], bouts[:, o:o + 1], None, ALU.add))

            for o in range(KD):
                for s in range(4):
                    i = o * 8 + s
                    pstf = skel.ps_tf[i % 2]

                    @block.tensor
                    def _(tensor, o=o, s=s, i=i, pstf=pstf, t=t):
                        if i == 0:
                            ya.wt(tensor)
                            if t >= 1:
                                skel.ad.wt(tensor)
                        elif i >= 2:
                            skel.ad.wt(tensor, skel.ad.n - 1)
                        skel.tr.inc(tensor.transpose(
                            pstf[:], ofm[:, o * PTT + s * 128: o * PTT + (s + 1) * 128],
                            skel.idf[:]))

                    @block.vector
                    def _(vector, o=o, s=s, pstf=pstf):
                        skel.tr.wt(vector)
                        dst = xr8[:, s * D + o * 128: s * D + (o + 1) * 128]
                        skel.ad.inc(vector.tensor_tensor(dst, dst, pstf[:], ALU.add))

            @block.sync
            def _(sync, t=t):
                skel.ad.wt(sync)
                o0 = out[t * PH:(t + 1) * PH, :].rearrange("(s p) d -> p s d", p=128)
                o1 = out[2048 + t * PH: 2048 + (t + 1) * PH, :].rearrange("(s p) d -> p s d", p=128)
                skel.sto.inc(sync.dma_start(
                    out=o0, in_=xr8.rearrange("p (s d) -> p s d", s=4)[:, 0:2]), 16)
                skel.sto.inc(sync.dma_start(
                    out=o1, in_=xr8.rearrange("p (s d) -> p s d", s=4)[:, 2:4]), 16)
        skel.finish()
    return b


def inmaps_l1attn(P, resid_full):
    parts = shard_rows(resid_full)
    patr = np.zeros((8, 128, 16), np.float32)
    patb = np.zeros((8, 16, 128), np.float32)
    for c in range(8):
        for hh in range(2):
            h = 2 * c + hh
            patr[c, hh * 64:(hh + 1) * 64, h] = 1.0
            patb[c, h, hh * 64:(hh + 1) * 64] = 1.0
    base = {
        "wint": np.ascontiguousarray(P["L1_attn_W_in"].T),
        "binv": P["L1_attn_b_in"].reshape(24, 128),
        "woutt": np.ascontiguousarray(P["L1_attn_W_out"].T),
        "boutv": P["L1_attn_b_out"].reshape(8, 128),
        "kd0": np.tile(P["L1_attn_k_dist_emb"][0], 2).reshape(128, 1),
        "kd1": np.tile(P["L1_attn_k_dist_emb"][1], 2).reshape(128, 1),
        "vd0": np.tile(P["L1_attn_v_dist_emb"][0], 2).reshape(128, 1),
        "vd1": np.tile(P["L1_attn_v_dist_emb"][1], 2).reshape(128, 1),
        "patr": patr, "patb": patb,
        "identf": np.eye(128, dtype=np.float32),
    }
    return [dict(base, resid=parts[c]) for c in range(NCORES)]


HB = 2176


def build_l2attn_a(b=None):
    if b is None:
        b = B()
    b.prefix = "l2attn_a_"
    nc = b.nc
    resid = b.din("resid", [ROWS, D])
    tails = b.din("tails", [2048, D])
    selmask = b.din("selmask", [128, 8])
    wintile = b.din("wintile", [D, 2048])
    wxp = b.din("wxp", [D, 33])
    convw = b.din("convw", [8, 128, 4])
    convb = b.din("convb", [8, 128])
    identf = b.din("identf", [128, 128])
    uscr = b.dout("uscr", [8, 128, 4096], BF16)
    zscr = b.dout("zscr", [8, 128, 4096], BF16)
    bcd_out = b.dout("bcd_out", [33, 4096])

    with b.seg():
        block = b.block_open()
        skel = Skel(b, block, resid, None)
        ldg = b.ch("ldg"); lds = b.ch("lds"); mm = b.ch("mm"); ev = b.ch("ev")
        cv = b.ch("cv"); pj = b.ch("pj"); zst = b.ch("zst"); ust = b.ch("ust")
        skel.consts(identf)

        win = b.sb([128, KD * 2048], BF16)
        wxps = b.sb([128, KD * 33], BF16)
        cwv = b.sb([128, 8 * 4])
        cbv = b.sb([128, 8])
        selm = b.sb([128, 8])

        @block.gpsimd
        def _(g):
            for c in range(KD):
                ldg.inc(g.dma_start(out=win[:, c * 2048:(c + 1) * 2048],
                                    in_=wintile[c * 128:(c + 1) * 128, :]), 16)
                ldg.inc(g.dma_start(out=wxps[:, c * 33:(c + 1) * 33],
                                    in_=wxp[c * 128:(c + 1) * 128, :]), 16)

        @block.sync
        def _(sync):
            lds.wt(sync)
            lds.inc(sync.dma_start(out=cwv.rearrange("p (c j) -> p c j", j=4),
                                   in_=convw.rearrange("c p j -> p c j")), 16)
            lds.inc(sync.dma_start(out=selm[:], in_=selmask), 16)

        emit_vec_load(b, block, cbv, convb, 1024, skel, skel.ps_tf[0], lds)

        xm = b.sb([128, KD * 2 * HB], BF16)
        xr1 = b.sb([128, D])
        tailbuf = b.sb([128, D])
        hsel = b.ch("hsel")
        z_sb = b.sb([128, 8 * 128], BF16)
        cacc = b.sb([128, 2048])
        cacc2 = b.sb([128, 2048])
        bcd_sb = b.sb([33, 4096])
        sprow = b.sb([1, 4096])
        ps_mm = (b.psum([128, 128]), b.psum([128, 128]))
        ps_pj = (b.psum([33, 512]), b.psum([33, 512]))

        for s in range(34):
            h, k = divmod(s, 17)

            @block.sync
            def _(sync, s=s, h=h, k=k):
                if s == 0:
                    mm.wt(sync); ldg.wt(sync); lds.wt(sync)
                else:
                    zst.wt(sync)
                    skel.ln.wt(sync)
                if k == 0:
                    hsel.wt(sync)
                    skel.ldx.inc(sync.dma_start(
                        out=tailbuf[:], in_=tails[h * 128:(h + 1) * 128, :]), 16)
                else:
                    rows = resid[(h * 2048 + (k - 1) * 128):(h * 2048 + k * 128), :]
                    skel.ldx.inc(sync.dma_start(out=xr1[:], in_=rows), 16)

            if k == 0:
                @block.vector
                def _(vector, h=h):
                    skel.ldx.wt(vector)
                    skel.ln.wt(vector)
                    lds.wt(vector)
                    hsel.inc(vector.tensor_scalar(xr1[:], tailbuf[:],
                                                  selm[:, 0:1], None, ALU.mult))
                for r in range(1, NCORES):
                    @block.sync
                    def _(sync2, h=h, r=r):
                        hsel.wt(sync2)
                        skel.ldx.inc(sync2.dma_start(
                            out=tailbuf[:],
                            in_=tails[r * 256 + h * 128: r * 256 + (h + 1) * 128, :]), 16)

                    @block.vector
                    def _(vector, r=r):
                        skel.ldx.wt(vector)
                        hsel.wt(vector)
                        hsel.inc(vector.scalar_tensor_tensor(
                            xr1[:], tailbuf[:], selm[:, r:r + 1],
                            xr1[:], ALU.mult, ALU.add))

                @block.vector
                def _(vector):
                    hsel.wt(vector)
                    skel.ln.inc(vector.engine_nop())

            if s >= 1:
                @block.vector
                def _(vector):
                    mm.wt(vector)
                    skel.ln.inc(vector.engine_nop())
            emit_ln_subtile(b, block, xr1[:], skel.xnb[:], skel.sq,
                            skel.stat, skel.ldx, skel.ln, eps_ap=skel.epsb[:])
            emit_tm_to_fm(b, block, skel.xnb[:],
                          lambda c: skel.xfm[:, c * 128:(c + 1) * 128],
                          skel.identb[:], skel.ps_t, skel.tr, skel.fmc, ready_ch=skel.ln)

            for o in range(16):
                pp = ps_mm[o % 2]

                @block.tensor
                def _(tensor, s=s, o=o, pp=pp):
                    if o == 0:
                        skel.fmc.wt(tensor)
                        if s > 0 or True:
                            ev.wt(tensor)
                    elif o >= 2:
                        ev.wt(tensor, ev.n - 1)
                    for c in range(KD):
                        ins = tensor.matmul(pp[:], win[:, c * 2048 + o * 128: c * 2048 + (o + 1) * 128],
                                            skel.xfm[:, c * 128:(c + 1) * 128],
                                            start=(c == 0), stop=(c == KD - 1))
                    mm.inc(ins)

                @block.vector
                def _(vector, s=s, h=h, k=k, o=o, pp=pp):
                    mm.wt(vector)
                    if o == 8:
                        zst.wt(vector)
                    if o < 8:
                        dst = xm[:, o * 2 * HB + h * HB + k * 128: o * 2 * HB + h * HB + (k + 1) * 128]
                    else:
                        dst = z_sb[:, (o - 8) * 128:(o - 7) * 128]
                    ev.inc(vector.tensor_copy(dst, pp[:]))

            if k >= 1:
                @block.sync
                def _(sync, s=s, h=h, k=k):
                    ev.wt(sync)
                    zst.inc(sync.dma_start(
                        out=zscr[:, :, h * 2048 + (k - 1) * 128: h * 2048 + k * 128].rearrange("c p q -> p c q"),
                        in_=z_sb.rearrange("p (c q) -> p c q", q=128)), 16)

        for c in range(KD):
            for h in range(2):
                base = c * 2 * HB + h * HB + 128

                @block.vector
                def _(vector, c=c, h=h, base=base):
                    ev.wt(vector)
                    cv.wt(vector)
                    cv.inc(vector.tensor_scalar(cacc[:], xm[:, base - 3: base + 2045],
                                                cwv[:, c * 4:c * 4 + 1], None, ALU.mult))
                    cv.wt(vector)
                    cv.inc(vector.scalar_tensor_tensor(cacc2[:], xm[:, base - 2: base + 2046],
                                                       cwv[:, c * 4 + 1:c * 4 + 2], cacc[:],
                                                       ALU.mult, ALU.add))
                    cv.wt(vector)
                    cv.inc(vector.scalar_tensor_tensor(cacc[:], xm[:, base - 1: base + 2047],
                                                       cwv[:, c * 4 + 2:c * 4 + 3], cacc2[:],
                                                       ALU.mult, ALU.add))
                    cv.wt(vector)
                    cv.inc(vector.scalar_tensor_tensor(cacc2[:], xm[:, base: base + 2048],
                                                       cwv[:, c * 4 + 3:c * 4 + 4], cacc[:],
                                                       ALU.mult, ALU.add))

                @block.scalar
                def _(scalar, c=c, h=h, base=base):
                    cv.wt(scalar)
                    cv.inc(scalar.activation(xm[:, base: base + 2048],
                                             cacc2[:], AF.Silu, bias=cbv[:, c:c + 1]))

            @block.gpsimd
            def _(g, c=c):
                cv.wt(g)
                ust.inc(g.dma_start(out=uscr[c, :, 0:2048],
                                    in_=xm[:, c * 2 * HB + 128: c * 2 * HB + 2176]), 16)
                ust.inc(g.dma_start(out=uscr[c, :, 2048:4096],
                                    in_=xm[:, c * 2 * HB + HB + 128: c * 2 * HB + HB + 2176]), 16)

        for tau in range(8):
            pp = ps_pj[tau % 2]

            @block.tensor
            def _(tensor, tau=tau, pp=pp):
                cv.wt(tensor)
                if tau >= 2:
                    pj.wt(tensor, pj.n - 1)
                hh = tau // 4
                for c in range(KD):
                    base = c * 2 * HB + hh * HB + 128 + (tau % 4) * 512
                    ins = tensor.matmul(pp[:], wxps[:, c * 33:(c + 1) * 33],
                                        xm[:, base: base + 512],
                                        start=(c == 0), stop=(c == KD - 1))
                mm.inc(ins)

            @block.vector
            def _(vector, tau=tau, pp=pp):
                mm.wt(vector)
                pj.inc(vector.tensor_copy(bcd_sb[:, tau * 512:(tau + 1) * 512], pp[:]))

        spd = b.ch("spd")
        sprow2 = b.sb([1, 4096])

        @block.gpsimd
        def _(g):
            pj.wt(g)
            spd.inc(g.dma_start(out=sprow2[:], in_=bcd_sb[32:33, :]), 16)

        @block.scalar
        def _(scalar):
            spd.wt(scalar)
            pj.inc(scalar.activation(sprow[:], sprow2[:], AF.Abs))
            pj.wt(scalar)
            pj.inc(scalar.activation(sprow[:], sprow[:], AF.Exp, scale=-1.0))
            pj.wt(scalar)
            pj.inc(scalar.activation(sprow[:], sprow[:], AF.Ln, bias=1.0))
            pj.wt(scalar)
            pj.inc(scalar.activation(sprow2[:], sprow2[:], AF.Relu))

        @block.vector
        def _(vector):
            pj.wt(vector)
            pj.inc(vector.tensor_add(sprow2[:], sprow2[:], sprow[:]))

        @block.gpsimd
        def _(g):
            pj.wt(g)
            spd.inc(g.dma_start(out=bcd_sb[32:33, :], in_=sprow2[:]), 16)

        @block.sync
        def _(sync):
            pj.wt(sync)
            spd.wt(sync)
            skel.sto.inc(sync.dma_start(out=bcd_out, in_=bcd_sb[:]), 16)
        skel.finish()

        @block.gpsimd
        def _(g):
            ust.wt(g)

        @block.sync
        def _(sync):
            zst.wt(sync)
    return b


def build_l2attn_b(b=None):
    if b is None:
        b = B()
    b.prefix = "l2attn_b_"
    nc = b.nc
    u_in = b.din("u_in", [8, 128, 4096], BF16)
    bcd = b.din("bcd", [264, 4096])
    logA = b.din("logA", [128, 16])
    dparam = b.din("dparam", [128, 1])
    y_out = b.dout("y_out", [128, 2, 16384], BF16)

    T = 16384
    CH = 4096
    NQ = CH // 512
    with b.seg():
        block = b.block_open()
        ld = b.ch("ld"); ldu = b.ch("ldu"); rw = b.ch("rw"); mm = b.ch("mm"); sc = b.ch("sc")
        vv = b.ch("vv"); yy = b.ch("yy"); sto = b.ch("sto")

        Acol = b.sb([128, 16])
        dpar = b.sb([128, 1])
        ones_row = b.sb([1, 128])
        ub = b.sb([128, T], BF16)
        utl = b.sb([128, T], BF16)
        drow = b.sb([1, CH])
        brow = b.sb([1, CH])
        crow = b.sb([1, CH])
        ablk = b.sb([128, CH], BF16)
        bublk = b.sb([128, CH], BF16)
        hblk = b.sb([128, CH], BF16)
        ycur = b.sb([128, T], BF16)
        yterm = b.sb([128, 512])
        stcol = b.sb([128, 2])
        ps_k = (b.psum([128, 512]), b.psum([128, 512]))

        @block.sync
        def _(sync):
            ld.inc(sync.dma_start(out=Acol[:], in_=logA), 16)
            ld.inc(sync.dma_start(out=dpar[:], in_=dparam), 16)

        @block.vector
        def _(vector):
            ld.wt(vector)
            vv.inc(vector.memset(ones_row[:], 1.0))

        @block.scalar
        def _(scalar):
            ld.wt(scalar)
            vv.wt(scalar)
            vv.inc(scalar.activation(Acol[:], Acol[:], AF.Exp))

        @block.vector
        def _(vector):
            vv.wt(vector)
            vv.inc(vector.tensor_scalar(Acol[:], Acol[:], -1.0, None, ALU.mult))

        for bb in range(2):
            @block.sync
            def _(sync, bb=bb):
                if bb == 1:
                    sto.wt(sync)
                    mm.wt(sync)
                    vv.wt(sync)
                for r8 in range(8):
                    ldu.inc(sync.dma_start(out=ub[:, r8 * 2048:(r8 + 1) * 2048],
                                           in_=u_in[r8, :, bb * 2048:(bb + 1) * 2048]), 16)

            for ch0 in range(T // CH):
                @block.sync
                def _(sync, bb=bb, ch0=ch0):
                    mm.wt(sync)
                    rw.wt(sync)
                    for rr in range(2):
                        rw.inc(sync.dma_start(
                            out=drow[0:1, rr * 2048:(rr + 1) * 2048],
                            in_=bcd[(2 * ch0 + rr) * 33 + 32:(2 * ch0 + rr) * 33 + 33,
                                    bb * 2048:(bb + 1) * 2048]), 16)
                for q0 in range(NQ):
                    blk = ch0 * NQ + q0
                    pp = ps_k[q0 % 2]

                    @block.tensor
                    def _(tensor, blk=blk, q0=q0, pp=pp, bb=bb):
                        ld.wt(tensor)
                        ldu.wt(tensor)
                        rw.wt(tensor)
                        vv.wt(tensor)
                        mm.inc(tensor.matmul(pp[:], ones_row[:],
                                             drow[0:1, q0 * 512:(q0 + 1) * 512],
                                             start=True, stop=True))

                    @block.vector
                    def _(vector, blk=blk, pp=pp):
                        mm.wt(vector)
                        vv.inc(vector.tensor_tensor(utl[:, blk * 512:(blk + 1) * 512],
                                                    ub[:, blk * 512:(blk + 1) * 512],
                                                    pp[:], ALU.mult))

            for n in range(16):
                for ch in range(T // CH):
                    @block.sync
                    def _(sync, bb=bb, n=n, ch=ch):
                        rw.wt(sync)
                        yy.wt(sync)
                        vv.wt(sync)
                        mm.wt(sync)
                        for rr in range(2):
                            base_r = (2 * ch + rr) * 33
                            csl2 = slice(rr * 2048, (rr + 1) * 2048)
                            cbb = slice(bb * 2048, (bb + 1) * 2048)
                            rw.inc(sync.dma_start(out=brow[0:1, csl2],
                                                  in_=bcd[base_r + n:base_r + n + 1, cbb]), 16)
                            rw.inc(sync.dma_start(out=crow[0:1, csl2],
                                                  in_=bcd[base_r + 16 + n:base_r + 17 + n, cbb]), 16)
                            rw.inc(sync.dma_start(out=drow[0:1, csl2],
                                                  in_=bcd[base_r + 32:base_r + 33, cbb]), 16)

                    for q in range(NQ):
                        blk = ch * NQ + q
                        pp = ps_k[q % 2]

                        @block.tensor
                        def _(tensor, blk=blk, q=q, pp=pp):
                            rw.wt(tensor)
                            vv.wt(tensor)
                            sc.wt(tensor)
                            mm.inc(tensor.matmul(pp[:], ones_row[:],
                                                 drow[0:1, q * 512:(q + 1) * 512],
                                                 start=True, stop=True))

                        @block.scalar
                        def _(scalar, q=q, pp=pp, n=n):
                            mm.wt(scalar)
                            sc.inc(scalar.activation(ablk[:, q * 512:(q + 1) * 512], pp[:],
                                                     AF.Exp, scale=Acol[:, n:n + 1]))

                        @block.tensor
                        def _(tensor, q=q, pp=pp):
                            sc.wt(tensor)
                            mm.inc(tensor.matmul(pp[:], ones_row[:],
                                                 brow[0:1, q * 512:(q + 1) * 512],
                                                 start=True, stop=True))

                        @block.vector
                        def _(vector, blk=blk, q=q, pp=pp):
                            mm.wt(vector)
                            vv.inc(vector.tensor_tensor(bublk[:, q * 512:(q + 1) * 512],
                                                        utl[:, blk * 512:(blk + 1) * 512],
                                                        pp[:], ALU.mult))

                    @block.vector
                    def _(vector, ch=ch):
                        vv.wt(vector)
                        sc.wt(vector)
                        init = 0.0 if ch == 0 else stcol[:, 0:1]
                        vv.inc(vector.tensor_tensor_scan(hblk[:], ablk[:], bublk[:],
                                                         init, ALU.mult, ALU.add))
                        vv.wt(vector)
                        vv.inc(vector.tensor_copy(stcol[:, 0:1], hblk[:, CH - 1:CH]))

                    for q in range(NQ):
                        blk = ch * NQ + q
                        pp = ps_k[q % 2]

                        @block.tensor
                        def _(tensor, blk=blk, q=q, pp=pp):
                            vv.wt(tensor)
                            yy.wt(tensor)
                            mm.inc(tensor.matmul(pp[:], ones_row[:],
                                                 crow[0:1, q * 512:(q + 1) * 512],
                                                 start=True, stop=True))

                        @block.vector
                        def _(vector, blk=blk, q=q, pp=pp, n=n):
                            mm.wt(vector)
                            yy.inc(vector.tensor_tensor(yterm[:],
                                                        hblk[:, q * 512:(q + 1) * 512],
                                                        pp[:], ALU.mult))
                            yy.wt(vector)
                            if n == 0:
                                yy.inc(vector.tensor_copy(ycur[:, blk * 512:(blk + 1) * 512],
                                                          yterm[:]))
                            else:
                                yy.inc(vector.tensor_tensor(ycur[:, blk * 512:(blk + 1) * 512],
                                                            ycur[:, blk * 512:(blk + 1) * 512],
                                                            yterm[:], ALU.add))

            for blk in range(T // 512):
                @block.vector
                def _(vector, blk=blk):
                    yy.wt(vector)
                    yy.inc(vector.scalar_tensor_tensor(
                        ycur[:, blk * 512:(blk + 1) * 512],
                        ub[:, blk * 512:(blk + 1) * 512], dpar[:, 0:1],
                        ycur[:, blk * 512:(blk + 1) * 512], ALU.mult, ALU.add))

            @block.sync
            def _(sync, bb=bb):
                yy.wt(sync)
                sto.inc(sync.dma_start(out=y_out[:, bb], in_=ycur[:]), 16)

        @block.sync
        def _(sync):
            sto.wt(sync)
    return b


def build_l2attn_c(b=None):
    if b is None:
        b = B()
    b.prefix = "l2attn_c_"
    nc = b.nc
    resid = b.din("resid", [ROWS, D])
    ytile = b.din("ytile", [8, 128, 4096], BF16)
    zscr = b.din("zscr", [8, 128, 4096], BF16)
    woutt = b.din("woutt", [D, D])
    identf = b.din("identf", [128, 128])
    out = b.dout("out", [ROWS, D])

    with b.seg():
        block = b.block_open()
        skel = Skel(b, block, resid, out)
        ldg = b.ch("ldg"); mm = b.ch("mm"); ev = b.ch("ev"); gg = b.ch("gg")
        skel.consts(identf)

        wout = b.sb([128, KD * D], BF16)
        y_sb = b.sb([128, KD * 4096], BF16)
        ofm = b.sb([128, KD * TT])
        ps_o = (b.psum([128, TT]), b.psum([128, TT]))

        @block.gpsimd
        def _(g):
            for c in range(KD):
                ldg.inc(g.dma_start(out=wout[:, c * D:(c + 1) * D],
                                    in_=woutt[c * 128:(c + 1) * 128, :]), 16)
            for c in range(KD):
                ldg.inc(g.dma_start(out=y_sb[:, c * 4096:(c + 1) * 4096],
                                    in_=ytile[c]), 16)

        zbuf = b.sb([128, 4096], BF16)
        zl = b.ch("zl")
        for c in range(KD):
            @block.sync
            def _(sync, c=c):
                gg.wt(sync)
                zl.wt(sync)
                zl.inc(sync.dma_start(out=zbuf[:], in_=zscr[c]), 16)

            @block.scalar
            def _(scalar, c=c):
                zl.wt(scalar)
                ldg.wt(scalar)
                gg.inc(scalar.activation(zbuf[:], zbuf[:], AF.Silu))

            @block.vector
            def _(vector, c=c):
                gg.wt(vector)
                gg.inc(vector.tensor_tensor(y_sb[:, c * 4096:(c + 1) * 4096],
                                            y_sb[:, c * 4096:(c + 1) * 4096],
                                            zbuf[:], ALU.mult))

        for t in range(NT):
            skel.load_rows(t, war_chs=(skel.ad,) if t else (mm, gg))
            for o in range(KD):
                pp = ps_o[o % 2]

                @block.tensor
                def _(tensor, o=o, pp=pp, t=t):
                    if o < 2 and t == 0:
                        gg.wt(tensor)
                        ldg.wt(tensor)
                    elif o < 2:
                        ev.wt(tensor)
                    else:
                        ev.wt(tensor, ev.n - 1)
                    for c in range(KD):
                        ins = tensor.matmul(pp[:], wout[:, c * D + o * 128: c * D + (o + 1) * 128],
                                            y_sb[:, c * 4096 + t * TT: c * 4096 + (t + 1) * TT],
                                            start=(c == 0), stop=(c == KD - 1))
                    mm.inc(ins)

                @block.vector
                def _(vector, o=o, pp=pp):
                    mm.wt(vector)
                    ev.inc(vector.tensor_copy(ofm[:, o * TT:(o + 1) * TT], pp[:]))

            skel.epilogue(t, ofm, ev)
        skel.finish()
    return b


def l2a_tails_from_parts(parts):
    tails = np.zeros((2048, D), np.float32)
    for r in range(NCORES):
        tails[r * 256:r * 256 + 128] = parts[r][1920:2048]
        tails[r * 256 + 128:r * 256 + 256] = parts[r][2048 + 1920:4096]
    return tails


def inmaps_l2attn_a(P, resid_full):
    parts = shard_rows(resid_full)
    tails = l2a_tails_from_parts(parts)
    base = {
        "wintile": P["L2_attn_W_in"],
        "wxp": P["L2_attn_W_xproj"],
        "convw": np.ascontiguousarray(P["L2_attn_conv_w"].reshape(8, 128, 1, 4)[:, :, 0, :]),
        "convb": P["L2_attn_conv_b"].reshape(8, 128),
        "identf": np.eye(128, dtype=np.float32),
        "tails": tails,
    }
    ims = []
    for c in range(NCORES):
        sm = np.zeros((128, 8), np.float32)
        if c > 0:
            sm[:, c - 1] = 1.0
        ims.append(dict(base, resid=parts[c], selmask=sm))
    return ims


def reshard_l2_u(uscr_parts):
    outs = []
    for j in range(NCORES):
        outs.append(np.ascontiguousarray(
            np.stack([uscr_parts[r][j] for r in range(NCORES)], 0)))
    return outs


def assemble_l2_bcd(bcd_parts):
    return np.ascontiguousarray(np.concatenate(bcd_parts, 0))


def reshard_l2_y(y_parts):
    import ml_dtypes
    Y = np.concatenate([y_parts[j] for j in range(NCORES)], 0)
    outs = []
    for c in range(NCORES):
        t = np.zeros((8, 128, 4096), dtype=Y.dtype)
        for ch in range(8):
            t[ch][:, 0:2048] = Y[ch * 128:(ch + 1) * 128, 0, c * 2048:(c + 1) * 2048]
            t[ch][:, 2048:4096] = Y[ch * 128:(ch + 1) * 128, 1, c * 2048:(c + 1) * 2048]
        outs.append(np.ascontiguousarray(t))
    return outs


def build_l0attn_a(b=None):
    if b is None:
        b = B()
    b.prefix = "l0attn_a_"
    nc = b.nc
    resid = b.din("resid", [2 * HB, D])
    wq = b.din("wq", [D, 512]); wk = b.din("wk", [D, 512])
    wv = b.din("wv", [D, 512]); wsp = b.din("wsp", [D, 512])
    wau = b.din("wau", [D, 64]); wad = b.din("wad", [64, 512])
    wbeta = b.din("wbeta", [D, 8])
    wu1 = b.din("wu1", [D, 64]); wu2 = b.din("wu2", [64, D])
    ginw = b.din("ginw", [8, 128])
    convw = b.din("convw", [8, 128, 4])
    convb = b.din("convb", [8, 128])
    bau = b.din("bau", [64, 1]); bad = b.din("bad", [4, 128])
    bbeta = b.din("bbeta", [8, 1])
    bu1 = b.din("bu1", [64, 1]); bu2 = b.din("bu2", [8, 128])
    identf = b.din("identf", [128, 128])
    qs = b.dout("qs", [8, 128, 2048]); ks = b.dout("ks", [8, 128, 2048])
    vs = b.dout("vs", [8, 128, 2048]); ds = b.dout("ds", [8, 128, 2048])
    aas = b.dout("aas", [8, 128, 2048])
    bbs = b.dout("bbs", [16, 2048])
    gates = b.dout("gates", [8, 128, 4096], BF16)

    with b.seg():
        block = b.block_open()
        skel = Skel(b, block, resid, None)
        ldg = b.ch("ldg"); lds = b.ch("lds"); mm = b.ch("mm"); ev = b.ch("ev")
        cv = b.ch("cv"); pj = b.ch("pj"); ust = b.ch("ust"); ga = b.ch("ga")
        skel.consts(identf)

        wqb = b.sb([128, KD * 512], BF16); wkb = b.sb([128, KD * 512], BF16)
        wvb = b.sb([128, KD * 512], BF16); wspb = b.sb([128, KD * 512], BF16)
        waub = b.sb([128, KD * 64], BF16); wadb = b.sb([64, 512], BF16)
        wbetab = b.sb([128, KD * 8], BF16)
        wu1b = b.sb([128, KD * 64], BF16); wu2b = b.sb([64, D], BF16)
        gin = b.sb([128, 8])
        cwv = b.sb([128, 8 * 4]); cbv = b.sb([128, 8])
        baus = b.sb([64, 1]); bads = b.sb([128, 4])
        bbetas = b.sb([8, 1]); bu1s = b.sb([64, 1]); bu2s = b.sb([128, 8])

        @block.gpsimd
        def _(g):
            for c in range(KD):
                ldg.inc(g.dma_start(out=wqb[:, c * 512:(c + 1) * 512], in_=wq[c * 128:(c + 1) * 128, :]), 16)
                ldg.inc(g.dma_start(out=wkb[:, c * 512:(c + 1) * 512], in_=wk[c * 128:(c + 1) * 128, :]), 16)
                ldg.inc(g.dma_start(out=wvb[:, c * 512:(c + 1) * 512], in_=wv[c * 128:(c + 1) * 128, :]), 16)
                ldg.inc(g.dma_start(out=wspb[:, c * 512:(c + 1) * 512], in_=wsp[c * 128:(c + 1) * 128, :]), 16)
                ldg.inc(g.dma_start(out=waub[:, c * 64:(c + 1) * 64], in_=wau[c * 128:(c + 1) * 128, :]), 16)
                ldg.inc(g.dma_start(out=wbetab[:, c * 8:(c + 1) * 8], in_=wbeta[c * 128:(c + 1) * 128, :]), 16)
                ldg.inc(g.dma_start(out=wu1b[:, c * 64:(c + 1) * 64], in_=wu1[c * 128:(c + 1) * 128, :]), 16)
            ldg.inc(g.dma_start(out=wadb[:], in_=wad), 16)
            ldg.inc(g.dma_start(out=wu2b[:], in_=wu2), 16)

        @block.sync
        def _(sync):
            lds.wt(sync)
            lds.inc(sync.dma_start(out=cwv.rearrange("p (c j) -> p c j", j=4),
                                   in_=convw.rearrange("c p j -> p c j")), 16)
            lds.inc(sync.dma_start(out=baus[:], in_=bau), 16)
            lds.inc(sync.dma_start(out=bbetas[:], in_=bbeta), 16)
            lds.inc(sync.dma_start(out=bu1s[:], in_=bu1), 16)

        eps6 = b.sb([128, 1])

        @block.vector
        def _(vector):
            skel.aux.inc(vector.memset(eps6[:], 1e-6))

        emit_vec_load(b, block, gin, ginw, 1024, skel, skel.ps_tf[0], lds)
        emit_vec_load(b, block, cbv, convb, 1024, skel, skel.ps_tf[0], lds)
        emit_vec_load(b, block, bads, bad, 512, skel, skel.ps_tf[0], lds)
        emit_vec_load(b, block, bu2s, bu2, 1024, skel, skel.ps_tf[0], lds)

        hm = b.sb([128, KD * 2 * HB], BF16)
        xsub = b.sb([128, KD * 128], BF16)
        evg = b.sb([128, 8 * 128], BF16)
        xr1 = b.sb([128, D])
        hnb = b.sb([128, D], BF16)
        au_sb = b.sb([64, 512], BF16)
        gu_sb = b.sb([64, 512], BF16)
        bb_sb = b.sb([8, 4096])
        ps_mm = (b.psum([128, 512]), b.psum([128, 512]))

        for s in range(34):
            h_, k = divmod(s, 17)

            @block.sync
            def _(sync, s=s, h_=h_, k=k):
                if s == 0:
                    mm.wt(sync); ldg.wt(sync); lds.wt(sync)
                else:
                    skel.ln.wt(sync)
                rows = resid[(h_ * HB + k * 128):(h_ * HB + (k + 1) * 128), :]
                skel.ldx.inc(sync.dma_start(out=xr1[:], in_=rows), 16)

            if s >= 1:
                @block.vector
                def _(vector):
                    skel.tr.wt(vector)
                    skel.ln.inc(vector.engine_nop())
            emit_ln_subtile(b, block, xr1[:], skel.xnb[:], skel.sq,
                            skel.stat, skel.ldx, skel.ln, eps_ap=skel.epsb[:])

            @block.vector
            def _(vector):
                skel.ln.wt(vector)
                vector.tensor_scalar(skel.stat[:, 6:7], skel.stat[:, 4:5], 1e-5, None, ALU.add)
                skel.ln.inc(vector.engine_nop())
                skel.ln.wt(vector)
                skel.ln.inc(vector.reciprocal(skel.stat[:, 7:8], skel.stat[:, 6:7]))
                skel.ln.wt(vector)
                skel.ln.inc(vector.tensor_tensor(skel.stat[:, 8:9], skel.stat[:, 4:5],
                                                 skel.stat[:, 7:8], ALU.mult))

            @block.scalar
            def _(scalar):
                skel.ln.wt(scalar)
                skel.ln.inc(scalar.activation(skel.stat[:, 9:10], skel.stat[:, 8:9], AF.Ln, bias=eps6[:, 0:1]))
                skel.ln.wt(scalar)
                skel.ln.inc(scalar.activation(skel.stat[:, 9:10], skel.stat[:, 9:10],
                                              AF.Exp, scale=-0.5))

            @block.vector
            def _(vector):
                skel.ln.wt(vector)
                skel.ln.inc(vector.tensor_scalar(hnb[:], skel.xnb[:], skel.stat[:, 9:10],
                                                 None, ALU.mult))

            if k >= 1:
                @block.vector
                def _(vector):
                    ga.wt(vector)
                    skel.fmc.inc(vector.engine_nop())

                def dstx(c):
                    return xsub[:, c * 128:(c + 1) * 128]
                emit_tm_to_fm(b, block, skel.xnb[:], dstx, skel.identb[:], skel.ps_t,
                              skel.tr, skel.fmc, ready_ch=skel.ln)

                @block.tensor
                def _(tensor, s=s):
                    skel.fmc.wt(tensor)
                    ga.wt(tensor)
                    ev.wt(tensor)
                    for c in range(KD):
                        ins = tensor.matmul(ps_mm[0][0:64, 0:128], wu1b[:, c * 64:(c + 1) * 64],
                                            xsub[:, c * 128:(c + 1) * 128],
                                            start=(c == 0), stop=(c == KD - 1))
                    mm.inc(ins)

                @block.scalar
                def _(scalar):
                    mm.wt(scalar)
                    ga.inc(scalar.activation(gu_sb[:, 0:128], ps_mm[0][0:64, 0:128],
                                             AF.Silu, bias=bu1s[:, 0:1]))

                for o in range(KD):
                    @block.tensor
                    def _(tensor, o=o):
                        ga.wt(tensor)
                        mm.inc(tensor.matmul(ps_mm[1][:, 0:128], wu2b[:, o * 128:(o + 1) * 128],
                                             gu_sb[:, 0:128], start=True, stop=True))

                    @block.scalar
                    def _(scalar, o=o):
                        mm.wt(scalar)
                        ust.wt(scalar)
                        ga.inc(scalar.activation(evg[:, o * 128:(o + 1) * 128],
                                                 ps_mm[1][:, 0:128], AF.Sigmoid,
                                                 bias=bu2s[:, o:o + 1]))

                @block.gpsimd
                def _(g, h_=h_, k=k):
                    ga.wt(g)
                    col = h_ * 2048 + (k - 1) * 128
                    ust.inc(g.dma_start(
                        out=gates[:, :, col:col + 128].rearrange("c p q -> p c q"),
                        in_=evg.rearrange("p (c q) -> p c q", q=128)), 16)

            for c in range(KD):
                pt = skel.ps_t[c % 2]

                @block.tensor
                def _(tensor, c=c, pt=pt):
                    if c == 0:
                        skel.ln.wt(tensor)
                        if skel.fmc.n > 0:
                            skel.fmc.wt(tensor)
                    elif c >= 2:
                        skel.fmc.wt(tensor, skel.fmc.n - 1)
                    skel.tr.inc(tensor.transpose(pt[:], hnb[:, c * 128:(c + 1) * 128],
                                                 skel.identb[:]))

                @block.vector
                def _(vector, c=c, s=s, h_=h_, k=k, pt=pt):
                    skel.tr.wt(vector)
                    dst = hm[:, c * 2 * HB + h_ * HB + k * 128: c * 2 * HB + h_ * HB + (k + 1) * 128]
                    skel.fmc.inc(vector.tensor_scalar(dst, pt[:], gin[:, c:c + 1],
                                                      None, ALU.mult))

        conv_scope = ExitStack()
        cacc = conv_scope.enter_context(nc.sbuf_tensor("cacc", [128, 2048], F32))
        cacc2 = conv_scope.enter_context(nc.sbuf_tensor("cacc2", [128, 2048], F32))
        for c in range(KD):
            for h_ in range(2):
                base = c * 2 * HB + h_ * HB + 128

                @block.vector
                def _(vector, c=c, h_=h_, base=base):
                    skel.fmc.wt(vector)
                    cv.wt(vector)
                    cv.inc(vector.tensor_scalar(cacc[:], hm[:, base - 3: base + 2045],
                                                cwv[:, c * 4:c * 4 + 1], None, ALU.mult))
                    cv.wt(vector)
                    cv.inc(vector.scalar_tensor_tensor(cacc2[:], hm[:, base - 2: base + 2046],
                                                       cwv[:, c * 4 + 1:c * 4 + 2], cacc[:],
                                                       ALU.mult, ALU.add))
                    cv.wt(vector)
                    cv.inc(vector.scalar_tensor_tensor(cacc[:], hm[:, base - 1: base + 2047],
                                                       cwv[:, c * 4 + 2:c * 4 + 3], cacc2[:],
                                                       ALU.mult, ALU.add))
                    cv.wt(vector)
                    cv.inc(vector.scalar_tensor_tensor(cacc2[:], hm[:, base: base + 2048],
                                                       cwv[:, c * 4 + 3:c * 4 + 4], cacc[:],
                                                       ALU.mult, ALU.add))

                @block.scalar
                def _(scalar, c=c, h_=h_, base=base):
                    cv.wt(scalar)
                    cv.inc(scalar.activation(hm[:, base: base + 2048],
                                             cacc2[:], AF.Silu, bias=cbv[:, c:c + 1]))

        conv_scope.close()
        evq = b.sb([128, 4 * 512])
        eva = b.sb([128, 512])
        evg = b.sb([128, 512], BF16)

        def hslice(c, tau):
            hh = tau // 4
            return hm[:, c * 2 * HB + hh * HB + 128 + (tau % 4) * 512:
                      c * 2 * HB + hh * HB + 128 + (tau % 4) * 512 + 512]

        for tau in range(8):
            for (wt_, dst, nm) in ((wqb, qs, "q"), (wkb, ks, "k"), (wvb, vs, "v"), (wspb, ds, "d")):
                for o in range(4):
                    pp = ps_mm[(tau * 4 + o) % 2]

                    @block.tensor
                    def _(tensor, tau=tau, o=o, wt_=wt_, pp=pp):
                        cv.wt(tensor)
                        ga.wt(tensor)
                        ev.wt(tensor, max(0, ev.n - 1))
                        for c in range(KD):
                            ins = tensor.matmul(pp[:], wt_[:, c * 512 + o * 128: c * 512 + (o + 1) * 128],
                                                hslice(c, tau), start=(c == 0), stop=(c == KD - 1))
                        mm.inc(ins)

                    @block.vector
                    def _(vector, tau=tau, o=o, dst=dst, pp=pp):
                        mm.wt(vector)
                        ust.wt(vector)
                        ev.inc(vector.tensor_copy(evq[:, (o % 4) * 512:(o % 4 + 1) * 512], pp[:]))

                    @block.gpsimd
                    def _(g, tau=tau, o=o, dst=dst):
                        ev.wt(g)
                        shard = (tau // 4) * 4 + o
                        ust.inc(g.dma_start(out=dst[shard, :, (tau % 4) * 512:(tau % 4 + 1) * 512],
                                            in_=evq[:, (o % 4) * 512:(o % 4 + 1) * 512]), 16)

            @block.tensor
            def _(tensor, tau=tau):
                cv.wt(tensor)
                ga.wt(tensor)
                ev.wt(tensor)
                for c in range(KD):
                    ins = tensor.matmul(ps_mm[0][0:64, :], waub[:, c * 64:(c + 1) * 64],
                                        hslice(c, tau), start=(c == 0), stop=(c == KD - 1))
                mm.inc(ins)

            @block.scalar
            def _(scalar, tau=tau):
                mm.wt(scalar)
                ga.inc(scalar.activation(au_sb[:], ps_mm[0][0:64, :], AF.Silu, bias=baus[:, 0:1]))

            for o in range(4):
                @block.tensor
                def _(tensor, tau=tau, o=o):
                    ga.wt(tensor)
                    mm.inc(tensor.matmul(ps_mm[1][:, 0:512], wadb[:, o * 128:(o + 1) * 128],
                                         au_sb[:], start=True, stop=True))

                @block.vector
                def _(vector, tau=tau, o=o):
                    mm.wt(vector)
                    ust.wt(vector)
                    ga.inc(vector.tensor_scalar(eva[:], ps_mm[1][:, 0:512],
                                                bads[:, o:o + 1], None, ALU.add))

                @block.gpsimd
                def _(g, tau=tau, o=o):
                    ga.wt(g)
                    shard = (tau // 4) * 4 + o
                    ust.inc(g.dma_start(out=aas[shard, :, (tau % 4) * 512:(tau % 4 + 1) * 512],
                                        in_=eva[:]), 16)

            @block.tensor
            def _(tensor, tau=tau):
                cv.wt(tensor)
                ga.wt(tensor)
                ev.wt(tensor)
                for c in range(KD):
                    ins = tensor.matmul(ps_mm[0][0:8, :], wbetab[:, c * 8:(c + 1) * 8],
                                        hslice(c, tau), start=(c == 0), stop=(c == KD - 1))
                mm.inc(ins)

            @block.vector
            def _(vector, tau=tau):
                mm.wt(vector)
                ga.inc(vector.tensor_scalar(bb_sb[:, tau * 512:(tau + 1) * 512],
                                            ps_mm[0][0:8, :], bbetas[:, 0:1], None, ALU.add))

        @block.sync
        def _(sync):
            ga.wt(sync)
            skel.sto.inc(sync.dma_start(
                out=bbs.rearrange("(h i) t -> i h t", h=2),
                in_=bb_sb.rearrange("i (h t) -> i h t", h=2)), 16)
        skel.finish()

        @block.gpsimd
        def _(g):
            ust.wt(g)
    return b


def build_l0attn_b(b=None):
    if b is None:
        b = B()
    b.prefix = "l0attn_b_"
    nc = b.nc
    q_in = b.din("q_in", [8, 128, 2048]).rearrange("r p t -> p r t")
    k_in = b.din("k_in", [8, 128, 2048]).rearrange("r p t -> p r t")
    v_in = b.din("v_in", [8, 128, 2048]).rearrange("r p t -> p r t")
    d_in = b.din("d_in", [8, 128, 2048]).rearrange("r p t -> p r t")
    aa_in = b.din("aa_in", [8, 128, 2048]).rearrange("r p t -> p r t")
    bb_in = b.din("bb_in", [16, 2048]).rearrange("(r i) t -> r i t", i=2)
    asp_col = b.din("asp_col", [128, 1])
    bsppat = b.din("bsppat", [128, 2])
    pat2 = b.din("pat2", [128, 2])
    pat2t = b.din("pat2t", [2, 128])
    masku = b.din("masku", [128, 64])
    maskl = b.din("maskl", [128, 64])
    maskui = b.din("maskui", [128, 64])
    identf = b.din("identf", [128, 128])
    ys = b.dout("ys", [128, 16384])
    betad = b.dtmp("betad", [128, 16384], BF16)

    T = 16384
    W = 256
    BL = 2048
    NB = T // BL
    CB = W + BL
    with b.seg():
        block = b.block_open()
        ld = b.ch("ld"); ldr = b.ch("ldr"); mm = b.ch("mm")
        vv = b.ch("vv"); sc = b.ch("sc"); yy = b.ch("yy"); sto = b.ch("sto")
        bst = b.ch("bst"); ldv = b.ch("ldv"); ldvg = b.ch("ldvg"); wyc = b.ch("wyc")
        ldw2 = b.ch("ldw2")

        lam = b.sb([128, T])
        aspc = b.sb([128, 1]); bspp = b.sb([128, 2])
        p2 = b.sb([128, 2]); p2t = b.sb([2, 128])
        msku = b.sb([128, 64]); mskui = b.sb([128, 64]); mskl = b.sb([128, 64])
        idf = b.sb([128, 128])
        identb2 = b.sb([128, 128], BF16)
        ones64 = b.sb([128, 64])
        rowt = b.sb([2, 512]); rowt2 = b.sb([2, 512])
        bchunk = b.sb([128, 512], BF16)

        @block.sync
        def _(sync):
            ld.inc(sync.dma_start(out=aspc[:], in_=asp_col), 16)
            ld.inc(sync.dma_start(out=bspp[:], in_=bsppat), 16)
            ld.inc(sync.dma_start(out=msku[:], in_=masku), 16)
            ld.inc(sync.dma_start(out=mskl[:], in_=maskl), 16)
            ld.inc(sync.dma_start(out=mskui[:], in_=maskui), 16)
            ld.inc(sync.dma_start(out=idf[:], in_=identf), 16)
            ld.inc(sync.dma_start(out=p2[:], in_=pat2), 16)
            ld.inc(sync.dma_start(out=p2t[:], in_=pat2t), 16)

        p2b = b.sb([128, 2], BF16)
        bsppb = b.sb([128, 2], BF16)

        @block.vector
        def _(vector):
            ld.wt(vector)
            vv.inc(vector.memset(ones64[:], 1.0))
            vv.wt(vector)
            vv.inc(vector.tensor_copy(identb2[:], idf[:]))
            vv.wt(vector)
            vv.inc(vector.tensor_copy(p2b[:], p2[:]))
            vv.wt(vector)
            vv.inc(vector.tensor_copy(bsppb[:], bspp[:]))

        ps_scope = ExitStack()
        ps_a = ps_scope.enter_context(nc.psum_tensor("ps_a", [128, 512], F32))
        ps_b = ps_scope.enter_context(nc.psum_tensor("ps_b", [128, 512], F32))
        ps_c = ps_scope.enter_context(nc.psum_tensor("ps_c", [128, 512], F32))
        ps_d = ps_scope.enter_context(nc.psum_tensor("ps_d", [128, 512], F32))

        with ExitStack() as s1:
            drw = s1.enter_context(nc.sbuf_tensor("drw", [128, NB * CB], F32))
            spk = s1.enter_context(nc.sbuf_tensor("spk", [128, T], BF16))
            mem = s1.enter_context(nc.sbuf_tensor("mem", [128, 16], F32))
            spkpad = s1.enter_context(nc.sbuf_tensor("spkpad", [128, 8], BF16))
            abuf = s1.enter_context(nc.sbuf_tensor("abuf", [128, 2048], F32))
            sps = s1.enter_context(nc.sbuf_tensor("sps", [128, 512], F32))
            bbg = s1.enter_context(nc.sbuf_tensor("bbg", [2, 2048], F32))

            @block.sync
            def _(sync):
                for j in range(NB):
                    if j == 0:
                        ldw2.inc(sync.dma_start(out=drw[:, W:CB],
                                              in_=d_in[:, 0]), 16)
                    else:
                        ldw2.inc(sync.dma_start(out=drw[:, j * CB:j * CB + W],
                                              in_=d_in[:, j - 1, BL - W:BL]), 16)
                        ldw2.inc(sync.dma_start(out=drw[:, j * CB + W:(j + 1) * CB],
                                              in_=d_in[:, j]), 16)

            @block.vector
            def _(vector):
                vv.inc(vector.memset(drw[:, 0:W], 0.0))
                vv.inc(vector.memset(mem[:], 0.0))

            drw3 = drw.rearrange("p (j c) -> p j c", c=CB)
            spk3 = spk.rearrange("p (j c) -> p j c", c=BL)

            @block.vector
            def _(vector):
                ld.wt(vector)
                ldw2.wt(vector)
                vv.wt(vector)
                m0, m1 = mem[:, 0:8], mem[:, 8:16]
                for tau in range(CB):
                    vv.inc(vector.scalar_tensor_tensor(m1, m0, 0.9, drw3[:, :, tau],
                                                       ALU.mult, ALU.add))
                    vv.wt(vector)
                    dst = spk3[:, :, tau - W] if tau >= W else spkpad[:, 0:8]
                    vv.inc(vector.tensor_scalar(dst, m1, 0.5, None, ALU.is_gt))
                    vv.wt(vector)
                    vv.inc(vector.scalar_tensor_tensor(m0, dst, -0.5, m1, ALU.mult, ALU.add))
                    vv.wt(vector)

            for g8 in range(T // 2048):
                @block.sync
                def _(sync, g8=g8):
                    vv.wt(sync)
                    ldr.wt(sync)
                    ldr.inc(sync.dma_start(out=abuf[:], in_=aa_in[:, g8]), 16)
                    ldr.inc(sync.dma_start(out=bbg[:], in_=bb_in[g8]), 16)

                for q4 in range(4):
                    blk = g8 * 4 + q4
                    cs = slice(blk * 512, (blk + 1) * 512)
                    csl = slice(q4 * 512, (q4 + 1) * 512)

                    @block.tensor
                    def _(tensor, cs=cs):
                        ldr.wt(tensor)
                        vv.wt(tensor)
                        sc.wt(tensor)
                        mm.inc(tensor.matmul(ps_a[0:2, :], p2b[:], spk[:, cs], start=True, stop=True))
                        mm.inc(tensor.matmul(ps_c[0:2, :], bsppb[:], spk[:, cs], start=True, stop=True))

                    @block.vector
                    def _(vector, csl=csl):
                        mm.wt(vector)
                        vv.inc(vector.tensor_scalar(rowt[:], ps_a[0:2, :], 0.0, None, ALU.is_gt))
                        vv.wt(vector)
                        vv.inc(vector.tensor_tensor(rowt2[:], ps_c[0:2, :], bbg[:, csl], ALU.add))

                    @block.scalar
                    def _(scalar):
                        vv.wt(scalar)
                        sc.inc(scalar.activation(rowt2[:], rowt2[:], AF.Sigmoid))

                    @block.vector
                    def _(vector):
                        sc.wt(vector)
                        vv.inc(vector.tensor_tensor(rowt2[:], rowt2[:], rowt[:], ALU.mult))

                    @block.tensor
                    def _(tensor):
                        vv.wt(tensor)
                        mm.inc(tensor.matmul(ps_b[:], p2t[:], rowt[:], start=True, stop=True))
                        mm.inc(tensor.matmul(ps_d[:], p2t[:], rowt2[:], start=True, stop=True))

                    @block.vector
                    def _(vector, cs=cs, csl=csl):
                        vv.wt(vector)
                        vv.inc(vector.scalar_tensor_tensor(lam[:, cs], spk[:, cs], aspc[:, 0:1],
                                                           abuf[:, csl], ALU.mult, ALU.add))

                    @block.scalar
                    def _(scalar, cs=cs):
                        vv.wt(scalar)
                        sc.inc(scalar.activation(sps[:], lam[:, cs], AF.Abs))
                        sc.wt(scalar)
                        sc.inc(scalar.activation(sps[:], sps[:], AF.Exp, scale=-1.0))
                        sc.wt(scalar)
                        sc.inc(scalar.activation(sps[:], sps[:], AF.Ln, bias=1.0))

                    @block.vector
                    def _(vector, cs=cs):
                        sc.wt(vector)
                        vv.inc(vector.tensor_scalar(lam[:, cs], lam[:, cs], 0.0, None, ALU.min))
                        vv.wt(vector)
                        vv.inc(vector.tensor_tensor(lam[:, cs], lam[:, cs], sps[:], ALU.subtract))
                        vv.wt(vector)
                        mm.wt(vector)
                        vv.inc(vector.tensor_tensor(lam[:, cs], lam[:, cs], ps_b[:], ALU.mult))
                        vv.wt(vector)
                        vv.inc(vector.tensor_copy(bchunk[:], ps_d[:]))

                    @block.gpsimd
                    def _(g, cs=cs):
                        vv.wt(g)
                        bst.inc(g.dma_start(out=betad[:, cs], in_=bchunk[:]), 16)

        qn = b.sb([128, T], BF16)
        kn = b.sb([128, T], BF16)
        qld = b.sb([128, 512])
        qsq = b.sb([128, 512])

        for (src_ap, dstn) in ((q_in, qn), (k_in, kn)):
            for blk in range(T // 512):
                cs = slice(blk * 512, (blk + 1) * 512)

                @block.sync
                def _(sync, src_ap=src_ap, blk=blk):
                    vv.wt(sync)
                    ldr.wt(sync)
                    ldr.inc(sync.dma_start(out=qld[:], in_=src_ap[:, blk // 4, (blk % 4) * 512:(blk % 4 + 1) * 512]), 16)

                @block.vector
                def _(vector):
                    ldr.wt(vector)
                    vv.wt(vector)
                    vv.inc(vector.tensor_tensor(qsq[:], qld[:], qld[:], ALU.mult))

                @block.tensor
                def _(tensor):
                    vv.wt(tensor)
                    mm.wt(tensor)
                    mm.inc(tensor.matmul(ps_a[0:2, :], p2[:], qsq[:], start=True, stop=True))

                @block.scalar
                def _(scalar):
                    mm.wt(scalar)
                    sc.inc(scalar.activation(rowt[:], ps_a[0:2, :], AF.Sqrt))

                @block.vector
                def _(vector):
                    sc.wt(vector)
                    vv.inc(vector.tensor_scalar(rowt[:], rowt[:], 1e-6, None, ALU.add))
                    vv.wt(vector)
                    vv.inc(vector.reciprocal(rowt2[:], rowt[:]))

                @block.tensor
                def _(tensor):
                    vv.wt(tensor)
                    mm.inc(tensor.matmul(ps_b[:], p2t[:], rowt2[:], start=True, stop=True))

                @block.vector
                def _(vector, dstn=dstn, cs=cs):
                    mm.wt(vector)
                    vv.inc(vector.tensor_tensor(dstn[:, cs], qld[:], ps_b[:], ALU.mult))

        ps_scope.close()
        vb = (b.sb([128, 2048], BF16), b.sb([128, 2048], BF16))
        bbuf = (b.sb([128, 2048], BF16), b.sb([128, 2048], BF16))
        ystage = (b.sb([128, 2048]), b.sb([128, 2048]))
        lamc = b.sb([128, 64])
        colst = b.sb([128, 8])
        fe1 = b.sb([128, 64]); fe2 = b.sb([128, 64])
        fgam = b.sb([128, 64]); fgrat = b.sb([128, 64])
        ktb = b.sb([128, 64], BF16); k2b = b.sb([128, 64], BF16)
        qtb = b.sb([128, 64], BF16); q2b = b.sb([128, 64], BF16)
        bkb = b.sb([128, 64]); wbb = b.sb([128, 64], BF16); w2b_ = b.sb([128, 64], BF16)
        Lsb = b.sb([128, 64], BF16); LTsb = b.sb([128, 64], BF16)
        Psb = b.sb([128, 64], BF16); PTsb = b.sb([128, 64], BF16)
        Pnb = b.sb([128, 64], BF16); PnTb = b.sb([128, 64], BF16)
        MTsb = b.sb([128, 64], BF16)
        Uf = b.sb([128, 64]); Ub = b.sb([128, 64], BF16)
        Sf = b.sb([128, 64]); Sb = b.sb([128, 64], BF16)
        Vtm = b.sb([128, 64], BF16)
        Ysc = b.sb([128, 64])
        ps_w = (b.psum([128, 64]), b.psum([128, 64]), b.psum([128, 64]), b.psum([128, 64]))
        ps_wt = b.psum([128, 64], BF16)

        @block.vector
        def _(vector):
            vv.wt(vector)
            wyc.inc(vector.memset(Sf[:], 0.0))
            wyc.wt(vector)
            wyc.inc(vector.tensor_copy(Sb[:], Sf[:]))

        NCH = T // 64
        for c in range(NCH):
            g2 = c // 32
            if c % 32 == 0:
                @block.sync
                def _(sync, g2=g2, c=c):
                    ldv.wt(sync)
                    if c > 0:
                        yy.wt(sync)
                        wyc.wt(sync)
                    bst.wt(sync)
                    ldv.inc(sync.dma_start(out=bbuf[g2 % 2][:], in_=betad[:, g2 * 2048:(g2 + 1) * 2048]), 16)

                @block.gpsimd
                def _(g, g2=g2, c=c):
                    ldvg.wt(g)
                    if c > 0:
                        wyc.wt(g)
                    ldvg.inc(g.dma_start(out=vb[g2 % 2][:], in_=v_in[:, g2]), 16)

                @block.vector
                def _(vector, c=c):
                    if c > 0:
                        sto.wt(vector)
                    yy.inc(vector.engine_nop())

            co = (c % 32) * 64
            vchunk = vb[g2 % 2][:, co:co + 64]
            bchunk2 = bbuf[g2 % 2][:, co:co + 64]
            cs = slice(c * 64, (c + 1) * 64)

            @block.vector
            def _(vector, cs=cs):
                vv.wt(vector)
                wyc.wt(vector)
                sc.wt(vector)
                wyc.inc(vector.tensor_tensor_scan(lamc[:], ones64[:], lam[:, cs],
                                                  0.0, ALU.mult, ALU.add))
                wyc.wt(vector)
                wyc.inc(vector.tensor_scalar(colst[:, 0:1], lamc[:, 63:64], 0.5, None, ALU.mult))
                wyc.wt(vector)
                wyc.inc(vector.tensor_scalar(colst[:, 1:2], colst[:, 0:1], -1.0, None, ALU.mult))

            @block.scalar
            def _(scalar):
                wyc.wt(scalar)
                sc.inc(scalar.activation(fe1[:], lamc[:], AF.Exp, bias=colst[:, 1:2]))
                sc.wt(scalar)
                sc.inc(scalar.activation(fe2[:], lamc[:], AF.Exp, scale=-1.0, bias=colst[:, 0:1]))
                sc.wt(scalar)
                sc.inc(scalar.activation(fgam[:], lamc[:], AF.Exp))
                sc.wt(scalar)
                sc.inc(scalar.activation(fgrat[:], lamc[:], AF.Exp, scale=-1.0,
                                         bias=lamc[:, 63:64]))
                sc.wt(scalar)
                sc.inc(scalar.activation(colst[:, 2:3], lamc[:, 63:64], AF.Exp))

            @block.vector
            def _(vector, cs=cs, bchunk2=bchunk2):
                sc.wt(vector)
                ldv.wt(vector)
                ldvg.wt(vector)
                wyc.inc(vector.tensor_tensor(ktb[:], kn[:, cs], fe1[:], ALU.mult))
                wyc.wt(vector)
                wyc.inc(vector.tensor_tensor(k2b[:], kn[:, cs], fgam[:], ALU.mult))
                wyc.wt(vector)
                wyc.inc(vector.tensor_tensor(qtb[:], qn[:, cs], fe1[:], ALU.mult))
                wyc.wt(vector)
                wyc.inc(vector.tensor_tensor(q2b[:], qn[:, cs], fgam[:], ALU.mult))
                wyc.wt(vector)
                wyc.inc(vector.tensor_tensor(bkb[:], kn[:, cs], bchunk2, ALU.mult))
                wyc.wt(vector)
                wyc.inc(vector.tensor_tensor(wbb[:], bkb[:], fe2[:], ALU.mult))
                wyc.wt(vector)
                wyc.inc(vector.tensor_tensor(w2b_[:], bkb[:], fgrat[:], ALU.mult))

            for p in range(2):
                r0 = p * 64
                rs = slice(r0, r0 + 64)

                def MM(tensor, out_ps, lhsT, rhs, start=True, stop=True):
                    return tensor.matmul(out_ps, lhsT, rhs, start=start, stop=stop)

                @block.tensor
                def _(tensor, p=p, rs=rs, vchunk=vchunk):
                    wyc.wt(tensor)
                    yy.wt(tensor)
                    mm.inc(tensor.transpose(ps_wt[0:64, :], vchunk[rs, :],
                                            identb2[rs, p * 64:(p + 1) * 64]))

                @block.vector
                def _(vector, p=p, rs=rs):
                    mm.wt(vector)
                    wyc.inc(vector.tensor_copy(Vtm[rs, :], ps_wt[0:64, :]))

                @block.tensor
                def _(tensor, p=p, rs=rs):
                    wyc.wt(tensor)
                    mm.inc(MM(tensor, ps_w[0][rs, :], ktb[rs, :], wbb[rs, :]))
                    mm.inc(MM(tensor, ps_w[1][rs, :], wbb[rs, :], ktb[rs, :]))

                @block.vector
                def _(vector, p=p, rs=rs):
                    mm.wt(vector)
                    wyc.inc(vector.tensor_tensor(Lsb[rs, :], ps_w[0][rs, :],
                                                 mskl[rs, :], ALU.mult))
                    wyc.wt(vector)
                    wyc.inc(vector.tensor_tensor(LTsb[rs, :], ps_w[1][rs, :],
                                                 msku[rs, :], ALU.mult))

                @block.tensor
                def _(tensor, p=p, rs=rs):
                    wyc.wt(tensor)
                    mm.inc(MM(tensor, ps_w[2][rs, :], k2b[rs, :], Sb[rs, :]))

                @block.vector
                def _(vector, p=p, rs=rs):
                    mm.wt(vector)
                    wyc.inc(vector.tensor_tensor(Uf[rs, :], Vtm[rs, :], ps_w[2][rs, :],
                                                 ALU.subtract))
                    wyc.wt(vector)
                    wyc.inc(vector.tensor_copy(Ub[rs, :], Uf[rs, :]))

                @block.tensor
                def _(tensor, p=p, rs=rs):
                    wyc.wt(tensor)
                    mm.inc(MM(tensor, ps_w[3][rs, :], LTsb[rs, :], Ub[rs, :]))

                @block.vector
                def _(vector, p=p, rs=rs):
                    mm.wt(vector)
                    wyc.inc(vector.tensor_tensor(Uf[rs, :], Uf[rs, :], ps_w[3][rs, :],
                                                 ALU.subtract))
                    wyc.wt(vector)
                    wyc.inc(vector.tensor_copy(Ub[rs, :], Uf[rs, :]))
                    wyc.wt(vector)
                    wyc.inc(vector.tensor_copy(Psb[rs, :], Lsb[rs, :]))
                    wyc.wt(vector)
                    wyc.inc(vector.tensor_copy(PTsb[rs, :], LTsb[rs, :]))

                for lev in range(5):
                    @block.tensor
                    def _(tensor, p=p, rs=rs):
                        wyc.wt(tensor)
                        mm.inc(MM(tensor, ps_w[0][rs, :], PTsb[rs, :], Psb[rs, :]))
                        mm.inc(MM(tensor, ps_w[1][rs, :], Psb[rs, :], PTsb[rs, :]))

                    @block.vector
                    def _(vector, p=p, rs=rs):
                        mm.wt(vector)
                        wyc.inc(vector.tensor_copy(Pnb[rs, :], ps_w[0][rs, :]))
                        wyc.wt(vector)
                        wyc.inc(vector.tensor_copy(PnTb[rs, :], ps_w[1][rs, :]))

                    @block.tensor
                    def _(tensor, p=p, rs=rs):
                        wyc.wt(tensor)
                        mm.inc(MM(tensor, ps_w[2][rs, :], PnTb[rs, :], Ub[rs, :]))

                    @block.vector
                    def _(vector, p=p, rs=rs):
                        mm.wt(vector)
                        wyc.inc(vector.tensor_tensor(Uf[rs, :], Uf[rs, :], ps_w[2][rs, :],
                                                     ALU.add))
                        wyc.wt(vector)
                        wyc.inc(vector.tensor_copy(Ub[rs, :], Uf[rs, :]))
                        wyc.wt(vector)
                        wyc.inc(vector.tensor_copy(Psb[rs, :], Pnb[rs, :]))
                        wyc.wt(vector)
                        wyc.inc(vector.tensor_copy(PTsb[rs, :], PnTb[rs, :]))

                @block.tensor
                def _(tensor, p=p, rs=rs):
                    wyc.wt(tensor)
                    mm.inc(MM(tensor, ps_w[0][rs, :], wbb[rs, :], qtb[rs, :]))

                @block.vector
                def _(vector, p=p, rs=rs):
                    mm.wt(vector)
                    wyc.inc(vector.tensor_tensor(MTsb[rs, :], ps_w[0][rs, :],
                                                 mskui[rs, :], ALU.mult))

                @block.tensor
                def _(tensor, p=p, rs=rs):
                    wyc.wt(tensor)
                    MM(tensor, ps_w[1][rs, :], MTsb[rs, :], Ub[rs, :], start=True, stop=False)
                    mm.inc(MM(tensor, ps_w[1][rs, :], q2b[rs, :], Sb[rs, :], start=False, stop=True))

                @block.vector
                def _(vector, p=p, rs=rs):
                    mm.wt(vector)
                    yy.inc(vector.tensor_copy(Ysc[rs, :], ps_w[1][rs, :]))

                @block.tensor
                def _(tensor, p=p, rs=rs):
                    yy.wt(tensor)
                    mm.inc(tensor.transpose(ps_w[2][0:64, :], Ysc[rs, :],
                                            idf[rs, p * 64:(p + 1) * 64]))

                @block.vector
                def _(vector, p=p, rs=rs, g2=g2, co=co):
                    mm.wt(vector)
                    yy.inc(vector.tensor_copy(ystage[g2 % 2][rs, co:co + 64], ps_w[2][0:64, :]))

                @block.tensor
                def _(tensor, p=p, rs=rs):
                    wyc.wt(tensor)
                    mm.inc(tensor.transpose(ps_wt[0:64, :], w2b_[rs, :],
                                            identb2[rs, p * 64:(p + 1) * 64]))

                @block.vector
                def _(vector, p=p, rs=rs):
                    mm.wt(vector)
                    wyc.inc(vector.tensor_copy(Pnb[rs, :], ps_wt[0:64, :]))

                @block.tensor
                def _(tensor, p=p, rs=rs):
                    wyc.wt(tensor)
                    mm.inc(MM(tensor, ps_w[3][rs, :], Pnb[rs, :], Ub[rs, :]))

                @block.vector
                def _(vector, p=p, rs=rs):
                    mm.wt(vector)
                    wyc.inc(vector.scalar_tensor_tensor(Sf[rs, :], Sf[rs, :], colst[rs, 2:3],
                                                        ps_w[3][rs, :], ALU.mult, ALU.add))
                    wyc.wt(vector)
                    wyc.inc(vector.tensor_copy(Sb[rs, :], Sf[rs, :]))

            if c % 32 == 31:
                @block.sync
                def _(sync, g2=g2):
                    yy.wt(sync)
                    sto.inc(sync.dma_start(out=ys[:, g2 * 2048:(g2 + 1) * 2048],
                                           in_=ystage[g2 % 2][:]), 16)

        @block.sync
        def _(sync):
            sto.wt(sync)

        @block.gpsimd
        def _(g):
            bst.wt(g)
    return b


def build_l0attn_c(b=None):
    if b is None:
        b = B()
    b.prefix = "l0attn_c_"
    nc = b.nc
    resid = b.din("resid", [ROWS, D])
    ytile = b.din("ytile", [8, 128, 2048])
    gates = b.din("gates", [8, 128, 4096], BF16)
    wo = b.din("wo", [512, D])
    hnwv = b.din("hnwv", [4, 128])
    ffw = b.din("ffw", [8, 128])
    hpatr = b.din("hpatr", [4, 128, 8])
    hpatb = b.din("hpatb", [4, 8, 128])
    wf1t = b.din("wf1t", [8, 32, 128, 128])
    wf3t = b.din("wf3t", [8, 32, 128, 128])
    wf2t = b.din("wf2t", [32, 8, 128, 128])
    identf = b.din("identf", [128, 128])
    out = b.dout("out", [ROWS, D])
    wf1b = b.dtmp("wf1b", [8, 32, 128, 128], BF16)
    wf3b = b.dtmp("wf3b", [8, 32, 128, 128], BF16)
    wf2b = b.dtmp("wf2b", [32, 8, 128, 128], BF16)

    EPS2 = 1e-6
    with b.seg():
        block = b.block_open()
        skel = Skel(b, block, resid, out)
        ldg = b.ch("ldg"); lds = b.ch("lds"); mm = b.ch("mm"); ev = b.ch("ev")
        wp = b.ch("wp"); yl = b.ch("yl"); hv = b.ch("hv"); rz = b.ch("rz")
        chs = {"wldA": b.ch("wldA"), "wldB": b.ch("wldB"),
               "mm": b.ch("mmf"), "ev": b.ch("evf")}
        skel.consts(identf)

        @block.gpsimd
        def _(g):
            for (s_, d_) in ((wf1t, wf1b), (wf3t, wf3b), (wf2t, wf2b)):
                wp.inc(g.dma_start(out=d_.rearrange("a c p q -> (a c p) q"),
                                   in_=s_.rearrange("a c p q -> (a c p) q")), 16)

        wob = b.sb([128, 4 * D], BF16)
        hnw = b.sb([128, 4])
        ffg = b.sb([128, 8])
        eps2 = b.sb([128, 1])
        hpr = b.sb([128, 4 * 8])
        hpb = b.sb([8, 4 * 128])

        @block.gpsimd
        def _(g):
            for c in range(4):
                ldg.inc(g.dma_start(out=wob[:, c * D:(c + 1) * D],
                                    in_=wo[c * 128:(c + 1) * 128, :]), 16)
            ldg.inc(g.dma_start(out=hpr.rearrange("p (c x) -> p c x", x=8),
                                in_=hpatr.rearrange("c p x -> p c x")), 16)
            ldg.inc(g.dma_start(out=hpb.rearrange("p (c x) -> p c x", x=128),
                                in_=hpatb.rearrange("c p x -> p c x")), 16)

        emit_vec_load(b, block, hnw, hnwv, 512, skel, skel.ps_tf[0], lds)
        emit_vec_load(b, block, ffg, ffw, 1024, skel, skel.ps_tf[0], lds)

        @block.vector
        def _(vector):
            skel.aux.inc(vector.memset(eps2[:], EPS2))

        @block.vector
        def _(vector):
            ldg.wt(vector)
            lds.wt(vector)
            skel.aux.wt(vector)
            for c in range(4):
                yl.inc(vector.tensor_scalar(wob[:, c * D:(c + 1) * D],
                                            wob[:, c * D:(c + 1) * D],
                                            hnw[:, c:c + 1], None, ALU.mult))

        yfm = b.sb([128, 4 * TT])
        ysq = b.sb([128, 4 * TT])
        rrow = b.sb([8, TT])
        ynf = b.sb([128, 4 * TT], BF16)
        gbuf = b.sb([128, 8 * TT], BF16)
        y2 = b.sb([128, KD * TT])
        zsq = b.sb([128, TT])
        zrow = b.sb([1, TT])
        zfm = b.sb([128, KD * TT], BF16)
        hsw = b.sb([128, 32 * TT], BF16)
        ofm = b.sb([128, KD * TT])
        wbufs = (b.sb([128, 32 * 128], BF16), b.sb([128, 32 * 128], BF16))
        psums = (b.psum([128, TT]), b.psum([128, TT]))
        ps_h = b.psum([128, TT])
        ps_r = b.psum([1, TT])

        for t in range(NT):
            skel.load_rows(t, war_chs=(skel.ad,) if t else (chs["mm"], wp))
            skel.ln_fmize(t, mm_war_ch=chs["mm"])

            @block.sync
            def _(sync, t=t):
                yl.wt(sync)
                if t >= 1:
                    mm.wt(sync)
                    chs["mm"].wt(sync)
                    hv.wt(sync)
                for c in range(8):
                    yl.inc(sync.dma_start(out=gbuf[:, c * TT:(c + 1) * TT],
                                          in_=gates[c, :, t * TT:(t + 1) * TT]), 16)
                for c in range(4):
                    bq, colq = divmod(t * TT, 2048)
                    yl.inc(sync.dma_start(out=yfm[:, c * TT:(c + 1) * TT],
                                          in_=ytile[bq * 4 + c, :, colq:colq + TT]), 16)

            @block.scalar
            def _(scalar, t=t):
                yl.wt(scalar)
                rz.wt(scalar)
                for c in range(4):
                    rz.inc(scalar.activation(ysq[:, c * TT:(c + 1) * TT],
                                             yfm[:, c * TT:(c + 1) * TT], AF.Square))

            @block.tensor
            def _(tensor, t=t):
                rz.wt(tensor)
                if t >= 1:
                    hv.wt(tensor)
                for c in range(4):
                    ins = tensor.matmul(ps_h[0:8, :], hpr[:, c * 8:(c + 1) * 8],
                                        ysq[:, c * TT:(c + 1) * TT],
                                        start=(c == 0), stop=(c == 3))
                mm.inc(ins)

            @block.scalar
            def _(scalar):
                mm.wt(scalar)
                rz.inc(scalar.activation(rrow[:], ps_h[0:8, :], AF.Ln, bias=eps2[0:8, 0:1],
                                  scale=1.0 / 64))
                rz.wt(scalar)
                rz.inc(scalar.activation(rrow[:], rrow[:], AF.Exp, scale=-0.5))

            for c in range(4):
                @block.tensor
                def _(tensor, c=c):
                    rz.wt(tensor)
                    hv.wt(tensor)
                    mm.inc(tensor.matmul(ps_h[:], hpb[:, c * 128:(c + 1) * 128],
                                         rrow[:], start=True, stop=True))

                @block.vector
                def _(vector, c=c):
                    mm.wt(vector)
                    hv.inc(vector.tensor_tensor(ynf[:, c * TT:(c + 1) * TT],
                                                yfm[:, c * TT:(c + 1) * TT],
                                                ps_h[:], ALU.mult))

            for o in range(KD):
                pp = psums[o % 2]

                @block.tensor
                def _(tensor, o=o, pp=pp, t=t):
                    hv.wt(tensor)
                    if o < 2:
                        if t >= 1 or True:
                            ev.wt(tensor)
                    else:
                        ev.wt(tensor, ev.n - 1)
                    for c in range(4):
                        ins = tensor.matmul(pp[:], wob[:, c * D + o * 128: c * D + (o + 1) * 128],
                                            ynf[:, c * TT:(c + 1) * TT],
                                            start=(c == 0), stop=(c == 3))
                    mm.inc(ins)

                @block.vector
                def _(vector, o=o, pp=pp):
                    mm.wt(vector)
                    ev.inc(vector.tensor_tensor(y2[:, o * TT:(o + 1) * TT], pp[:],
                                                gbuf[:, o * TT:(o + 1) * TT], ALU.mult))
                    ev.wt(vector)
                    ev.inc(vector.tensor_tensor(y2[:, o * TT:(o + 1) * TT],
                                                y2[:, o * TT:(o + 1) * TT],
                                                skel.xfm[:, o * TT:(o + 1) * TT], ALU.add))

            for o in range(KD):
                @block.scalar
                def _(scalar, o=o):
                    ev.wt(scalar)
                    rz.wt(scalar)
                    rz.inc(scalar.activation(zsq[:], y2[:, o * TT:(o + 1) * TT], AF.Square))

                @block.tensor
                def _(tensor, o=o):
                    rz.wt(tensor)
                    mm.inc(tensor.matmul(ps_r[0:1, :], skel.ones128[:], zsq[:],
                                         start=(o == 0), stop=(o == KD - 1)))

            @block.scalar
            def _(scalar):
                mm.wt(scalar)
                rz.inc(scalar.activation(zrow[:], ps_r[0:1, :], AF.Ln, bias=eps2[0:1, 0:1],
                                  scale=1.0 / 1024))
                rz.wt(scalar)
                rz.inc(scalar.activation(zrow[:], zrow[:], AF.Exp, scale=-0.5))

            @block.tensor
            def _(tensor):
                rz.wt(tensor)
                hv.wt(tensor)
                mm.inc(tensor.matmul(ps_h[:], skel.ones_row[:], zrow[:],
                                     start=True, stop=True))

            for o in range(KD):
                @block.vector
                def _(vector, o=o):
                    mm.wt(vector)
                    hv.wt(vector)
                    hv.inc(vector.tensor_tensor(zsq[:], y2[:, o * TT:(o + 1) * TT],
                                                ps_h[:], ALU.mult))
                    hv.wt(vector)
                    hv.inc(vector.tensor_scalar(zfm[:, o * TT:(o + 1) * TT], zsq[:],
                                                ffg[:, o:o + 1], None, ALU.mult))

            def ev_f1(f, pst, t=t):
                @block.scalar
                def _(scalar, f=f, pst=pst):
                    chs["mm"].wt(scalar)
                    chs["ev"].inc(scalar.activation(hsw[:, f * TT:(f + 1) * TT], pst[:],
                                                    AF.Silu))
            emit_stream_matmul(b, block, skel, src_fm=zfm, src_tt=TT, kc=8, fc=32,
                               wtiles=wf1b, wbufs=wbufs, psums=psums, chs=chs,
                               evac=ev_f1, t=t, first_tile=(t == 0), wait_src_ch=hv,
                               wready_ch=wp)

            def ev_f3(f, pst, t=t):
                @block.vector
                def _(vector, f=f, pst=pst):
                    chs["mm"].wt(vector)
                    chs["ev"].inc(vector.scalar_tensor_tensor(
                        hsw[:, f * TT:(f + 1) * TT], pst[:], 1.0,
                        hsw[:, f * TT:(f + 1) * TT], ALU.mult, ALU.mult))
            emit_stream_matmul(b, block, skel, src_fm=zfm, src_tt=TT, kc=8, fc=32,
                               wtiles=wf3b, wbufs=wbufs, psums=psums, chs=chs,
                               evac=ev_f3, t=t, first_tile=False, wait_src_ch=chs["ev"])

            def ev_f2(f, pst, t=t):
                @block.vector
                def _(vector, f=f, pst=pst):
                    chs["mm"].wt(vector)
                    chs["ev"].inc(vector.tensor_tensor(ofm[:, f * TT:(f + 1) * TT], pst[:],
                                                       y2[:, f * TT:(f + 1) * TT], ALU.add))
            emit_stream_matmul(b, block, skel, src_fm=hsw, src_tt=TT, kc=32, fc=8,
                               wtiles=wf2b, wbufs=wbufs, psums=psums, chs=chs,
                               evac=ev_f2, t=t, first_tile=False, wait_src_ch=chs["ev"])

            skel.epilogue(t, ofm, chs["ev"])
        skel.finish()
    return b


def l0_reshard_qkv(parts_dict, key):
    outs = []
    for j in range(NCORES):
        outs.append(np.ascontiguousarray(
            np.stack([parts_dict[r][key][j] for r in range(NCORES)], 0)))
    return outs


def l0_reshard_bb(parts):
    outs = []
    for j in range(NCORES):
        outs.append(np.ascontiguousarray(
            np.concatenate([parts[r][2 * j:2 * j + 2] for r in range(NCORES)], 0)))
    return outs


def l0_reshard_y(y_parts):
    outs = []
    for c in range(NCORES):
        outs.append(np.ascontiguousarray(np.stack(
            [y_parts[r][:, c * 2048:(c + 1) * 2048] for r in range(NCORES)], 0)))
    return outs


def inmaps_l0attn_a(P, resid_full):
    a = resid_full.reshape(2, 16384, D)
    parts = []
    for c in range(NCORES):
        seg = []
        for bb in range(2):
            lo = c * 2048 - 128
            halo = np.zeros((128, D), np.float32) if lo < 0 else a[bb, lo:c * 2048]
            seg.append(np.concatenate([halo, a[bb, c * 2048:(c + 1) * 2048]], 0))
        parts.append(np.ascontiguousarray(np.concatenate(seg, 0)))
    g = lambda n: P['L0_attn_' + n]
    base = {
        "wq": g('Wq'), "wk": g('Wk'), "wv": g('Wv'), "wsp": g('Wspike'),
        "wau": g('Wau'), "wad": g('Wad'), "wbeta": g('Wbeta'),
        "wu1": g('Wu1'), "wu2": g('Wu2'),
        "ginw": g('norm_in_w').reshape(8, 128),
        "convw": g('conv_w').reshape(8, 128, 1, 4)[:, :, 0, :],
        "convb": g('conv_b').reshape(8, 128),
        "bau": g('bau').reshape(64, 1), "bad": g('bad').reshape(4, 128),
        "bbeta": g('bbeta').reshape(8, 1),
        "bu1": g('bu1').reshape(64, 1), "bu2": g('bu2').reshape(8, 128),
        "identf": np.eye(128, dtype=np.float32),
    }
    return [dict(base, resid=parts[c]) for c in range(NCORES)]


def inmaps_l0attn_b(P, aparts):
    qres = {k: l0_reshard_qkv(aparts, k) for k in ("qs", "ks", "vs", "ds", "aas")}
    bbr = l0_reshard_bb([aparts[c]["bbs"] for c in range(NCORES)])
    asp = P['L0_attn_alpha_spike'][0]
    bsp = P['L0_attn_beta_spike'][0]
    masku = np.tile(np.triu(np.ones((64, 64), np.float32), 1), (2, 1))
    maskl = np.tile(np.tril(np.ones((64, 64), np.float32), -1), (2, 1))
    maskui = np.tile(np.triu(np.ones((64, 64), np.float32), 0), (2, 1))
    pat2 = np.zeros((128, 2), np.float32)
    pat2[0:64, 0] = 1; pat2[64:128, 1] = 1
    pat2t = np.zeros((2, 128), np.float32)
    pat2t[0, 0:64] = 1; pat2t[1, 64:128] = 1
    ims = []
    for j in range(NCORES):
        p0, p1 = 2 * j, 2 * j + 1
        aspc = np.concatenate([asp[p0 % 8], asp[p1 % 8]]).reshape(128, 1)
        bsppat = np.zeros((128, 2), np.float32)
        bsppat[0:64, 0] = bsp[p0 % 8]
        bsppat[64:128, 1] = bsp[p1 % 8]
        ims.append({
            "q_in": qres["qs"][j], "k_in": qres["ks"][j], "v_in": qres["vs"][j],
            "d_in": qres["ds"][j], "aa_in": qres["aas"][j], "bb_in": bbr[j],
            "asp_col": np.ascontiguousarray(aspc), "bsppat": bsppat,
            "pat2": pat2, "pat2t": pat2t,
            "masku": masku, "maskl": maskl, "maskui": maskui,
            "identf": np.eye(128, dtype=np.float32),
        })
    return ims


def inmaps_l0attn_c(P, resid_full, yparts, gates_parts):
    parts = shard_rows(resid_full)
    g = lambda n: P['L0_attn_' + n]
    hpatr = np.zeros((4, 128, 8), np.float32)
    hpatb = np.zeros((4, 8, 128), np.float32)
    for c in range(4):
        for sub in range(2):
            h = 2 * c + sub
            hpatr[c, sub * 64:(sub + 1) * 64, h] = 1
            hpatb[c, h, sub * 64:(sub + 1) * 64] = 1
    base = {
        "wo": g('Wo'),
        "hnwv": g('head_norm_w').reshape(4, 128),
        "ffw": g('ff_norm_w').reshape(8, 128),
        "hpatr": hpatr, "hpatb": hpatb,
        "wf1t": tiles4(g('Wff1'), 8, 32),
        "wf3t": tiles4(g('Wff3'), 8, 32),
        "wf2t": tiles4(g('Wff2'), 32, 8),
        "identf": np.eye(128, dtype=np.float32),
    }
    yt = l0_reshard_y(yparts)
    return [dict(base, resid=parts[c], ytile=yt[c], gates=gates_parts[c])
            for c in range(NCORES)]



import os as _os
FF = 4096
DBG = False
SKIP_PREP = False
SKIP_MAIN = False
PH_FLAG = 9
PH = 9
def build_l1ffn_k(b=None):
    if b is None:
        b = B()
    b.prefix = "l1ffn_k_"
    nc = b.nc
    resid = b.din("resid", [ROWS, D])
    w1t = b.din("w1t", [D, FF])
    w2t = b.din("w2t", [FF, D])
    b1 = b.din("b1", [32, 128])
    b2 = b.din("b2", [8, 128])
    identf = b.din("identf", [128, 128])
    out = b.dout("out", [ROWS, D])
    dbg_w1 = b.dout("dbg_w1", [128, FF], BF16)
    dbg_xn = b.dout("dbg_xn", [128, D], BF16)
    dbg_xfm = b.dout("dbg_xfm", [128, 512], BF16)
    dbg_hfm = b.dout("dbg_hfm", [128, 512], BF16)
    dbg_ofm = b.dout("dbg_ofm", [128, 512])
    dbg_rs = b.dout("dbg_rs", [128, 1])
    dbg_asum = b.dout("dbg_asum", [128, 40])

    with b.seg():
        block = b.block_open()
        ld = b.ch("ld"); wp = b.ch("wp"); mm1 = b.ch("mm1"); ev1 = b.ch("ev1")
        ldw = b.ch("ldw"); ldx = b.ch("ldx"); dbg = b.ch("dbg")
        mm2 = b.ch("mm2"); ev2 = b.ch("ev2"); tr = b.ch("tr"); ad = b.ch("ad")
        sto = b.ch("sto"); ln = b.ch("ln"); fmc = b.ch("fmc"); aux = b.ch("aux")

        w1b = b.sb([128, KD * FF], BF16)
        w2b = b.sb([128, KF * D], BF16)
        b1s = b.sb([128, 32])
        b2s = b.sb([128, 8])
        identb = b.sb([128, 128], BF16)
        idf = b.sb([128, 128])
        ones128 = b.sb([128, 1])
        ones_row = b.sb([1, 128])
        rs = b.sb([128, 1])
        rsn = b.sb([128, 1])
        epsb = b.sb([128, 1])

        @block.sync
        def _(sync):
            ld.inc(sync.dma_start(out=idf[:], in_=identf), 16)

        @block.vector
        def _(vector):
            ld.wt(vector)
            aux.inc(vector.tensor_copy(identb[:], idf[:]))
            aux.inc(vector.memset(ones128[:], 1.0))
            aux.inc(vector.memset(ones_row[:], 1.0))
            aux.inc(vector.memset(epsb[:], 1e-5))

        @block.scalar
        def _(scalar):
            aux.wt(scalar)
            ln.inc(scalar.nop())

        @block.vector
        def _(vector):
            aux.wt(vector)
            ln.inc(vector.engine_nop())

        if SKIP_PREP:
            @block.vector
            def _(vector):
                wp.inc(vector.memset(w1b[:], 1.0))
                wp.inc(vector.memset(w2b[:], 1.0))
                wp.inc(vector.memset(b1s[:], 0.0))
                wp.inc(vector.memset(b2s[:], 0.0))
        elif True:
          with ExitStack() as prep:
            wld = prep.enter_context(nc.sbuf_tensor("wld", [128, FF], F32))
            wldi = prep.enter_context(nc.sbuf_tensor("wldi", [128, FF], I32))
            asum = prep.enter_context(nc.sbuf_tensor("asum", [128, 40], F32))
            scl = prep.enter_context(nc.sbuf_tensor("scl", [1, 4], F32))
            ps_s = prep.enter_context(nc.psum_tensor("ps_s", [128, 8], F32))
            ps_bias = prep.enter_context(nc.psum_tensor("ps_bias", [128, 40], F32))

            for (wt_ap, nchunk, ncols, wres, rs_dst, nelem) in (
                    (w1t, KD, FF, w1b, rs, D * FF),
                    (w2t, KF, D, w2b, rs, FF * D)):
                for c in range(nchunk):
                    @block.sync
                    def _(sync, c=c):
                        wp.wt(sync)
                        ldw.wt(sync)
                        ldw.inc(sync.dma_start(out=wld[:, 0:ncols],
                                              in_=wt_ap[c * 128:(c + 1) * 128, :]), 16)

                    @block.scalar
                    def _(scalar, c=c):
                        ldw.wt(scalar)
                        wp.inc(scalar.activation(wldi[:, 0:ncols].bitcast(F32), wld[:, 0:ncols],
                                                 AF.Abs, accum_out=asum[:, c:c + 1]))
                @block.vector
                def _(vector):
                    wp.wt(vector)
                    wp.inc(vector.reduce_sum(asum[:, 38:39], asum[:, 0:nchunk], axis=AX.X))

                @block.tensor
                def _(tensor):
                    wp.wt(tensor)
                    mm1.inc(tensor.matmul(ps_s[0:1, 0:1], asum[:, 38:39], ones128[:],
                                          start=True, stop=True))

                @block.vector
                def _(vector):
                    mm1.wt(vector)
                    wp.inc(vector.tensor_scalar(scl[0:1, 0:1], ps_s[0:1, 0:1], 1.0 / nelem, 1e-8,
                                         ALU.mult, ALU.max))
                    wp.wt(vector)
                    wp.inc(vector.tensor_scalar(scl[0:1, 1:2], scl[0:1, 0:1], 0.5, None, ALU.mult))

                @block.tensor
                def _(tensor):
                    wp.wt(tensor)
                    mm1.inc(tensor.matmul(ps_s[:, 1:2], ones_row[:],
                                          scl[0:1, 1:2], start=True, stop=True))

                @block.vector
                def _(vector):
                    mm1.wt(vector)
                    wp.inc(vector.tensor_copy(rs_dst[:], ps_s[:, 1:2]))
                    wp.wt(vector)
                    wp.inc(vector.tensor_scalar(rsn[:], rs_dst[:], -1.0, None, ALU.mult))

                for c in range(nchunk):
                    @block.sync
                    def _(sync, c=c):
                        wp.wt(sync)
                        ldw.wt(sync)
                        ldw.inc(sync.dma_start(out=wld[:, 0:ncols],
                                              in_=wt_ap[c * 128:(c + 1) * 128, :]), 16)

                    @block.vector
                    def _(vector, c=c):
                        ldw.wt(vector)
                        wp.wt(vector)
                        wp.inc(vector.tensor_scalar(wldi[:, 0:ncols].bitcast(F32), wld[:, 0:ncols],
                                                    rsn[:, 0:1], None, ALU.is_lt))
                        wp.wt(vector)
                        wp.inc(vector.scalar_tensor_tensor(wres[:, c * ncols:(c + 1) * ncols],
                                                           wld[:, 0:ncols], rs_dst[:, 0:1],
                                                           wldi[:, 0:ncols].bitcast(F32),
                                                           ALU.is_gt, ALU.subtract))

            @block.sync
            def _(sync):
                wp.wt(sync)
                ldw.wt(sync)
                ldw.inc(sync.dma_start(out=wld[0:32, 0:128], in_=b1), 16)
                ldw.inc(sync.dma_start(out=wld[0:8, 128:256], in_=b2), 16)

            @block.tensor
            def _(tensor):
                ldw.wt(tensor)
                mm1.inc(tensor.transpose(ps_bias[:, 0:32], wld[0:32, 0:128], idf[0:32, 0:32]))
                mm1.inc(tensor.transpose(ps_bias[:, 32:40], wld[0:8, 128:256], idf[0:8, 0:8]))

            @block.vector
            def _(vector):
                mm1.wt(vector)
                wp.inc(vector.tensor_copy(b1s[:], ps_bias[:, 0:32]))
                wp.inc(vector.tensor_copy(b2s[:], ps_bias[:, 32:40]))

        xr4 = b.sb([128, 4 * D])
        sq = b.sb([128, D])
        stat = b.sb([128, 16])
        xnb = b.sb([128, D], BF16)
        xfm = b.sb([128, KD * TT], BF16)
        hfm = b.sb([128, KF * TT], BF16)
        ofm = b.sb([128, 8 * TT])
        ps_t = (b.psum([128, 128], BF16), b.psum([128, 128], BF16))
        ps_tf = b.psum([128, 128])
        ps_tf2 = b.psum([128, 128])
        ps_a = b.psum([128, TT])
        ps_b = b.psum([128, TT])
        ps_c = b.psum([128, TT])
        ps_d = b.psum([128, TT])

        for t in range(NT if not SKIP_MAIN else 0):
            rows = resid[t * TT:(t + 1) * TT, :].rearrange("(s p) d -> p s d", p=128)
            orow = out[t * TT:(t + 1) * TT, :].rearrange("(s p) d -> p s d", p=128)

            @block.sync
            def _(sync, t=t, rows=rows):
                if t == 0:
                    mm1.wt(sync)
                    wp.wt(sync)
                else:
                    sto.wt(sync)
                ldx.inc(sync.dma_start(out=xr4.rearrange("p (s d) -> p s d", s=4), in_=rows), 16)

            for s in range(4):
                if s == 0 and t >= 1:
                    @block.vector
                    def _(vector, t=t):
                        mm1.wt(vector)
                        if t == 1 and DBG:
                            dbg.wt(vector)
                        ln.inc(vector.engine_nop())
                emit_ln_subtile(b, block, xr4[:, s * D:(s + 1) * D], xnb[:], sq, stat, ldx, ln, eps_ap=epsb[:])
                if PH < 2:
                    continue
                emit_tm_to_fm(b, block, xnb[:],
                              lambda c, s=s: xfm[:, c * TT + s * 128: c * TT + (s + 1) * 128],
                              identb[:], ps_t, tr, fmc, ready_ch=ln)
                @block.vector
                def _(vector):
                    tr.wt(vector)
                    ln.inc(vector.engine_nop())

            if t == 0 and DBG:
                @block.gpsimd
                def _(g):
                    fmc.wt(g)
                    wp.wt(g)
                    dbg.inc(g.dma_start(out=dbg_w1, in_=w1b[:, 0:FF]), 16)
                    dbg.inc(g.dma_start(out=dbg_rs, in_=rs[:]), 16)
                    dbg.inc(g.dma_start(out=dbg_xn, in_=xnb[:]), 16)
                    dbg.inc(g.dma_start(out=dbg_xfm, in_=xfm[:, 0:512]), 16)

            for f in range(KF if PH >= 3 else 0):
                pst = ps_a if f % 2 == 0 else ps_b

                @block.tensor
                def _(tensor, f=f, pst=pst, t=t):
                    if f < 2:
                        fmc.wt(tensor)
                        wp.wt(tensor)
                        if t >= 1:
                            ev1.wt(tensor)
                    else:
                        ev1.wt(tensor, ev1.n - 1)
                    for c in range(KD):
                        ins = tensor.matmul(pst[:], w1b[:, c * FF + f * 128: c * FF + (f + 1) * 128],
                                            xfm[:, c * TT:(c + 1) * TT],
                                            start=(c == 0), stop=(c == KD - 1))
                    mm1.inc(ins)

                @block.scalar
                def _(scalar, f=f, pst=pst, t=t):
                    if f == 0 and t >= 1:
                        mm2.wt(scalar)
                    mm1.wt(scalar)
                    ev1.inc(scalar.activation(hfm[:, f * TT:(f + 1) * TT], pst[:],
                                              AF.Gelu, bias=b1s[:, f:f + 1]))

            if t == 0 and DBG:
                @block.gpsimd
                def _(g):
                    ev1.wt(g)
                    dbg.inc(g.dma_start(out=dbg_hfm, in_=hfm[:, 0:512]), 16)

            for o in range(KD if PH >= 4 else 0):
                pst = ps_c if o % 2 == 0 else ps_d

                @block.tensor
                def _(tensor, o=o, pst=pst, t=t):
                    if o < 2:
                        ev1.wt(tensor)
                        if t >= 1:
                            ev2.wt(tensor)
                    else:
                        ev2.wt(tensor, ev2.n - 1)
                    for f in range(KF):
                        ins = tensor.matmul(pst[:], w2b[:, f * D + o * 128: f * D + (o + 1) * 128],
                                            hfm[:, f * TT:(f + 1) * TT],
                                            start=(f == 0), stop=(f == KF - 1))
                    mm2.inc(ins)

                @block.vector
                def _(vector, o=o, pst=pst, t=t):
                    if o == 0 and t >= 1:
                        tr.wt(vector)
                    mm2.wt(vector)
                    ev2.inc(vector.tensor_scalar(ofm[:, o * TT:(o + 1) * TT], pst[:],
                                                 b2s[:, o:o + 1], None, ALU.add))

            if t == 0 and DBG:
                @block.gpsimd
                def _(g):
                    ev2.wt(g)
                    dbg.inc(g.dma_start(out=dbg_ofm, in_=ofm[:, 0:512]), 16)

            for o in range(KD if PH >= 5 else 0):
                for s in range(4):
                    i = o * 4 + s
                    pstf = ps_tf if i % 2 == 0 else ps_tf2

                    @block.tensor
                    def _(tensor, o=o, s=s, i=i, pstf=pstf, t=t):
                        if i == 0:
                            ev2.wt(tensor)
                            if t >= 1:
                                ad.wt(tensor)
                        elif i >= 2:
                            ad.wt(tensor, ad.n - 1)
                        tr.inc(tensor.transpose(pstf[:],
                                               ofm[:, o * TT + s * 128: o * TT + (s + 1) * 128],
                                               idf[:]))

                    @block.vector
                    def _(vector, o=o, s=s, pstf=pstf):
                        tr.wt(vector)
                        dst = xr4[:, s * D + o * 128: s * D + (o + 1) * 128]
                        ad.inc(vector.tensor_tensor(dst, dst, pstf[:], ALU.add))

            @block.sync
            def _(sync, t=t, orow=orow):
                ad.wt(sync)
                sto.inc(sync.dma_start(out=orow, in_=xr4.rearrange("p (s d) -> p s d", s=4)), 16)

        if SKIP_MAIN:
            @block.sync
            def _(sync):
                wp.wt(sync)
                sto.inc(sync.dma_start(out=out[0:128, :], in_=w1b[:, 0:2048].bitcast(F32)), 16)
        @block.sync
        def _(sync):
            sto.wt(sync)

        if DBG:
            @block.gpsimd
            def _(g):
                dbg.wt(g)
    return b




def inmaps_l1ffn_k(P, resid_full):
    parts = shard_rows(resid_full)
    base = {
        "w1t": np.ascontiguousarray(P["L1_ffn_W1"].T),
        "w2t": np.ascontiguousarray(P["L1_ffn_W2"].T),
        "b1": P["L1_ffn_b1"].reshape(32, 128),
        "b2": P["L1_ffn_b2"].reshape(8, 128),
        "identf": np.eye(128, dtype=np.float32),
    }
    return [dict(base, resid=parts[c]) for c in range(NCORES)]


_BUILD_CACHE = {}


def _get(name, fn):
    if name not in _BUILD_CACHE:
        _BUILD_CACHE[name] = fn()
    return _BUILD_CACHE[name]


def _run(b, im):
    res = run_bass_kernel_spmd(b.nc, im, core_ids=list(range(NCORES)))
    return res.results


def kernel(**inputs):
    P = inputs
    resid = np.ascontiguousarray(P["L0_attn_x"].reshape(32768, D).astype(np.float32))

    outsA = ["qs", "ks", "vs", "ds", "aas", "bbs", "gates"]
    r = _run(_get("l0a", build_l0attn_a), inmaps_l0attn_a(P, resid))
    aparts = [{k: r[c][k] for k in outsA} for c in range(NCORES)]
    r = _run(_get("l0b", build_l0attn_b), inmaps_l0attn_b(P, aparts))
    yparts = [r[j]["ys"] for j in range(NCORES)]
    gates_parts = [aparts[c]["gates"] for c in range(NCORES)]
    r = _run(_get("l0c", build_l0attn_c), inmaps_l0attn_c(P, resid, yparts, gates_parts))
    resid = np.concatenate([r[c]["out"] for c in range(NCORES)], 0)
    del aparts, yparts, gates_parts

    def from_parts(rr):
        return np.concatenate([rr[c]["out"] for c in range(NCORES)], 0)

    def sharded(xcat):
        return [xcat[c * ROWS:(c + 1) * ROWS] for c in range(NCORES)]

    def im_with_resid(base_ims, xcat):
        sh = sharded(xcat)
        for c in range(NCORES):
            base_ims[c]["resid"] = np.ascontiguousarray(sh[c])
        return base_ims

    ims = inmaps_l0ffn(P, np.zeros((32768, D), np.float32))
    r = _run(_get("l0f", build_l0ffn), im_with_resid(ims, resid))
    resid = from_parts(r)

    ims = inmaps_l1attn(P, np.zeros((32768, D), np.float32))
    r = _run(_get("l1a", build_l1attn), im_with_resid(ims, resid))
    resid = from_parts(r)

    ims = inmaps_l1ffn_k(P, np.zeros((32768, D), np.float32))
    r = _run(_get("l1f", build_l1ffn_k), im_with_resid(ims, resid))
    resid = from_parts(r)

    imA = inmaps_l2attn_a_fromcat(P, resid)
    r = _run(_get("l2a", build_l2attn_a), imA)
    uparts = [r[c]["uscr"] for c in range(NCORES)]
    zparts = [r[c]["zscr"] for c in range(NCORES)]
    bcd = assemble_l2_bcd([r[c]["bcd_out"] for c in range(NCORES)])
    uin = reshard_l2_u(uparts)
    logA = P["L2_attn_log_A"]; dparam = P["L2_attn_D_param"]
    imB = [{"u_in": uin[j], "bcd": bcd, "logA": np.ascontiguousarray(logA[j * 128:(j + 1) * 128]),
            "dparam": np.ascontiguousarray(dparam[j * 128:(j + 1) * 128].reshape(128, 1))}
           for j in range(NCORES)]
    r = _run(_get("l2b", build_l2attn_b), imB)
    yt = reshard_l2_y([r[j]["y_out"] for j in range(NCORES)])
    sh = sharded(resid)
    imC = [{"resid": np.ascontiguousarray(sh[c]), "ytile": yt[c], "zscr": zparts[c],
            "woutt": np.ascontiguousarray(P["L2_attn_W_out"]),
            "identf": np.eye(128, dtype=np.float32)} for c in range(NCORES)]
    r = _run(_get("l2c", build_l2attn_c), imC)
    resid = from_parts(r)

    ims = inmaps_l2ffn(P, np.zeros((32768, D), np.float32))
    r = _run(_get("l2f", build_l2ffn), im_with_resid(ims, resid))
    resid = from_parts(r)

    sh = sharded(resid)
    imA = inmaps_l3attn_a(P, np.zeros((32768, D), np.float32))
    for c in range(NCORES):
        imA[c]["resid"] = np.ascontiguousarray(sh[c])
    r = _run(_get("l3a", build_l3attn_a), imA)
    av_all = np.ascontiguousarray(np.stack([r[c]["av_out"] for c in range(NCORES)], 0))
    attnT_parts = [r[c]["attnT"] for c in range(NCORES)]
    imB = inmaps_l3attn_b(P, np.zeros((32768, D), np.float32), av_all, attnT_parts)
    for c in range(NCORES):
        imB[c]["resid"] = np.ascontiguousarray(sh[c])
    r = _run(_get("l3b", build_l3attn_b), imB)
    resid = from_parts(r)

    ims = inmaps_l3ffn(P, np.zeros((32768, D), np.float32))
    r = _run(_get("l3f", build_l3ffn), im_with_resid(ims, resid))
    resid = from_parts(r)

    b0 = np.concatenate([resid[c * ROWS:(c * ROWS) + 2048] for c in range(NCORES)], 0)
    b1 = np.concatenate([resid[c * ROWS + 2048:(c + 1) * ROWS] for c in range(NCORES)], 0)
    return np.ascontiguousarray(np.stack([b0, b1], 0))


def inmaps_l2attn_a_fromcat(P, resid_cat):
    parts = [resid_cat[c * ROWS:(c + 1) * ROWS] for c in range(NCORES)]
    tails = l2a_tails_from_parts(parts)
    base = {
        "wintile": P["L2_attn_W_in"],
        "wxp": P["L2_attn_W_xproj"],
        "convw": np.ascontiguousarray(P["L2_attn_conv_w"].reshape(8, 128, 1, 4)[:, :, 0, :]),
        "convb": P["L2_attn_conv_b"].reshape(8, 128),
        "identf": np.eye(128, dtype=np.float32),
        "tails": tails,
    }
    ims = []
    for c in range(NCORES):
        sm = np.zeros((128, 8), np.float32)
        if c > 0:
            sm[:, c - 1] = 1.0
        ims.append(dict(base, resid=np.ascontiguousarray(parts[c]), selmask=sm))
    return ims


# version 11 (final)
# speedup vs baseline: 1.0070x; 1.0070x over previous
from contextlib import ExitStack
import numpy as np
import concourse.bass as bass
import concourse.mybir as mybir
from concourse.bass_utils import run_bass_kernel_spmd

F32 = mybir.dt.float32
BF16 = mybir.dt.bfloat16
I32 = mybir.dt.int32
U8 = mybir.dt.uint8
AF = mybir.ActivationFunctionType
ALU = mybir.AluOpType
AX = mybir.AxisListType

NCORES = 8
ROWS = 4096
D = 1024
LN_EPS = 1e-5


class Chan:
    def __init__(self, es, nc, name):
        self.sem = es.enter_context(nc.semaphore(name))
        self.n = 0

    def inc(self, instr, amt=1):
        instr.then_inc(self.sem, amt)
        self.n += amt
        return instr

    def wt(self, eng, n=None):
        eng.wait_ge(self.sem, self.n if n is None else n)


class B:
    def __init__(self):
        self.nc = bass.Bass("TRN2", target_bir_lowering=False)
        self.es = ExitStack()
        self._ch = {}
        self._uid = 0
        self.fused = False
        self.prefix = ""
        self.block = None
        self._scopes = []
        self._dreg = {}
        self._input_names = set()

    def _nm(self, pfx):
        self._uid += 1
        return f"{self.prefix}{pfx}{self._uid}"

    def ch(self, name):
        name = self.prefix + name
        if name not in self._ch:
            self._ch[name] = Chan(self.es, self.nc, name)
        return self._ch[name]

    def _alloc_ctx(self):
        return self._scopes[-1] if self._scopes else self.es

    def sb(self, shape, dt=F32, name=None):
        return self._alloc_ctx().enter_context(
            self.nc.sbuf_tensor(name or self._nm("sb"), shape, dt))

    def psum(self, shape, dt=F32, name=None):
        return self._alloc_ctx().enter_context(
            self.nc.psum_tensor(name or self._nm("ps"), shape, dt))

    def din(self, name, shape, dt=F32):
        if self.fused:
            key = self.prefix + name
            if key in self._dreg:
                return self._dreg[key]
            self._input_names.add(key)
            ap = self.nc.dram_tensor(key, shape, dt, kind="ExternalInput").ap()
            self._dreg[key] = ap
            return ap
        return self.nc.dram_tensor(name, shape, dt, kind="ExternalInput").ap()

    def reg(self, key, shape, dt=F32, kind="Internal"):
        ap = self.nc.dram_tensor(key, shape, dt, kind=kind).ap()
        self._dreg[key] = ap
        return ap

    def dout(self, name, shape, dt=F32):
        if self.fused:
            key = self.prefix + name
            if key in self._dreg:
                return self._dreg[key]
            ap = self.nc.dram_tensor(key, shape, dt, kind="Internal").ap()
            self._dreg[key] = ap
            return ap
        return self.nc.dram_tensor(name, shape, dt, kind="ExternalOutput").ap()

    def dtmp(self, name, shape, dt=F32):
        if self.fused:
            name = self.prefix + name
        return self.nc.dram_tensor(name, shape, dt, kind="Internal").ap()

    def seg(self):
        if not self.fused:
            return self.es
        scope = ExitStack()
        self._scopes.append(scope)
        outer = self

        class _Seg:
            def __enter__(self):
                return scope.__enter__()

            def __exit__(self, *a):
                outer._scopes.pop()
                return scope.__exit__(*a)
        return _Seg()

    def block_open(self):
        if self.fused:
            return self.block
        self.block = self.es.enter_context(self.nc.Block())
        return self.block


def emit_ln_subtile(b, block, xr_s, xn_out, sq, stat, ld_ch, ln_ch, eps_ap=None,
                    gamma=None):
    Dd = xr_s.shape[-1]

    @block.scalar
    def _(scalar):
        ld_ch.wt(scalar)
        ln_ch.wt(scalar)
        ln_ch.inc(scalar.activation(sq[:, 0:Dd], xr_s, AF.Square,
                                    accum_out=stat[:, 0:1]))

    @block.vector
    def _(vector):
        ld_ch.wt(vector)
        ln_ch.wt(vector)
        ln_ch.inc(vector.reduce_sum(stat[:, 1:2], xr_s, axis=AX.X))
        ln_ch.wt(vector)
        ln_ch.inc(vector.tensor_scalar(stat[:, 2:3], stat[:, 1:2], 1.0 / Dd, None, ALU.mult))
        ln_ch.wt(vector)
        ln_ch.inc(vector.tensor_scalar(stat[:, 3:4], stat[:, 0:1], 1.0 / Dd, None, ALU.mult))
        ln_ch.wt(vector)
        ln_ch.inc(vector.tensor_tensor(stat[:, 4:5], stat[:, 2:3], stat[:, 2:3], ALU.mult))
        ln_ch.wt(vector)
        ln_ch.inc(vector.tensor_tensor(stat[:, 4:5], stat[:, 3:4], stat[:, 4:5], ALU.subtract))

    @block.scalar
    def _(scalar):
        ln_ch.wt(scalar)
        ln_ch.inc(scalar.activation(stat[:, 5:6], stat[:, 4:5], AF.Ln, bias=eps_ap))
        ln_ch.wt(scalar)
        ln_ch.inc(scalar.activation(stat[:, 5:6], stat[:, 5:6], AF.Exp, scale=-0.5))

    @block.vector
    def _(vector):
        ln_ch.wt(vector)
        ln_ch.inc(vector.tensor_scalar(xn_out, xr_s, stat[:, 2:3], stat[:, 5:6],
                                       ALU.subtract, ALU.mult))


def emit_tm_to_fm(b, block, xn_tm_bf, fm_dst_cols, identb, ps_t, mm_ch, ev_ch,
                  kd=8, evac_engine="vector", ready_ch=None):
    slots = [ps_t[0][:, 0:128], ps_t[1][:, 0:128]]
    for c in range(kd):
        slot = slots[c % 2]

        @block.tensor
        def _(tensor, c=c, slot=slot):
            if c == 0:
                if ready_ch is not None:
                    ready_ch.wt(tensor)
                if ev_ch.n > 0:
                    ev_ch.wt(tensor)
            elif c >= 2:
                ev_ch.wt(tensor, ev_ch.n - 1)
            mm_ch.inc(tensor.transpose(slot, xn_tm_bf[:, c * 128:(c + 1) * 128], identb))

        eng = getattr(block, evac_engine)

        @eng
        def _(e, c=c, slot=slot):
            mm_ch.wt(e)
            ev_ch.inc(e.tensor_copy(fm_dst_cols(c), slot))


def simrun(b, in_map, out_names, fp64=False):
    from concourse.bass_interp import MultiCoreSim
    sim = MultiCoreSim(b.nc, num_cores=1, num_workers=1, trace=False)
    for k, v in in_map.items():
        sim.cores[0].tensor(k)[:] = v
    sim.simulate(check_with_hw=False)
    return {k: np.asarray(sim.cores[0].tensor(k)) for k in out_names}


"""Per-sub-block launch builders. Each build_X() returns (b, meta) where the
program computes resid_out = resid + block(LN(resid)) for its shard.

Row sharding (all launches): core c owns token range t in [c*2048,(c+1)*2048)
of BOTH batches: rows = concat(b0[tr], b1[tr]) -> [4096, 1024].
"""
import numpy as np

TT = 512
NT = ROWS // TT
KD = D // 128
KF = 4096 // 128


class Skel:

    def __init__(self, b, block, resid, out):
        self.b = b
        self.block = block
        self.resid = resid
        self.out = out
        self.ldx = b.ch("ldx"); self.ln = b.ch("ln"); self.tr = b.ch("tr")
        self.fmc = b.ch("fmc"); self.ad = b.ch("ad"); self.sto = b.ch("sto")
        self.aux = b.ch("aux"); self.ld0 = b.ch("ld0")
        self.idf = b.sb([128, 128])
        self.identb = b.sb([128, 128], BF16)
        self.epsb = b.sb([128, 1])
        self.ones128 = b.sb([128, 1])
        self.ones_row = b.sb([1, 128])
        self.xr4 = b.sb([128, 4 * D])
        self.sq = b.sb([128, D])
        self.stat = b.sb([128, 16])
        self.xnb = b.sb([128, D], BF16)
        self.vsc = b.sb([32, 128])
        self.xfm = b.sb([128, KD * TT], BF16)
        self.ps_t = (b.psum([128, 128], BF16), b.psum([128, 128], BF16))
        self.ps_tf = (b.psum([128, 128]), b.psum([128, 128]))

    def consts(self, identf_ap):
        b, block = self.b, self.block
        ld0, aux, ln = self.ld0, self.aux, self.ln

        @block.sync
        def _(sync):
            ld0.inc(sync.dma_start(out=self.idf[:], in_=identf_ap), 16)

        @block.vector
        def _(vector):
            ld0.wt(vector)
            aux.inc(vector.tensor_copy(self.identb[:], self.idf[:]))
            aux.inc(vector.memset(self.epsb[:], LN_EPS))
            aux.inc(vector.memset(self.ones128[:], 1.0))
            aux.inc(vector.memset(self.ones_row[:], 1.0))

        @block.scalar
        def _(scalar):
            aux.wt(scalar)
            ln.inc(scalar.nop())

        @block.vector
        def _(vector):
            aux.wt(vector)
            ln.inc(vector.engine_nop())

    def load_rows(self, t, war_chs=()):
        b, block = self.b, self.block
        skel = self

        @block.sync
        def _(sync, t=t):
            for ch in war_chs:
                ch.wt(sync)
            if t >= 1:
                skel.sto.wt(sync)
            rows = skel.resid[t * TT:(t + 1) * TT, :].rearrange("(s p) d -> p s d", p=128)
            skel.ldx.inc(sync.dma_start(
                out=skel.xr4.rearrange("p (s d) -> p s d", s=4), in_=rows), 16)

    def ln_fmize(self, t, mm_war_ch=None):
        b, block, skel = self.b, self.block, self
        for s in range(4):
            if s == 0 and t >= 1 and mm_war_ch is not None:
                @block.vector
                def _(vector):
                    mm_war_ch.wt(vector)
                    skel.ln.inc(vector.engine_nop())
            emit_ln_subtile(b, block, skel.xr4[:, s * D:(s + 1) * D], skel.xnb[:],
                            skel.sq, skel.stat, skel.ldx, skel.ln, eps_ap=skel.epsb[:])
            emit_tm_to_fm(b, block, skel.xnb[:],
                          lambda c, s=s: skel.xfm[:, c * TT + s * 128: c * TT + (s + 1) * 128],
                          self.identb[:], self.ps_t, skel.tr, skel.fmc, ready_ch=skel.ln)

    def epilogue(self, t, ofm, ready_ch):
        b, block, skel = self.b, self.block, self
        for o in range(KD):
            for s in range(4):
                i = o * 4 + s
                pstf = self.ps_tf[i % 2]

                @block.tensor
                def _(tensor, o=o, s=s, i=i, pstf=pstf, t=t):
                    if i == 0:
                        ready_ch.wt(tensor)
                        if t >= 1:
                            skel.ad.wt(tensor)
                    elif i >= 2:
                        skel.ad.wt(tensor, skel.ad.n - 1)
                    skel.tr.inc(tensor.transpose(
                        pstf[:], ofm[:, o * TT + s * 128: o * TT + (s + 1) * 128], skel.idf[:]))

                @block.vector
                def _(vector, o=o, s=s, i=i, pstf=pstf):
                    if i == 0:
                        skel.ldx.wt(vector)
                    skel.tr.wt(vector)
                    dst = skel.xr4[:, s * D + o * 128: s * D + (o + 1) * 128]
                    skel.ad.inc(vector.tensor_tensor(dst, dst, pstf[:], ALU.add))

        @block.sync
        def _(sync, t=t):
            skel.ad.wt(sync)
            orow = skel.out[t * TT:(t + 1) * TT, :].rearrange("(s p) d -> p s d", p=128)
            skel.sto.inc(sync.dma_start(
                out=orow, in_=skel.xr4.rearrange("p (s d) -> p s d", s=4)), 16)

    def finish(self):
        block, skel = self.block, self

        @block.sync
        def _(sync):
            skel.sto.wt(sync)


def emit_vec_load(b, block, dst_pc, src_vec_ap, n, skel, ps_small, ch):
    nch = n // 128

    @block.sync
    def _(sync):
        ch.wt(sync)
        skel.tr.wt(sync)
        ch.inc(sync.dma_start(out=skel.vsc[0:nch, 0:128], in_=src_vec_ap), 16)

    @block.tensor
    def _(tensor):
        ch.wt(tensor)
        skel.tr.inc(tensor.transpose(ps_small[:, 0:nch], skel.vsc[0:nch, 0:128],
                                     skel.idf[0:nch, 0:nch]))

    @block.vector
    def _(vector):
        skel.tr.wt(vector)
        ch.inc(vector.tensor_copy(dst_pc[:, 0:nch], ps_small[:, 0:nch]))


def emit_stream_matmul(b, block, skel, *, src_fm, src_tt, kc, fc, wtiles, wbufs,
                       psums, chs, evac, t, first_tile, wait_src_ch, wready_ch=None):
    wld = [chs["wldA"], chs["wldB"]]
    for f in range(fc):
        wb = wbufs[f % 2]
        pst = psums[f % 2]
        lch = wld[f % 2]

        @block.sync
        def _(sync, f=f, wb=wb, lch=lch):
            if not (first_tile and f < 2):
                chs["mm"].wt(sync, max(0, chs["mm"].n - 1))
            if wready_ch is not None and first_tile and f < 2:
                wready_ch.wt(sync)
            lch.inc(sync.dma_start(
                out=wb[:, 0:kc * 128].rearrange("p (c q) -> p c q", q=128),
                in_=wtiles[:, f].rearrange("a p q -> p a q")), 16)

        @block.tensor
        def _(tensor, f=f, wb=wb, pst=pst, lch=lch, t=t):
            lch.wt(tensor)
            if f == 0:
                wait_src_ch.wt(tensor)
                if not first_tile:
                    chs["ev"].wt(tensor)
            elif f >= 2:
                chs["ev"].wt(tensor, chs["ev"].n - 1)
            for c in range(kc):
                ins = tensor.matmul(pst[:], wb[:, c * 128:(c + 1) * 128],
                                    src_fm[:, c * src_tt:(c + 1) * src_tt],
                                    start=(c == 0), stop=(c == kc - 1))
            chs["mm"].inc(ins)

        evac(f, pst)


def shard_rows(x2d):
    a = x2d.reshape(2, 16384, -1)
    return [np.ascontiguousarray(
        np.concatenate([a[0, c * 2048:(c + 1) * 2048], a[1, c * 2048:(c + 1) * 2048]], 0))
        for c in range(NCORES)]


def unshard_rows(parts):
    b0 = np.concatenate([p[:2048] for p in parts], 0)
    b1 = np.concatenate([p[2048:] for p in parts], 0)
    return np.stack([b0, b1], 0)


def tiles4(w, kc, fc):
    return np.ascontiguousarray(
        w.reshape(kc, 128, fc, 128).transpose(0, 2, 1, 3))


def build_l2ffn(b=None):
    if b is None:
        b = B()
    b.prefix = "l2ffn_"
    nc = b.nc
    resid = b.din("resid", [ROWS, D])
    wt_in = {}
    for nm, kc, fc in (("w1", 8, 32), ("w2", 8, 32), ("h1", 32, 32),
                       ("h2", 32, 32), ("w3", 32, 8)):
        wt_in[nm] = b.din(nm + "t", [kc, fc, 128, 128])
    sv_in = {nm: b.din(nm + "s", [n // 128, 128]) for nm, n in
             (("w1", 4096), ("w2", 4096), ("h1", 4096), ("h2", 4096), ("w3", 1024))}
    identf = b.din("identf", [128, 128])
    out = b.dout("out", [ROWS, D])
    wbt = {nm: b.dtmp(nm + "b", [kc, fc, 128, 128], BF16) for nm, kc, fc in
           (("w1", 8, 32), ("w2", 8, 32), ("h1", 32, 32), ("h2", 32, 32), ("w3", 32, 8))}

    with b.seg():
        block = b.block_open()
        skel = Skel(b, block, resid, out)
        wp = b.ch("wp"); lds = b.ch("lds")
        chs = {k: b.ch(k) for k in ("wldA", "wldB", "mm", "ev")}
        skel.consts(identf)

        @block.gpsimd
        def _(g):
            for nm in ("w1", "w2", "h1", "h2", "w3"):
                wp.inc(g.dma_start(out=wbt[nm].rearrange("a b p q -> (a b p) q"),
                                   in_=wt_in[nm].rearrange("a b p q -> (a b p) q")), 16)

        svs = {nm: b.sb([128, 32]) for nm in ("w1", "w2", "h1", "h2")}
        svs["w3"] = b.sb([128, 8])
        ps_small = b.psum([128, 32])
        for nm in ("w1", "w2", "h1", "h2", "w3"):
            emit_vec_load(b, block, svs[nm], sv_in[nm], svs[nm].shape[1] * 128,
                          skel, ps_small, lds)

        hbuf1 = b.sb([128, 32 * TT], BF16)
        hbuf2 = b.sb([128, 32 * TT], BF16)
        ofm = b.sb([128, 8 * TT])
        wbufs = (b.sb([128, 32 * 128], BF16), b.sb([128, 32 * 128], BF16))
        psums = (b.psum([128, TT]), b.psum([128, TT]))

        for t in range(NT):
            skel.load_rows(t, war_chs=(skel.ad,) if t else (chs["mm"], wp))
            skel.ln_fmize(t, mm_war_ch=chs["mm"])

            def ev_s1a(f, pst, t=t):
                @block.scalar
                def _(scalar, f=f, pst=pst):
                    chs["mm"].wt(scalar)
                    if f == 0:
                        lds.wt(scalar)
                    chs["ev"].inc(scalar.activation(
                        hbuf1[:, f * TT:(f + 1) * TT], pst[:], AF.Silu,
                        scale=svs["w1"][:, f:f + 1]))
            emit_stream_matmul(b, block, skel, src_fm=skel.xfm, src_tt=TT, kc=8, fc=32,
                               wtiles=wbt["w1"], wbufs=wbufs, psums=psums, chs=chs,
                               evac=ev_s1a, t=t, first_tile=(t == 0), wait_src_ch=skel.fmc,
                               wready_ch=wp)

            def ev_s1b(f, pst, t=t):
                @block.vector
                def _(vector, f=f, pst=pst):
                    chs["mm"].wt(vector)
                    chs["ev"].inc(vector.scalar_tensor_tensor(
                        hbuf1[:, f * TT:(f + 1) * TT], pst[:], svs["w2"][:, f:f + 1],
                        hbuf1[:, f * TT:(f + 1) * TT], ALU.mult, ALU.mult))
            emit_stream_matmul(b, block, skel, src_fm=skel.xfm, src_tt=TT, kc=8, fc=32,
                               wtiles=wbt["w2"], wbufs=wbufs, psums=psums, chs=chs,
                               evac=ev_s1b, t=t, first_tile=False, wait_src_ch=chs["ev"])

            def ev_s2(f, pst, t=t):
                @block.scalar
                def _(scalar, f=f, pst=pst):
                    chs["mm"].wt(scalar)
                    chs["ev"].inc(scalar.activation(
                        hbuf2[:, f * TT:(f + 1) * TT], pst[:], AF.Gelu,
                        scale=svs["h1"][:, f:f + 1]))
            emit_stream_matmul(b, block, skel, src_fm=hbuf1, src_tt=TT, kc=32, fc=32,
                               wtiles=wbt["h1"], wbufs=wbufs, psums=psums, chs=chs,
                               evac=ev_s2, t=t, first_tile=False, wait_src_ch=chs["ev"])

            def ev_s3(f, pst, t=t):
                @block.scalar
                def _(scalar, f=f, pst=pst):
                    chs["mm"].wt(scalar)
                    chs["ev"].inc(scalar.activation(
                        hbuf1[:, f * TT:(f + 1) * TT], pst[:], AF.Gelu,
                        scale=svs["h2"][:, f:f + 1]))
            emit_stream_matmul(b, block, skel, src_fm=hbuf2, src_tt=TT, kc=32, fc=32,
                               wtiles=wbt["h2"], wbufs=wbufs, psums=psums, chs=chs,
                               evac=ev_s3, t=t, first_tile=False, wait_src_ch=chs["ev"])

            def ev_s4(f, pst, t=t):
                @block.vector
                def _(vector, f=f, pst=pst):
                    chs["mm"].wt(vector)
                    chs["ev"].inc(vector.tensor_scalar(
                        ofm[:, f * TT:(f + 1) * TT], pst[:], svs["w3"][:, f:f + 1],
                        None, ALU.mult))
            emit_stream_matmul(b, block, skel, src_fm=hbuf1, src_tt=TT, kc=32, fc=8,
                               wtiles=wbt["w3"], wbufs=wbufs, psums=psums, chs=chs,
                               evac=ev_s4, t=t, first_tile=False, wait_src_ch=chs["ev"])

            skel.epilogue(t, ofm, chs["ev"])
        skel.finish()
    return b


def inmaps_l2ffn(P, resid_full):
    parts = shard_rows(resid_full)
    base = {
        "w1t": tiles4(np.ascontiguousarray(P["L2_ffn_w1_w"].T), 8, 32),
        "w2t": tiles4(np.ascontiguousarray(P["L2_ffn_w2_w"].T), 8, 32),
        "h1t": tiles4(np.ascontiguousarray(P["L2_ffn_h1_w"].T), 32, 32),
        "h2t": tiles4(np.ascontiguousarray(P["L2_ffn_h2_w"].T), 32, 32),
        "w3t": tiles4(np.ascontiguousarray(P["L2_ffn_w3_w"].T), 32, 8),
        "w1s": P["L2_ffn_w1_s"].reshape(32, 128),
        "w2s": P["L2_ffn_w2_s"].reshape(32, 128),
        "h1s": P["L2_ffn_h1_s"].reshape(32, 128),
        "h2s": P["L2_ffn_h2_s"].reshape(32, 128),
        "w3s": P["L2_ffn_w3_s"].reshape(8, 128),
        "identf": np.eye(128, dtype=np.float32),
    }
    return [dict(base, resid=parts[c]) for c in range(NCORES)]


def build_l3ffn(b=None):
    if b is None:
        b = B()
    b.prefix = "l3ffn_"
    nc = b.nc
    resid = b.din("resid", [ROWS, D])
    k0top = b.din("k0top", [8, 32, 128, 128], U8)
    k0bot = b.din("k0bot", [8, 32, 128, 128], U8)
    k1t = b.din("k1t", [32, 32, 128, 128], U8)
    k2t = b.din("k2t", [32, 8, 128, 128], U8)
    s0 = b.din("s0", [1, 1]); s1 = b.din("s1", [1, 1]); s2 = b.din("s2", [1, 1])
    identf = b.din("identf", [128, 128])
    out = b.dout("out", [ROWS, D])
    w0b = b.dtmp("w0b", [8, 32, 128, 128], BF16)
    w1b = b.dtmp("w1b", [32, 32, 128, 128], BF16)
    w2b = b.dtmp("w2b", [32, 8, 128, 128], BF16)

    with b.seg():
        block = b.block_open()
        skel = Skel(b, block, resid, out)
        wp = b.ch("wp"); wl = b.ch("wl"); lds = b.ch("lds"); wst = b.ch("wst")
        chs = {k: b.ch(k) for k in ("wldA", "wldB", "mm", "ev")}
        skel.consts(identf)

        svs = {}
        ps_small = b.psum([128, 4])
        sctmp = b.sb([1, 4])
        for i, (nm, sap) in enumerate((("s0", s0), ("s1", s1), ("s2", s2))):
            svs[nm] = b.sb([128, 1])

            @block.sync
            def _(sync, sap=sap, i=i):
                lds.inc(sync.dma_start(out=sctmp[0:1, i:i + 1], in_=sap), 16)

        @block.vector
        def _(vector):
            lds.wt(vector)
            skel.aux.inc(vector.tensor_scalar(sctmp[0:1, 0:3], sctmp[0:1, 0:3],
                                              2.0, None, ALU.mult))

        @block.tensor
        def _(tensor):
            skel.aux.wt(tensor)
            for i in range(3):
                ins = tensor.matmul(ps_small[:, i:i + 1], skel.ones_row[:],
                                    sctmp[0:1, i:i + 1], start=True, stop=True)
            skel.tr.inc(ins)

        @block.vector
        def _(vector):
            skel.tr.wt(vector)
            for i, nm in enumerate(("s0", "s1", "s2")):
                skel.aux.inc(vector.tensor_copy(svs[nm][:], ps_small[:, i:i + 1]))

        u8a = b.sb([128, 32 * 128], U8)
        u8b = b.sb([128, 32 * 128], U8)
        wcv = b.sb([128, 32 * 128], BF16)
        for kc in range(8):
            @block.sync
            def _(sync, kc=kc):
                wp.wt(sync)
                wl.wt(sync)
                wl.inc(sync.dma_start(out=u8a.rearrange("p (c q) -> p c q", q=128),
                                      in_=k0top[kc]), 16)
                wl.inc(sync.dma_start(out=u8b.rearrange("p (c q) -> p c q", q=128),
                                      in_=k0bot[kc]), 16)

            @block.vector
            def _(vector, kc=kc):
                wl.wt(vector)
                wst.wt(vector)
                wp.inc(vector.tensor_tensor(wcv[:], u8a[:], u8b[:], ALU.subtract))

            @block.gpsimd
            def _(g, kc=kc):
                wp.wt(g)
                wst.inc(g.dma_start(out=w0b[kc].rearrange("c p q -> p c q"),
                                   in_=wcv.rearrange("p (c q) -> p c q", q=128)), 16)

        for (src, dst, nkc, nfc) in ((k1t, w1b, 32, 32), (k2t, w2b, 32, 8)):
            for kc in range(nkc):
                @block.sync
                def _(sync, src=src, kc=kc, nfc=nfc):
                    wp.wt(sync)
                    wl.wt(sync)
                    wl.inc(sync.dma_start(
                        out=u8a[:, 0:nfc * 128].rearrange("p (c q) -> p c q", q=128),
                        in_=src[kc]), 16)

                @block.vector
                def _(vector, kc=kc, nfc=nfc):
                    wl.wt(vector)
                    wst.wt(vector)
                    wp.inc(vector.tensor_scalar(wcv[:, 0:nfc * 128], u8a[:, 0:nfc * 128],
                                                2.0, -1.0, ALU.mult, ALU.add))

                @block.gpsimd
                def _(g, dst=dst, kc=kc, nfc=nfc):
                    wp.wt(g)
                    wst.inc(g.dma_start(
                        out=dst[kc].rearrange("c p q -> p c q"),
                        in_=wcv[:, 0:nfc * 128].rearrange("p (c q) -> p c q", q=128)), 16)

        hbuf1 = b.sb([128, 32 * TT], BF16)
        hbuf2 = b.sb([128, 32 * TT], BF16)
        ofm = b.sb([128, 8 * TT])
        wbufs = (b.sb([128, 32 * 128], BF16), b.sb([128, 32 * 128], BF16))
        psums = (b.psum([128, TT]), b.psum([128, TT]))

        for t in range(NT):
            skel.load_rows(t, war_chs=(skel.ad,) if t else (chs["mm"], wp))
            skel.ln_fmize(t, mm_war_ch=chs["mm"])

            def ev_s0(f, pst, t=t):
                @block.scalar
                def _(scalar, f=f, pst=pst):
                    chs["mm"].wt(scalar)
                    if f == 0:
                        skel.aux.wt(scalar)
                    chs["ev"].inc(scalar.activation(
                        hbuf1[:, f * TT:(f + 1) * TT], pst[:], AF.Relu,
                        scale=svs["s0"][:, 0:1]))
            emit_stream_matmul(b, block, skel, src_fm=skel.xfm, src_tt=TT, kc=8, fc=32,
                               wtiles=w0b, wbufs=wbufs, psums=psums, chs=chs,
                               evac=ev_s0, t=t, first_tile=(t == 0), wait_src_ch=skel.fmc,
                               wready_ch=wst)

            def ev_s1(f, pst, t=t):
                @block.scalar
                def _(scalar, f=f, pst=pst):
                    chs["mm"].wt(scalar)
                    chs["ev"].inc(scalar.activation(
                        hbuf2[:, f * TT:(f + 1) * TT], pst[:], AF.Relu,
                        scale=svs["s1"][:, 0:1]))
            emit_stream_matmul(b, block, skel, src_fm=hbuf1, src_tt=TT, kc=32, fc=32,
                               wtiles=w1b, wbufs=wbufs, psums=psums, chs=chs,
                               evac=ev_s1, t=t, first_tile=False, wait_src_ch=chs["ev"])

            def ev_s2(f, pst, t=t):
                @block.vector
                def _(vector, f=f, pst=pst):
                    chs["mm"].wt(vector)
                    chs["ev"].inc(vector.tensor_scalar(
                        ofm[:, f * TT:(f + 1) * TT], pst[:], svs["s2"][:, 0:1],
                        None, ALU.mult))
            emit_stream_matmul(b, block, skel, src_fm=hbuf2, src_tt=TT, kc=32, fc=8,
                               wtiles=w2b, wbufs=wbufs, psums=psums, chs=chs,
                               evac=ev_s2, t=t, first_tile=False, wait_src_ch=chs["ev"])

            skel.epilogue(t, ofm, chs["ev"])
        skel.finish()
    return b


def inmaps_l3ffn(P, resid_full):
    parts = shard_rows(resid_full)
    k0 = P["L3_ffn_k0"].view(np.uint8)
    base = {
        "k0top": tiles4(k0[0:1024], 8, 32),
        "k0bot": tiles4(k0[1024:2048], 8, 32),
        "k1t": tiles4(P["L3_ffn_k1"].view(np.uint8), 32, 32),
        "k2t": tiles4(P["L3_ffn_k2"].view(np.uint8), 32, 8),
        "s0": P["L3_ffn_s0"].reshape(1, 1),
        "s1": P["L3_ffn_s1"].reshape(1, 1),
        "s2": P["L3_ffn_s2"].reshape(1, 1),
        "identf": np.eye(128, dtype=np.float32),
    }
    return [dict(base, resid=parts[c]) for c in range(NCORES)]


def build_l0ffn(b=None):
    if b is None:
        b = B()
    b.prefix = "l0ffn_"
    nc = b.nc
    resid = b.din("resid", [ROWS, D])
    gate_w = b.din("gate_w", [D, 8])
    w0t = b.din("w0t", [D, D])
    w1t = b.din("w1t", [D, D])
    b0v = b.din("b0v", [8, 128])
    b1v = b.din("b1v", [8, 128])
    identf = b.din("identf", [128, 128])
    out = b.dout("out", [ROWS, D])

    with b.seg():
        block = b.block_open()
        skel = Skel(b, block, resid, out)
        lds = b.ch("lds"); mm = b.ch("mm"); ev = b.ch("ev"); gt = b.ch("gt")
        ldg = b.ch("ldg")
        skel.consts(identf)

        gw = b.sb([128, 8 * 8], BF16)
        w0 = b.sb([128, 8 * D], BF16)
        w1 = b.sb([128, 8 * D], BF16)
        b0s = b.sb([128, 8])
        b1s = b.sb([128, 8])

        @block.gpsimd
        def _(g):
            for c in range(8):
                ldg.inc(g.dma_start(out=gw[:, c * 8:(c + 1) * 8],
                                    in_=gate_w[c * 128:(c + 1) * 128, :]), 16)
            for c in range(8):
                ldg.inc(g.dma_start(out=w0[:, c * D:(c + 1) * D],
                                    in_=w0t[c * 128:(c + 1) * 128, :]), 16)
                ldg.inc(g.dma_start(out=w1[:, c * D:(c + 1) * D],
                                    in_=w1t[c * 128:(c + 1) * 128, :]), 16)

        emit_vec_load(b, block, b0s, b0v, 1024, skel, skel.ps_tf[0], lds)
        emit_vec_load(b, block, b1s, b1v, 1024, skel, skel.ps_tf[0], lds)

        e0fm = b.sb([128, KD * TT], BF16)
        e1fm = b.sb([128, KD * TT], BF16)
        ofm = b.sb([128, KD * TT])
        lg = b.sb([8, TT])
        g_tm = b.sb([128, 4 * 8])
        gs = b.sb([128, 12])
        c01 = b.sb([128, 4 * 2])
        c0r = b.sb([1, TT])
        c1r = b.sb([1, TT])
        neginf = b.sb([128, 8])
        msk = b.sb([128, 8], U8)
        ps_a = b.psum([128, TT])
        ps_b = b.psum([128, TT])
        ps_c0 = b.psum([128, TT])
        ps_c1 = b.psum([128, TT])

        @block.vector
        def _(vector):
            skel.aux.inc(vector.memset(neginf[:], -1e30))

        for t in range(NT):
            skel.load_rows(t, war_chs=(skel.ad,) if t else (mm, lds))
            skel.ln_fmize(t, mm_war_ch=mm)

            @block.tensor
            def _(tensor, t=t):
                skel.fmc.wt(tensor)
                if t == 0:
                    lds.wt(tensor)
                    ldg.wt(tensor)
                else:
                    ev.wt(tensor)
                for c in range(KD):
                    ins = tensor.matmul(ps_a[0:8, :], gw[:, c * 8:(c + 1) * 8],
                                        skel.xfm[:, c * TT:(c + 1) * TT],
                                        start=(c == 0), stop=(c == KD - 1))
                gt.inc(ins)

            @block.vector
            def _(vector):
                gt.wt(vector)
                gt.inc(vector.tensor_copy(lg[:], ps_a[0:8, :]))

            for s in range(4):
                @block.tensor
                def _(tensor, s=s):
                    gt.wt(tensor)
                    if s >= 2:
                        gt.wt(tensor, gt.n - 1)
                    skel.tr.inc(tensor.transpose(
                        skel.ps_tf[s % 2][:, 0:8], lg[0:8, s * 128:(s + 1) * 128],
                        skel.idf[0:8, 0:8]))

                @block.vector
                def _(vector, s=s):
                    skel.tr.wt(vector)
                    gt.inc(vector.tensor_copy(g_tm[:, s * 8:(s + 1) * 8],
                                              skel.ps_tf[s % 2][:, 0:8]))

            for s in range(4):
                gsl = g_tm[:, s * 8:(s + 1) * 8]

                @block.vector
                def _(vector, s=s, gsl=gsl):
                    gt.wt(vector)
                    gt.inc(vector.reduce_max(gs[:, 0:1], gsl, axis=AX.X))
                    gt.wt(vector)
                    gt.inc(vector.tensor_scalar(gs[:, 1:2], gs[:, 0:1], -1.0, None, ALU.mult))

                @block.scalar
                def _(scalar, s=s, gsl=gsl):
                    gt.wt(scalar)
                    gt.inc(scalar.activation(skel.sq[:, 0:8], gsl, AF.Exp,
                                             bias=gs[:, 1:2]))

                @block.vector
                def _(vector, s=s):
                    gt.wt(vector)
                    gt.inc(vector.reduce_sum(gs[:, 2:3], skel.sq[:, 0:8], axis=AX.X))
                    gt.wt(vector)
                    gt.inc(vector.reciprocal(gs[:, 3:4], gs[:, 2:3]))
                    gt.wt(vector)
                    gt.inc(vector.tensor_scalar(skel.sq[:, 0:8], skel.sq[:, 0:8],
                                                gs[:, 3:4], None, ALU.mult))
                    gt.wt(vector)
                    gt.inc(vector.reduce_max(gs[:, 4:5], skel.sq[:, 0:8], axis=AX.X))
                    gt.wt(vector)
                    gt.inc(vector.tensor_scalar(c01[:, s * 2:s * 2 + 1], skel.sq[:, 0:1],
                                                gs[:, 4:5], gs[:, 4:5], ALU.is_equal, ALU.mult))
                    gt.wt(vector)
                    gt.inc(vector.tensor_scalar(skel.sq[:, 8:16], skel.sq[:, 0:8], gs[:, 4:5],
                                                1e30, ALU.is_ge, ALU.mult))
                    gt.wt(vector)
                    gt.inc(vector.tensor_tensor(skel.sq[:, 8:16], skel.sq[:, 0:8],
                                                skel.sq[:, 8:16], ALU.subtract))
                    gt.wt(vector)
                    gt.inc(vector.reduce_max(gs[:, 5:6], skel.sq[:, 8:16], axis=AX.X))
                    gt.wt(vector)
                    gt.inc(vector.tensor_scalar(c01[:, s * 2 + 1:s * 2 + 2], skel.sq[:, 9:10],
                                                gs[:, 5:6], gs[:, 5:6], ALU.is_equal, ALU.mult))

            for o in range(KD):
                pst = ps_a if o % 2 == 0 else ps_b

                @block.tensor
                def _(tensor, o=o, pst=pst, t=t):
                    if o < 2:
                        gt.wt(tensor)
                        if t >= 1:
                            ev.wt(tensor)
                    else:
                        ev.wt(tensor, ev.n - 2)
                    for c in range(KD):
                        ins = tensor.matmul(pst[:], w0[:, c * D + o * 128: c * D + (o + 1) * 128],
                                            skel.xfm[:, c * TT:(c + 1) * TT],
                                            start=(c == 0), stop=(c == KD - 1))
                    mm.inc(ins)

                @block.scalar
                def _(scalar, o=o, pst=pst):
                    mm.wt(scalar)
                    ev.inc(scalar.activation(e0fm[:, o * TT:(o + 1) * TT], pst[:],
                                             AF.Silu, bias=b0s[:, o:o + 1]))

                @block.tensor
                def _(tensor, o=o, pst=pst):
                    ev.wt(tensor)
                    for c in range(KD):
                        ins = tensor.matmul(pst[:], w1[:, c * D + o * 128: c * D + (o + 1) * 128],
                                            skel.xfm[:, c * TT:(c + 1) * TT],
                                            start=(c == 0), stop=(c == KD - 1))
                    mm.inc(ins)

                @block.scalar
                def _(scalar, o=o, pst=pst):
                    mm.wt(scalar)
                    ev.inc(scalar.activation(e1fm[:, o * TT:(o + 1) * TT], pst[:],
                                             AF.Silu, bias=b1s[:, o:o + 1]))

            for s in range(4):
                for j in range(2):
                    i = s * 2 + j
                    dstrow = (c0r if j == 0 else c1r)[0:1, s * 128:(s + 1) * 128]

                    @block.tensor
                    def _(tensor, s=s, j=j, i=i):
                        gt.wt(tensor) if i == 0 else gt.wt(tensor, gt.n - 1)
                        skel.tr.inc(tensor.transpose(skel.ps_tf[i % 2][0:1, 0:128],
                                                     c01[:, i:i + 1], skel.idf[:]))

                    @block.vector
                    def _(vector, i=i, dstrow=dstrow):
                        skel.tr.wt(vector)
                        gt.inc(vector.tensor_copy(dstrow, skel.ps_tf[i % 2][0:1, 0:128]))

            @block.tensor
            def _(tensor):
                gt.wt(tensor)
                ev.wt(tensor)
                ins = tensor.matmul(ps_c0[:], skel.ones_row[:], c0r[0:1, :],
                                    start=True, stop=True)
                ins = tensor.matmul(ps_c1[:], skel.ones_row[:], c1r[0:1, :],
                                    start=True, stop=True)
                mm.inc(ins)

            for o in range(KD):
                @block.vector
                def _(vector, o=o):
                    mm.wt(vector)
                    ev.inc(vector.tensor_tensor(ofm[:, o * TT:(o + 1) * TT],
                                                e0fm[:, o * TT:(o + 1) * TT],
                                                ps_c0[:], ALU.mult))
                    ev.wt(vector)
                    ev.inc(vector.tensor_tensor(skel.sq[:, 0:TT],
                                                e1fm[:, o * TT:(o + 1) * TT],
                                                ps_c1[:], ALU.mult))
                    ev.wt(vector)
                    ev.inc(vector.tensor_tensor(ofm[:, o * TT:(o + 1) * TT],
                                                ofm[:, o * TT:(o + 1) * TT],
                                                skel.sq[:, 0:TT], ALU.add))

            skel.epilogue(t, ofm, ev)
        skel.finish()
    return b


def inmaps_l0ffn(P, resid_full):
    parts = shard_rows(resid_full)
    base = {
        "gate_w": P["L0_ffn_gate_w"],
        "w0t": np.ascontiguousarray(P["L0_ffn_w0"].T),
        "w1t": np.ascontiguousarray(P["L0_ffn_w1"].T),
        "b0v": P["L0_ffn_b0"].reshape(8, 128),
        "b1v": P["L0_ffn_b1"].reshape(8, 128),
        "identf": np.eye(128, dtype=np.float32),
    }
    return [dict(base, resid=parts[c]) for c in range(NCORES)]


def build_l3attn_a(b=None):
    if b is None:
        b = B()
    b.prefix = "l3attn_a_"
    nc = b.nc
    resid = b.din("resid", [ROWS, D])
    wembt = b.din("wembt", [D, 2048])
    wat = b.din("wat", [64, 64])
    identf = b.din("identf", [128, 128])
    attnT = b.dout("attnT", [NT * 4, 128, 1024], BF16)
    av_out = b.dout("av_out", [128, 520])

    SCALE = 64.0 ** -0.5
    with b.seg():
        block = b.block_open()
        skel = Skel(b, block, resid, None)
        ldg = b.ch("ldg"); mm = b.ch("mm"); ev = b.ch("ev"); sc = b.ch("sc")
        av = b.ch("av"); sv = b.ch("sv"); pst_ch = b.ch("pst")
        skel.consts(identf)

        wemb = b.sb([128, KD * 2048], BF16)
        wa = b.sb([128, 64], BF16)

        @block.gpsimd
        def _(g):
            for c in range(KD):
                ldg.inc(g.dma_start(out=wemb[:, c * 2048:(c + 1) * 2048],
                                    in_=wembt[c * 128:(c + 1) * 128, :]), 16)
            ldg.inc(g.dma_start(out=wa[0:64, :], in_=wat), 16)
            ldg.inc(g.dma_start(out=wa[64:128, :], in_=wat), 16)

        emb = b.sb([128, 16 * TT], BF16)
        v_tm = b.sb([128, 4 * 1024], BF16)
        p_tm = b.sb([128, 64])
        pstats = b.sb([128, 8])
        att_sb = b.sb([128, 8 * 128], BF16)
        av_sb = b.sb([128, 512])
        s_sb = b.sb([128, 8])
        p_all = b.sb([128, 64 * 64], BF16)
        ps_pool = (b.psum([128, TT]), b.psum([128, TT]))
        ps_av = (b.psum([128, 64]), b.psum([128, 64]))

        @block.vector
        def _(vector):
            skel.aux.wt(vector)
            av.inc(vector.memset(av_sb[:], 0.0))
            sv.inc(vector.memset(s_sb[:], 0.0))

        for t in range(NT):
            skel.load_rows(t, war_chs=() if t else (mm, ldg))
            skel.ln_fmize(t, mm_war_ch=mm)

            for o in range(16):
                pp = ps_pool[o % 2]

                @block.tensor
                def _(tensor, o=o, pp=pp, t=t):
                    if o < 2:
                        skel.fmc.wt(tensor)
                        if t == 0:
                            ldg.wt(tensor)
                        else:
                            ev.wt(tensor)
                            sc.wt(tensor)
                    else:
                        ev.wt(tensor, ev.n - 1)
                    for c in range(KD):
                        ins = tensor.matmul(pp[:], wemb[:, c * 2048 + o * 128: c * 2048 + (o + 1) * 128],
                                            skel.xfm[:, c * TT:(c + 1) * TT],
                                            start=(c == 0), stop=(c == KD - 1))
                    mm.inc(ins)

                @block.vector
                def _(vector, o=o, pp=pp):
                    mm.wt(vector)
                    ev.inc(vector.tensor_copy(emb[:, o * TT:(o + 1) * TT], pp[:]))

            for s in range(4):
                for c in range(8):
                    i = s * 8 + c
                    pt = skel.ps_t[i % 2]

                    @block.tensor
                    def _(tensor, s=s, c=c, i=i, pt=pt, t=t):
                        if i == 0:
                            ev.wt(tensor)
                            if t >= 1:
                                pst_ch.wt(tensor)
                        else:
                            pst_ch.wt(tensor, pst_ch.n - 1)
                        mm.inc(tensor.transpose(
                            pt[:], emb[:, (8 + c) * TT + s * 128:(8 + c) * TT + (s + 1) * 128],
                            skel.identb[:]))

                    @block.vector
                    def _(vector, s=s, c=c, pt=pt):
                        mm.wt(vector)
                        pst_ch.inc(vector.tensor_copy(v_tm[:, s * 1024 + c * 128: s * 1024 + (c + 1) * 128], pt[:]))

            for s in range(4):
                for h in range(16):
                    j = h // 2; hh = h % 2
                    c = h // 2; r0 = (h % 2) * 64
                    pp = ps_pool[(s * 16 + h) % 2]

                    @block.tensor
                    def _(tensor, s=s, h=h, c=c, r0=r0, pp=pp):
                        pst_ch.wt(tensor)
                        sc.wt(tensor)
                        mm.inc(tensor.matmul(
                            pp[0:128, 0:64],
                            emb[r0:r0 + 64, c * TT + s * 128: c * TT + (s + 1) * 128],
                            wa[r0:r0 + 64, :], start=True, stop=True))

                    @block.scalar
                    def _(scalar, pp=pp):
                        mm.wt(scalar)
                        sc.inc(scalar.activation(p_tm[:], pp[0:128, 0:64], AF.Copy,
                                                 scale=SCALE))

                    @block.vector
                    def _(vector):
                        sc.wt(vector)
                        sc.inc(vector.reduce_max(pstats[:, 0:1], p_tm[:], axis=AX.X))
                        sc.wt(vector)
                        sc.inc(vector.tensor_scalar(pstats[:, 1:2], pstats[:, 0:1], -1.0, None, ALU.mult))

                    @block.scalar
                    def _(scalar):
                        sc.wt(scalar)
                        sc.inc(scalar.activation(p_tm[:], p_tm[:], AF.Exp, bias=pstats[:, 1:2]))

                    @block.vector
                    def _(vector, s=s, h=h):
                        sc.wt(vector)
                        sc.inc(vector.reduce_sum(pstats[:, 2:3], p_tm[:], axis=AX.X))
                        sc.wt(vector)
                        sc.inc(vector.reciprocal(pstats[:, 3:4], pstats[:, 2:3]))
                        sc.wt(vector)
                        sc.inc(vector.tensor_scalar(
                            p_all[:, (s * 16 + h) * 64:(s * 16 + h) * 64 + 64],
                            p_tm[:], pstats[:, 3:4], None, ALU.mult))

                    @block.tensor
                    def _(tensor, s=s, h=h, hh=hh, j=j):
                        sc.wt(tensor)
                        pst_ch.wt(tensor, max(0, pst_ch.n - 1))
                        mm.inc(tensor.transpose(
                            skel.ps_t[j % 2][hh * 64:hh * 64 + 64, 0:128],
                            p_all[:, (s * 16 + h) * 64:(s * 16 + h) * 64 + 64],
                            skel.identb[:]))

                    if hh == 1:
                        @block.vector
                        def _(vector, s=s, j=j):
                            mm.wt(vector)
                            sv.wt(vector)
                            pst_ch.inc(vector.tensor_scalar(
                                att_sb[:, j * 128:(j + 1) * 128],
                                skel.ps_t[j % 2][:, 0:128], 0.0, 0.0, ALU.add,
                                ALU.add, accum_out=pstats[:, 4:5]))
                            sv.inc(vector.tensor_tensor(
                                s_sb[:, j:j + 1], s_sb[:, j:j + 1], pstats[:, 4:5], ALU.add))

                @block.sync
                def _(sync, s=s, t=t):
                    pst_ch.wt(sync)
                    skel.sto.inc(sync.dma_start(out=attnT[t * 4 + s], in_=att_sb[:]), 16)

            for j in range(8):
                for hh in range(2):
                    @block.tensor
                    def _(tensor, j=j, hh=hh, t=t):
                        if hh == 0 and j < 2:
                            sc.wt(tensor)
                            if t >= 1 or j >= 2:
                                av.wt(tensor)
                        elif hh == 0:
                            av.wt(tensor, av.n - 1)
                        for s in range(4):
                            h = 2 * j + hh
                            ins = tensor.matmul(
                                ps_av[j % 2][hh * 64: hh * 64 + 64, 0:64],
                                p_all[:, (s * 16 + h) * 64:(s * 16 + h) * 64 + 64],
                                v_tm[:, s * 1024 + h * 64: s * 1024 + (h + 1) * 64],
                                start=(s == 0), stop=(s == 3))
                        mm.inc(ins)

                @block.vector
                def _(vector, j=j):
                    mm.wt(vector)
                    av.wt(vector)
                    av.inc(vector.tensor_tensor(av_sb[:, j * 64:(j + 1) * 64],
                                                av_sb[:, j * 64:(j + 1) * 64],
                                                ps_av[j % 2][:, 0:64], ALU.add))

        @block.sync
        def _(sync):
            av.wt(sync)
            sv.wt(sync)
            skel.sto.inc(sync.dma_start(out=av_out[:, 0:512], in_=av_sb[:]), 16)
            skel.sto.inc(sync.dma_start(out=av_out[:, 512:520], in_=s_sb[:]), 16)
        skel.finish()
    return b


def build_l3attn_b(b=None):
    if b is None:
        b = B()
    b.prefix = "l3attn_b_"
    nc = b.nc
    resid = b.din("resid", [ROWS, D])
    av_all = b.din("av_all", [NCORES, 128, 520])
    attnT = b.din("attnT", [NT * 4, 128, 1024], BF16)
    woutt = b.din("woutt", [D, D])
    boutv = b.din("boutv", [8, 128])
    identf = b.din("identf", [128, 128])
    out = b.dout("out", [ROWS, D])

    with b.seg():
        block = b.block_open()
        skel = Skel(b, block, resid, out)
        ldg = b.ch("ldg"); lds = b.ch("lds"); mm = b.ch("mm"); ev = b.ch("ev")
        ya = b.ch("ya"); lda = b.ch("lda")
        skel.consts(identf)

        wout = b.sb([128, KD * D], BF16)
        bouts = b.sb([128, 8])
        avg = b.sb([128, 520])
        avt = b.sb([128, 520])
        outg = b.sb([128, 512], BF16)
        rst = b.sb([128, 16], )
        att_t = b.sb([128, 4 * 1024], BF16)
        y_fm = b.sb([128, KD * TT], BF16)
        ofm = b.sb([128, KD * TT])
        ps_y = (b.psum([128, 128]), b.psum([128, 128]))
        ps_o = (b.psum([128, TT]), b.psum([128, TT]))

        @block.gpsimd
        def _(g):
            for c in range(KD):
                ldg.inc(g.dma_start(out=wout[:, c * D:(c + 1) * D],
                                    in_=woutt[c * 128:(c + 1) * 128, :]), 16)

        emit_vec_load(b, block, bouts, boutv, 1024, skel, skel.ps_tf[0], lds)

        @block.sync
        def _(sync):
            lds.wt(sync)
            lds.inc(sync.dma_start(out=avg[:], in_=av_all[0]), 16)
            lds.inc(sync.dma_start(out=avt[:], in_=av_all[1]), 16)

        @block.vector
        def _(vector):
            lds.wt(vector)
            ya.inc(vector.tensor_add(avg[:], avg[:], avt[:]))
        for cc in range(2, NCORES):
            @block.sync
            def _(sync, cc=cc):
                ya.wt(sync)
                lds.wt(sync)
                lds.inc(sync.dma_start(out=avt[:], in_=av_all[cc]), 16)

            @block.vector
            def _(vector):
                lds.wt(vector)
                ya.wt(vector)
                ya.inc(vector.tensor_add(avg[:], avg[:], avt[:]))

        @block.vector
        def _(vector):
            ya.wt(vector)
            ya.inc(vector.reciprocal(rst[:, 0:8], avg[:, 512:520]))
            ya.wt(vector)
            ya.inc(vector.tensor_scalar(rst[:, 8:16], avg[:, 512:520], 0.0, 1e30,
                                        ALU.is_equal, ALU.mult))
            ya.wt(vector)
            ya.inc(vector.tensor_tensor(rst[:, 0:8], rst[:, 0:8], rst[:, 8:16], ALU.subtract))
            for j in range(8):
                ya.wt(vector)
                ya.inc(vector.tensor_scalar(outg[:, j * 64:(j + 1) * 64],
                                            avg[:, j * 64:(j + 1) * 64],
                                            rst[:, j:j + 1], None, ALU.mult))

        for t in range(NT):
            skel.load_rows(t, war_chs=(skel.ad,) if t else (mm, ldg))
            skel.ln_fmize(t, mm_war_ch=mm)

            @block.sync
            def _(sync, t=t):
                if t >= 1:
                    mm.wt(sync)
                for s in range(4):
                    lda.inc(sync.dma_start(out=att_t[:, s * 1024:(s + 1) * 1024],
                                           in_=attnT[t * 4 + s]), 16)

            for s in range(4):
                for h in range(16):
                    j = h // 2; hh = h % 2; i = s * 16 + h
                    pp = ps_y[i % 2]

                    @block.tensor
                    def _(tensor, s=s, h=h, j=j, hh=hh, i=i, pp=pp, t=t):
                        if i == 0:
                            lda.wt(tensor)
                            ya.wt(tensor)
                            if t >= 1:
                                ev.wt(tensor)
                        else:
                            ev.wt(tensor, ev.n - 1)
                        mm.inc(tensor.matmul(
                            pp[0:64, 0:128],
                            outg[hh * 64:hh * 64 + 64, j * 64:(j + 1) * 64],
                            att_t[hh * 64:hh * 64 + 64, s * 1024 + j * 128: s * 1024 + (j + 1) * 128],
                            start=True, stop=True))

                    @block.vector
                    def _(vector, s=s, h=h, pp=pp):
                        mm.wt(vector)
                        c = h // 2; r0 = (h % 2) * 64
                        ev.inc(vector.tensor_copy(
                            y_fm[r0:r0 + 64, c * TT + s * 128: c * TT + (s + 1) * 128],
                            pp[0:64, 0:128]))

            for o in range(KD):
                pp = ps_o[o % 2]

                @block.tensor
                def _(tensor, o=o, pp=pp, t=t):
                    if o < 2:
                        ev.wt(tensor)
                        if t >= 1:
                            ya.wt(tensor)
                    else:
                        ya.wt(tensor, ya.n - 1)
                    for c in range(KD):
                        ins = tensor.matmul(pp[:], wout[:, c * D + o * 128: c * D + (o + 1) * 128],
                                            y_fm[:, c * TT:(c + 1) * TT],
                                            start=(c == 0), stop=(c == KD - 1))
                    mm.inc(ins)

                @block.vector
                def _(vector, o=o, pp=pp):
                    mm.wt(vector)
                    ya.inc(vector.tensor_scalar(ofm[:, o * TT:(o + 1) * TT], pp[:],
                                                bouts[:, o:o + 1], None, ALU.add))

            skel.epilogue(t, ofm, ya)
        skel.finish()
    return b


def inmaps_l3attn_a(P, resid_full):
    parts = shard_rows(resid_full)
    base = {
        "wembt": np.ascontiguousarray(P["L3_attn_W_emb"].T),
        "wat": np.ascontiguousarray(P["L3_attn_W_anchor"].T),
        "identf": np.eye(128, dtype=np.float32),
    }
    return [dict(base, resid=parts[c]) for c in range(NCORES)]


def inmaps_l3attn_b(P, resid_full, av_all, attnT_parts):
    parts = shard_rows(resid_full)
    base = {
        "woutt": np.ascontiguousarray(P["L3_attn_W_out"].T),
        "boutv": P["L3_attn_b_out"].reshape(8, 128),
        "av_all": av_all,
        "identf": np.eye(128, dtype=np.float32),
    }
    return [dict(base, resid=parts[c], attnT=attnT_parts[c]) for c in range(NCORES)]


def build_l1attn(b=None):
    if b is None:
        b = B()
    b.prefix = "l1attn_"
    nc = b.nc
    resid = b.din("resid", [ROWS, D])
    wint = b.din("wint", [D, 3072])
    binv = b.din("binv", [24, 128])
    woutt = b.din("woutt", [D, D])
    boutv = b.din("boutv", [8, 128])
    kd0 = b.din("kd0", [128, 1])
    kd1 = b.din("kd1", [128, 1])
    vd0 = b.din("vd0", [128, 1])
    vd1 = b.din("vd1", [128, 1])
    patr = b.din("patr", [8, 128, 16])
    patb = b.din("patb", [8, 16, 128])
    identf = b.din("identf", [128, 128])
    out = b.dout("out", [ROWS, D])

    PTT = 512
    PH = 256
    with b.seg():
        block = b.block_open()
        skel = Skel(b, block, resid, out)
        ldg = b.ch("ldg"); lds = b.ch("lds"); mm = b.ch("mm"); ev = b.ch("ev")
        at = b.ch("at"); ya = b.ch("ya")
        skel.consts(identf)

        win = b.sb([128, KD * 3072], BF16)
        wout = b.sb([128, KD * D], BF16)
        bins = b.sb([128, 24])
        bouts = b.sb([128, 8])
        kdt = b.sb([128, 4])
        patr_sb = b.sb([128, 8 * 16], BF16)
        patb_sb = b.sb([16, 8 * 128], BF16)

        @block.gpsimd
        def _(g):
            for c in range(KD):
                ldg.inc(g.dma_start(out=win[:, c * 3072:(c + 1) * 3072],
                                    in_=wint[c * 128:(c + 1) * 128, :]), 16)
                ldg.inc(g.dma_start(out=wout[:, c * D:(c + 1) * D],
                                    in_=woutt[c * 128:(c + 1) * 128, :]), 16)
            ldg.inc(g.dma_start(out=patr_sb.rearrange("p (c x) -> p c x", x=16),
                                in_=patr.rearrange("c p x -> p c x")), 16)
            ldg.inc(g.dma_start(out=patb_sb.rearrange("p (c x) -> p c x", x=128),
                                in_=patb.rearrange("c p x -> p c x")), 16)

        @block.sync
        def _(sync):
            lds.wt(sync)
            lds.inc(sync.dma_start(out=kdt[:, 0:1], in_=kd0), 16)
            lds.inc(sync.dma_start(out=kdt[:, 1:2], in_=kd1), 16)
            lds.inc(sync.dma_start(out=kdt[:, 2:3], in_=vd0), 16)
            lds.inc(sync.dma_start(out=kdt[:, 3:4], in_=vd1), 16)

        emit_vec_load(b, block, bins, binv, 3072, skel, skel.ps_tf[0], lds)
        emit_vec_load(b, block, bouts, boutv, 1024, skel, skel.ps_tf[0], lds)

        xr8 = b.sb([128, 4 * D])
        xfm_ = b.sb([128, KD * PTT], BF16)
        qkv = b.sb([128, 24 * PTT], BF16)
        A00 = b.sb([16, PH]); A01 = b.sb([16, PH])
        A10 = b.sb([16, PH]); A11 = b.sb([16, PH])
        pstats = b.sb([16, 6 * PH])
        kmod = b.sb([128, PH], BF16)
        prod = b.sb([128, PH], BF16)
        Pa = b.sb([16, PH], BF16)
        Pb = b.sb([16, PH], BF16)
        o_fm = b.sb([128, KD * PTT], BF16)
        ofm = b.sb([128, KD * PTT])
        ps_mm = (b.psum([128, PH]), b.psum([128, PH]))
        ps_att = (b.psum([16, PH]), b.psum([16, PH]))

        NPT = ROWS // 2 // PH
        for t in range(NPT):
            @block.sync
            def _(sync, t=t):
                if t == 0:
                    mm.wt(sync); ldg.wt(sync); lds.wt(sync)
                else:
                    skel.sto.wt(sync)
                r0 = resid[t * PH:(t + 1) * PH, :].rearrange("(s p) d -> p s d", p=128)
                r1 = resid[2048 + t * PH: 2048 + (t + 1) * PH, :].rearrange("(s p) d -> p s d", p=128)
                skel.ldx.inc(sync.dma_start(
                    out=xr8.rearrange("p (s d) -> p s d", s=4)[:, 0:2], in_=r0), 16)
                skel.ldx.inc(sync.dma_start(
                    out=xr8.rearrange("p (s d) -> p s d", s=4)[:, 2:4], in_=r1), 16)

            for s in range(4):
                if s == 0 and t >= 1:
                    @block.vector
                    def _(vector):
                        mm.wt(vector)
                        skel.ln.inc(vector.engine_nop())
                emit_ln_subtile(b, block, xr8[:, s * D:(s + 1) * D], skel.xnb[:],
                                skel.sq, skel.stat, skel.ldx, skel.ln, eps_ap=skel.epsb[:])
                emit_tm_to_fm(b, block, skel.xnb[:],
                              lambda c, s=s: xfm_[:, c * PTT + s * 128: c * PTT + (s + 1) * 128],
                              skel.identb[:], skel.ps_t, skel.tr, skel.fmc, ready_ch=skel.ln)

            for o in range(24):
                for half in range(2):
                    pp = ps_mm[(o * 2 + half) % 2]

                    @block.tensor
                    def _(tensor, o=o, half=half, pp=pp, t=t):
                        i = o * 2 + half
                        if i < 2:
                            skel.fmc.wt(tensor)
                            if t >= 1:
                                ev.wt(tensor)
                                at.wt(tensor)
                                ya.wt(tensor)
                        else:
                            ev.wt(tensor, ev.n - 1)
                        for c in range(KD):
                            ins = tensor.matmul(
                                pp[:], win[:, c * 3072 + o * 128: c * 3072 + (o + 1) * 128],
                                xfm_[:, c * PTT + half * PH: c * PTT + (half + 1) * PH],
                                start=(c == 0), stop=(c == KD - 1))
                        mm.inc(ins)

                    @block.vector
                    def _(vector, o=o, half=half, pp=pp):
                        mm.wt(vector)
                        ev.inc(vector.tensor_scalar(
                            qkv[:, o * PTT + half * PH: o * PTT + (half + 1) * PH],
                            pp[:], bins[:, o:o + 1], None, ALU.add))

            for (s_, t_, Adst) in ((0, 0, A00), (0, 1, A01), (1, 0, A10), (1, 1, A11)):
                kdcol = 0 if s_ == t_ else 1
                pa = ps_att[(s_ * 2 + t_) % 2]

                for c in range(KD):
                    @block.vector
                    def _(vector, c=c, s_=s_, t_=t_, kdcol=kdcol):
                        ev.wt(vector)
                        at.wt(vector)
                        mm.wt(vector)
                        at.inc(vector.tensor_scalar(
                            kmod[:], qkv[:, (8 + c) * PTT + t_ * PH:(8 + c) * PTT + (t_ + 1) * PH],
                            kdt[:, kdcol:kdcol + 1], None, ALU.add))
                        at.wt(vector)
                        at.inc(vector.tensor_tensor(
                            prod[:], qkv[:, c * PTT + s_ * PH: c * PTT + (s_ + 1) * PH],
                            kmod[:], ALU.mult))

                    @block.tensor
                    def _(tensor, c=c, pa=pa):
                        at.wt(tensor)
                        if c == 0:
                            at.wt(tensor)
                        mm.inc(tensor.matmul(pa[:], patr_sb[:, c * 16:(c + 1) * 16],
                                             prod[:], start=(c == 0), stop=(c == KD - 1)))

                @block.vector
                def _(vector, Adst=Adst, pa=pa):
                    mm.wt(vector)
                    at.inc(vector.tensor_scalar(Adst[:], pa[:], 0.125, None, ALU.mult))

            for s_ in range(2):
                Aa = A00 if s_ == 0 else A10
                Ab = A01 if s_ == 0 else A11

                @block.vector
                def _(vector, Aa=Aa, Ab=Ab):
                    at.wt(vector)
                    at.inc(vector.tensor_tensor(pstats[:, 0:PH], Aa[:], Ab[:], ALU.max))
                    at.wt(vector)
                    at.inc(vector.tensor_tensor(pstats[:, PH:2*PH], Aa[:], pstats[:, 0:PH], ALU.subtract))
                    at.wt(vector)
                    at.inc(vector.tensor_tensor(pstats[:, 2*PH:3*PH], Ab[:], pstats[:, 0:PH], ALU.subtract))

                @block.scalar
                def _(scalar):
                    at.wt(scalar)
                    at.inc(scalar.activation(pstats[:, PH:2*PH], pstats[:, PH:2*PH], AF.Exp))
                    at.wt(scalar)
                    at.inc(scalar.activation(pstats[:, 2*PH:3*PH], pstats[:, 2*PH:3*PH], AF.Exp))

                @block.vector
                def _(vector, Aa=Aa, Ab=Ab):
                    mm.wt(vector)
                    at.wt(vector)
                    at.inc(vector.tensor_tensor(pstats[:, 3*PH:4*PH], pstats[:, PH:2*PH],
                                                pstats[:, 2*PH:3*PH], ALU.add))
                    at.wt(vector)
                    at.inc(vector.reciprocal(pstats[:, 4*PH:5*PH], pstats[:, 3*PH:4*PH]))
                    at.wt(vector)
                    at.inc(vector.tensor_tensor(Pa[:], pstats[:, PH:2*PH],
                                                pstats[:, 4*PH:5*PH], ALU.mult))
                    at.wt(vector)
                    at.inc(vector.tensor_tensor(Pb[:], pstats[:, 2*PH:3*PH],
                                                pstats[:, 4*PH:5*PH], ALU.mult))

                for c in range(KD):
                    for t_ in range(2):
                        p_st = Pa if t_ == 0 else Pb
                        vdcol = 2 + (0 if s_ == t_ else 1)
                        pp = ps_mm[t_]

                        @block.tensor
                        def _(tensor, c=c, t_=t_, p_st=p_st, pp=pp):
                            at.wt(tensor)
                            ya.wt(tensor)
                            mm.inc(tensor.matmul(pp[:], patb_sb[:, c * 128:(c + 1) * 128],
                                                 p_st[:], start=True, stop=True))

                        @block.vector
                        def _(vector, c=c, s_=s_, t_=t_, vdcol=vdcol, pp=pp):
                            mm.wt(vector)
                            ya.wt(vector)
                            ya.inc(vector.tensor_scalar(
                                kmod[:], qkv[:, (16 + c) * PTT + t_ * PH:(16 + c) * PTT + (t_ + 1) * PH],
                                kdt[:, vdcol:vdcol + 1], None, ALU.add))
                            ya.wt(vector)
                            if t_ == 0:
                                ya.inc(vector.tensor_tensor(
                                    o_fm[:, c * PTT + s_ * PH: c * PTT + (s_ + 1) * PH],
                                    kmod[:], pp[:], ALU.mult))
                            else:
                                ya.inc(vector.tensor_tensor(prod[:], kmod[:], pp[:], ALU.mult))
                                ya.wt(vector)
                                ya.inc(vector.tensor_tensor(
                                    o_fm[:, c * PTT + s_ * PH: c * PTT + (s_ + 1) * PH],
                                    o_fm[:, c * PTT + s_ * PH: c * PTT + (s_ + 1) * PH],
                                    prod[:], ALU.add))

            for o in range(KD):
                for half in range(2):
                    pp = ps_mm[(o * 2 + half) % 2]

                    @block.tensor
                    def _(tensor, o=o, half=half, pp=pp):
                        i = o * 2 + half
                        ya.wt(tensor)
                        if i >= 2:
                            ya.wt(tensor, ya.n - 1)
                        for c in range(KD):
                            ins = tensor.matmul(
                                pp[:], wout[:, c * D + o * 128: c * D + (o + 1) * 128],
                                o_fm[:, c * PTT + half * PH: c * PTT + (half + 1) * PH],
                                start=(c == 0), stop=(c == KD - 1))
                        mm.inc(ins)

                    @block.vector
                    def _(vector, o=o, half=half, pp=pp):
                        mm.wt(vector)
                        ya.inc(vector.tensor_scalar(
                            ofm[:, o * PTT + half * PH: o * PTT + (half + 1) * PH],
                            pp[:], bouts[:, o:o + 1], None, ALU.add))

            for o in range(KD):
                for s in range(4):
                    i = o * 8 + s
                    pstf = skel.ps_tf[i % 2]

                    @block.tensor
                    def _(tensor, o=o, s=s, i=i, pstf=pstf, t=t):
                        if i == 0:
                            ya.wt(tensor)
                            if t >= 1:
                                skel.ad.wt(tensor)
                        elif i >= 2:
                            skel.ad.wt(tensor, skel.ad.n - 1)
                        skel.tr.inc(tensor.transpose(
                            pstf[:], ofm[:, o * PTT + s * 128: o * PTT + (s + 1) * 128],
                            skel.idf[:]))

                    @block.vector
                    def _(vector, o=o, s=s, pstf=pstf):
                        skel.tr.wt(vector)
                        dst = xr8[:, s * D + o * 128: s * D + (o + 1) * 128]
                        skel.ad.inc(vector.tensor_tensor(dst, dst, pstf[:], ALU.add))

            @block.sync
            def _(sync, t=t):
                skel.ad.wt(sync)
                o0 = out[t * PH:(t + 1) * PH, :].rearrange("(s p) d -> p s d", p=128)
                o1 = out[2048 + t * PH: 2048 + (t + 1) * PH, :].rearrange("(s p) d -> p s d", p=128)
                skel.sto.inc(sync.dma_start(
                    out=o0, in_=xr8.rearrange("p (s d) -> p s d", s=4)[:, 0:2]), 16)
                skel.sto.inc(sync.dma_start(
                    out=o1, in_=xr8.rearrange("p (s d) -> p s d", s=4)[:, 2:4]), 16)
        skel.finish()
    return b


def inmaps_l1attn(P, resid_full):
    parts = shard_rows(resid_full)
    patr = np.zeros((8, 128, 16), np.float32)
    patb = np.zeros((8, 16, 128), np.float32)
    for c in range(8):
        for hh in range(2):
            h = 2 * c + hh
            patr[c, hh * 64:(hh + 1) * 64, h] = 1.0
            patb[c, h, hh * 64:(hh + 1) * 64] = 1.0
    base = {
        "wint": np.ascontiguousarray(P["L1_attn_W_in"].T),
        "binv": P["L1_attn_b_in"].reshape(24, 128),
        "woutt": np.ascontiguousarray(P["L1_attn_W_out"].T),
        "boutv": P["L1_attn_b_out"].reshape(8, 128),
        "kd0": np.tile(P["L1_attn_k_dist_emb"][0], 2).reshape(128, 1),
        "kd1": np.tile(P["L1_attn_k_dist_emb"][1], 2).reshape(128, 1),
        "vd0": np.tile(P["L1_attn_v_dist_emb"][0], 2).reshape(128, 1),
        "vd1": np.tile(P["L1_attn_v_dist_emb"][1], 2).reshape(128, 1),
        "patr": patr, "patb": patb,
        "identf": np.eye(128, dtype=np.float32),
    }
    return [dict(base, resid=parts[c]) for c in range(NCORES)]


HB = 2176


def build_l2attn_a(b=None):
    if b is None:
        b = B()
    b.prefix = "l2attn_a_"
    nc = b.nc
    resid = b.din("resid", [ROWS, D])
    tails = b.din("tails", [2048, D])
    selmask = b.din("selmask", [128, 8])
    wintile = b.din("wintile", [D, 2048])
    wxp = b.din("wxp", [D, 33])
    convw = b.din("convw", [8, 128, 4])
    convb = b.din("convb", [8, 128])
    identf = b.din("identf", [128, 128])
    uscr = b.dout("uscr", [8, 128, 4096], BF16)
    zscr = b.dout("zscr", [8, 128, 4096], BF16)
    bcd_out = b.dout("bcd_out", [33, 4096])

    with b.seg():
        block = b.block_open()
        skel = Skel(b, block, resid, None)
        ldg = b.ch("ldg"); lds = b.ch("lds"); mm = b.ch("mm"); ev = b.ch("ev")
        cv = b.ch("cv"); pj = b.ch("pj"); zst = b.ch("zst"); ust = b.ch("ust")
        skel.consts(identf)

        win = b.sb([128, KD * 2048], BF16)
        wxps = b.sb([128, KD * 33], BF16)
        cwv = b.sb([128, 8 * 4])
        cbv = b.sb([128, 8])
        selm = b.sb([128, 8])

        @block.gpsimd
        def _(g):
            for c in range(KD):
                ldg.inc(g.dma_start(out=win[:, c * 2048:(c + 1) * 2048],
                                    in_=wintile[c * 128:(c + 1) * 128, :]), 16)
                ldg.inc(g.dma_start(out=wxps[:, c * 33:(c + 1) * 33],
                                    in_=wxp[c * 128:(c + 1) * 128, :]), 16)

        @block.sync
        def _(sync):
            lds.wt(sync)
            lds.inc(sync.dma_start(out=cwv.rearrange("p (c j) -> p c j", j=4),
                                   in_=convw.rearrange("c p j -> p c j")), 16)
            lds.inc(sync.dma_start(out=selm[:], in_=selmask), 16)

        emit_vec_load(b, block, cbv, convb, 1024, skel, skel.ps_tf[0], lds)

        xm = b.sb([128, KD * 2 * HB], BF16)
        xr1 = b.sb([128, D])
        tailbuf = b.sb([128, D])
        hsel = b.ch("hsel")
        z_sb = b.sb([128, 8 * 128], BF16)
        cacc = b.sb([128, 2048])
        cacc2 = b.sb([128, 2048])
        bcd_sb = b.sb([33, 4096])
        sprow = b.sb([1, 4096])
        ps_mm = (b.psum([128, 128]), b.psum([128, 128]))
        ps_pj = (b.psum([33, 512]), b.psum([33, 512]))

        for s in range(34):
            h, k = divmod(s, 17)

            @block.sync
            def _(sync, s=s, h=h, k=k):
                if s == 0:
                    mm.wt(sync); ldg.wt(sync); lds.wt(sync)
                else:
                    zst.wt(sync)
                    skel.ln.wt(sync)
                if k == 0:
                    hsel.wt(sync)
                    skel.ldx.inc(sync.dma_start(
                        out=tailbuf[:], in_=tails[h * 128:(h + 1) * 128, :]), 16)
                else:
                    rows = resid[(h * 2048 + (k - 1) * 128):(h * 2048 + k * 128), :]
                    skel.ldx.inc(sync.dma_start(out=xr1[:], in_=rows), 16)

            if k == 0:
                @block.vector
                def _(vector, h=h):
                    skel.ldx.wt(vector)
                    skel.ln.wt(vector)
                    lds.wt(vector)
                    hsel.inc(vector.tensor_scalar(xr1[:], tailbuf[:],
                                                  selm[:, 0:1], None, ALU.mult))
                for r in range(1, NCORES):
                    @block.sync
                    def _(sync2, h=h, r=r):
                        hsel.wt(sync2)
                        skel.ldx.inc(sync2.dma_start(
                            out=tailbuf[:],
                            in_=tails[r * 256 + h * 128: r * 256 + (h + 1) * 128, :]), 16)

                    @block.vector
                    def _(vector, r=r):
                        skel.ldx.wt(vector)
                        hsel.wt(vector)
                        hsel.inc(vector.scalar_tensor_tensor(
                            xr1[:], tailbuf[:], selm[:, r:r + 1],
                            xr1[:], ALU.mult, ALU.add))

                @block.vector
                def _(vector):
                    hsel.wt(vector)
                    skel.ln.inc(vector.engine_nop())

            if s >= 1:
                @block.vector
                def _(vector):
                    mm.wt(vector)
                    skel.ln.inc(vector.engine_nop())
            emit_ln_subtile(b, block, xr1[:], skel.xnb[:], skel.sq,
                            skel.stat, skel.ldx, skel.ln, eps_ap=skel.epsb[:])
            emit_tm_to_fm(b, block, skel.xnb[:],
                          lambda c: skel.xfm[:, c * 128:(c + 1) * 128],
                          skel.identb[:], skel.ps_t, skel.tr, skel.fmc, ready_ch=skel.ln)

            for o in range(16):
                pp = ps_mm[o % 2]

                @block.tensor
                def _(tensor, s=s, o=o, pp=pp):
                    if o == 0:
                        skel.fmc.wt(tensor)
                        if s > 0 or True:
                            ev.wt(tensor)
                    elif o >= 2:
                        ev.wt(tensor, ev.n - 1)
                    for c in range(KD):
                        ins = tensor.matmul(pp[:], win[:, c * 2048 + o * 128: c * 2048 + (o + 1) * 128],
                                            skel.xfm[:, c * 128:(c + 1) * 128],
                                            start=(c == 0), stop=(c == KD - 1))
                    mm.inc(ins)

                @block.vector
                def _(vector, s=s, h=h, k=k, o=o, pp=pp):
                    mm.wt(vector)
                    if o == 8:
                        zst.wt(vector)
                    if o < 8:
                        dst = xm[:, o * 2 * HB + h * HB + k * 128: o * 2 * HB + h * HB + (k + 1) * 128]
                    else:
                        dst = z_sb[:, (o - 8) * 128:(o - 7) * 128]
                    ev.inc(vector.tensor_copy(dst, pp[:]))

            if k >= 1:
                @block.sync
                def _(sync, s=s, h=h, k=k):
                    ev.wt(sync)
                    zst.inc(sync.dma_start(
                        out=zscr[:, :, h * 2048 + (k - 1) * 128: h * 2048 + k * 128].rearrange("c p q -> p c q"),
                        in_=z_sb.rearrange("p (c q) -> p c q", q=128)), 16)

        for c in range(KD):
            for h in range(2):
                base = c * 2 * HB + h * HB + 128

                @block.vector
                def _(vector, c=c, h=h, base=base):
                    ev.wt(vector)
                    cv.wt(vector)
                    cv.inc(vector.tensor_scalar(cacc[:], xm[:, base - 3: base + 2045],
                                                cwv[:, c * 4:c * 4 + 1], None, ALU.mult))
                    cv.wt(vector)
                    cv.inc(vector.scalar_tensor_tensor(cacc2[:], xm[:, base - 2: base + 2046],
                                                       cwv[:, c * 4 + 1:c * 4 + 2], cacc[:],
                                                       ALU.mult, ALU.add))
                    cv.wt(vector)
                    cv.inc(vector.scalar_tensor_tensor(cacc[:], xm[:, base - 1: base + 2047],
                                                       cwv[:, c * 4 + 2:c * 4 + 3], cacc2[:],
                                                       ALU.mult, ALU.add))
                    cv.wt(vector)
                    cv.inc(vector.scalar_tensor_tensor(cacc2[:], xm[:, base: base + 2048],
                                                       cwv[:, c * 4 + 3:c * 4 + 4], cacc[:],
                                                       ALU.mult, ALU.add))

                @block.scalar
                def _(scalar, c=c, h=h, base=base):
                    cv.wt(scalar)
                    cv.inc(scalar.activation(xm[:, base: base + 2048],
                                             cacc2[:], AF.Silu, bias=cbv[:, c:c + 1]))

            @block.gpsimd
            def _(g, c=c):
                cv.wt(g)
                ust.inc(g.dma_start(out=uscr[c, :, 0:2048],
                                    in_=xm[:, c * 2 * HB + 128: c * 2 * HB + 2176]), 16)
                ust.inc(g.dma_start(out=uscr[c, :, 2048:4096],
                                    in_=xm[:, c * 2 * HB + HB + 128: c * 2 * HB + HB + 2176]), 16)

        for tau in range(8):
            pp = ps_pj[tau % 2]

            @block.tensor
            def _(tensor, tau=tau, pp=pp):
                cv.wt(tensor)
                if tau >= 2:
                    pj.wt(tensor, pj.n - 1)
                hh = tau // 4
                for c in range(KD):
                    base = c * 2 * HB + hh * HB + 128 + (tau % 4) * 512
                    ins = tensor.matmul(pp[:], wxps[:, c * 33:(c + 1) * 33],
                                        xm[:, base: base + 512],
                                        start=(c == 0), stop=(c == KD - 1))
                mm.inc(ins)

            @block.vector
            def _(vector, tau=tau, pp=pp):
                mm.wt(vector)
                pj.inc(vector.tensor_copy(bcd_sb[:, tau * 512:(tau + 1) * 512], pp[:]))

        spd = b.ch("spd")
        sprow2 = b.sb([1, 4096])

        @block.gpsimd
        def _(g):
            pj.wt(g)
            spd.inc(g.dma_start(out=sprow2[:], in_=bcd_sb[32:33, :]), 16)

        @block.scalar
        def _(scalar):
            spd.wt(scalar)
            pj.inc(scalar.activation(sprow[:], sprow2[:], AF.Abs))
            pj.wt(scalar)
            pj.inc(scalar.activation(sprow[:], sprow[:], AF.Exp, scale=-1.0))
            pj.wt(scalar)
            pj.inc(scalar.activation(sprow[:], sprow[:], AF.Ln, bias=1.0))
            pj.wt(scalar)
            pj.inc(scalar.activation(sprow2[:], sprow2[:], AF.Relu))

        @block.vector
        def _(vector):
            pj.wt(vector)
            pj.inc(vector.tensor_add(sprow2[:], sprow2[:], sprow[:]))

        @block.gpsimd
        def _(g):
            pj.wt(g)
            spd.inc(g.dma_start(out=bcd_sb[32:33, :], in_=sprow2[:]), 16)

        @block.sync
        def _(sync):
            pj.wt(sync)
            spd.wt(sync)
            skel.sto.inc(sync.dma_start(out=bcd_out, in_=bcd_sb[:]), 16)
        skel.finish()

        @block.gpsimd
        def _(g):
            ust.wt(g)

        @block.sync
        def _(sync):
            zst.wt(sync)
    return b


def build_l2attn_b(b=None):
    if b is None:
        b = B()
    b.prefix = "l2attn_b_"
    nc = b.nc
    u_in = b.din("u_in", [8, 128, 4096], BF16)
    bcd = b.din("bcd", [264, 4096])
    logA = b.din("logA", [128, 16])
    dparam = b.din("dparam", [128, 1])
    y_out = b.dout("y_out", [128, 2, 16384], BF16)

    T = 16384
    CH = 4096
    NQ = CH // 512
    with b.seg():
        block = b.block_open()
        ld = b.ch("ld"); ldu = b.ch("ldu"); rw = b.ch("rw"); mm = b.ch("mm"); sc = b.ch("sc")
        vv = b.ch("vv"); yy = b.ch("yy"); sto = b.ch("sto")

        Acol = b.sb([128, 16])
        dpar = b.sb([128, 1])
        ones_row = b.sb([1, 128])
        ub = b.sb([128, T], BF16)
        utl = b.sb([128, T], BF16)
        drow = b.sb([1, CH])
        brow = b.sb([1, CH])
        crow = b.sb([1, CH])
        ablk = b.sb([128, CH], BF16)
        bublk = b.sb([128, CH], BF16)
        hblk = b.sb([128, CH], BF16)
        ycur = b.sb([128, T], BF16)
        yterm = b.sb([128, 512])
        stcol = b.sb([128, 2])
        ps_k = (b.psum([128, 512]), b.psum([128, 512]))

        @block.sync
        def _(sync):
            ld.inc(sync.dma_start(out=Acol[:], in_=logA), 16)
            ld.inc(sync.dma_start(out=dpar[:], in_=dparam), 16)

        @block.vector
        def _(vector):
            ld.wt(vector)
            vv.inc(vector.memset(ones_row[:], 1.0))

        @block.scalar
        def _(scalar):
            ld.wt(scalar)
            vv.wt(scalar)
            vv.inc(scalar.activation(Acol[:], Acol[:], AF.Exp))

        @block.vector
        def _(vector):
            vv.wt(vector)
            vv.inc(vector.tensor_scalar(Acol[:], Acol[:], -1.0, None, ALU.mult))

        for bb in range(2):
            @block.sync
            def _(sync, bb=bb):
                if bb == 1:
                    sto.wt(sync)
                    mm.wt(sync)
                    vv.wt(sync)
                for r8 in range(8):
                    ldu.inc(sync.dma_start(out=ub[:, r8 * 2048:(r8 + 1) * 2048],
                                           in_=u_in[r8, :, bb * 2048:(bb + 1) * 2048]), 16)

            for ch0 in range(T // CH):
                @block.sync
                def _(sync, bb=bb, ch0=ch0):
                    mm.wt(sync)
                    rw.wt(sync)
                    for rr in range(2):
                        rw.inc(sync.dma_start(
                            out=drow[0:1, rr * 2048:(rr + 1) * 2048],
                            in_=bcd[(2 * ch0 + rr) * 33 + 32:(2 * ch0 + rr) * 33 + 33,
                                    bb * 2048:(bb + 1) * 2048]), 16)
                for q0 in range(NQ):
                    blk = ch0 * NQ + q0
                    pp = ps_k[q0 % 2]

                    @block.tensor
                    def _(tensor, blk=blk, q0=q0, pp=pp, bb=bb):
                        ld.wt(tensor)
                        ldu.wt(tensor)
                        rw.wt(tensor)
                        vv.wt(tensor)
                        mm.inc(tensor.matmul(pp[:], ones_row[:],
                                             drow[0:1, q0 * 512:(q0 + 1) * 512],
                                             start=True, stop=True))

                    @block.vector
                    def _(vector, blk=blk, pp=pp):
                        mm.wt(vector)
                        vv.inc(vector.tensor_tensor(utl[:, blk * 512:(blk + 1) * 512],
                                                    ub[:, blk * 512:(blk + 1) * 512],
                                                    pp[:], ALU.mult))

            for n in range(16):
                for ch in range(T // CH):
                    @block.sync
                    def _(sync, bb=bb, n=n, ch=ch):
                        rw.wt(sync)
                        yy.wt(sync)
                        vv.wt(sync)
                        mm.wt(sync)
                        for rr in range(2):
                            base_r = (2 * ch + rr) * 33
                            csl2 = slice(rr * 2048, (rr + 1) * 2048)
                            cbb = slice(bb * 2048, (bb + 1) * 2048)
                            rw.inc(sync.dma_start(out=brow[0:1, csl2],
                                                  in_=bcd[base_r + n:base_r + n + 1, cbb]), 16)
                            rw.inc(sync.dma_start(out=crow[0:1, csl2],
                                                  in_=bcd[base_r + 16 + n:base_r + 17 + n, cbb]), 16)
                            rw.inc(sync.dma_start(out=drow[0:1, csl2],
                                                  in_=bcd[base_r + 32:base_r + 33, cbb]), 16)

                    for q in range(NQ):
                        blk = ch * NQ + q
                        pp = ps_k[q % 2]

                        @block.tensor
                        def _(tensor, blk=blk, q=q, pp=pp):
                            rw.wt(tensor)
                            vv.wt(tensor)
                            sc.wt(tensor)
                            mm.inc(tensor.matmul(pp[:], ones_row[:],
                                                 drow[0:1, q * 512:(q + 1) * 512],
                                                 start=True, stop=True))

                        @block.scalar
                        def _(scalar, q=q, pp=pp, n=n):
                            mm.wt(scalar)
                            sc.inc(scalar.activation(ablk[:, q * 512:(q + 1) * 512], pp[:],
                                                     AF.Exp, scale=Acol[:, n:n + 1]))

                        @block.tensor
                        def _(tensor, q=q, pp=pp):
                            sc.wt(tensor)
                            mm.inc(tensor.matmul(pp[:], ones_row[:],
                                                 brow[0:1, q * 512:(q + 1) * 512],
                                                 start=True, stop=True))

                        @block.vector
                        def _(vector, blk=blk, q=q, pp=pp):
                            mm.wt(vector)
                            vv.inc(vector.tensor_tensor(bublk[:, q * 512:(q + 1) * 512],
                                                        utl[:, blk * 512:(blk + 1) * 512],
                                                        pp[:], ALU.mult))

                    @block.vector
                    def _(vector, ch=ch):
                        vv.wt(vector)
                        sc.wt(vector)
                        init = 0.0 if ch == 0 else stcol[:, 0:1]
                        vv.inc(vector.tensor_tensor_scan(hblk[:], ablk[:], bublk[:],
                                                         init, ALU.mult, ALU.add))
                        vv.wt(vector)
                        vv.inc(vector.tensor_copy(stcol[:, 0:1], hblk[:, CH - 1:CH]))

                    for q in range(NQ):
                        blk = ch * NQ + q
                        pp = ps_k[q % 2]

                        @block.tensor
                        def _(tensor, blk=blk, q=q, pp=pp):
                            vv.wt(tensor)
                            yy.wt(tensor)
                            mm.inc(tensor.matmul(pp[:], ones_row[:],
                                                 crow[0:1, q * 512:(q + 1) * 512],
                                                 start=True, stop=True))

                        @block.vector
                        def _(vector, blk=blk, q=q, pp=pp, n=n):
                            mm.wt(vector)
                            yy.inc(vector.tensor_tensor(yterm[:],
                                                        hblk[:, q * 512:(q + 1) * 512],
                                                        pp[:], ALU.mult))
                            yy.wt(vector)
                            if n == 0:
                                yy.inc(vector.tensor_copy(ycur[:, blk * 512:(blk + 1) * 512],
                                                          yterm[:]))
                            else:
                                yy.inc(vector.tensor_tensor(ycur[:, blk * 512:(blk + 1) * 512],
                                                            ycur[:, blk * 512:(blk + 1) * 512],
                                                            yterm[:], ALU.add))

            for blk in range(T // 512):
                @block.vector
                def _(vector, blk=blk):
                    yy.wt(vector)
                    yy.inc(vector.scalar_tensor_tensor(
                        ycur[:, blk * 512:(blk + 1) * 512],
                        ub[:, blk * 512:(blk + 1) * 512], dpar[:, 0:1],
                        ycur[:, blk * 512:(blk + 1) * 512], ALU.mult, ALU.add))

            @block.sync
            def _(sync, bb=bb):
                yy.wt(sync)
                sto.inc(sync.dma_start(out=y_out[:, bb], in_=ycur[:]), 16)

        @block.sync
        def _(sync):
            sto.wt(sync)
    return b


def build_l2attn_c(b=None):
    if b is None:
        b = B()
    b.prefix = "l2attn_c_"
    nc = b.nc
    resid = b.din("resid", [ROWS, D])
    ytile = b.din("ytile", [8, 128, 4096], BF16)
    zscr = b.din("zscr", [8, 128, 4096], BF16)
    woutt = b.din("woutt", [D, D])
    identf = b.din("identf", [128, 128])
    out = b.dout("out", [ROWS, D])

    with b.seg():
        block = b.block_open()
        skel = Skel(b, block, resid, out)
        ldg = b.ch("ldg"); mm = b.ch("mm"); ev = b.ch("ev"); gg = b.ch("gg")
        skel.consts(identf)

        wout = b.sb([128, KD * D], BF16)
        y_sb = b.sb([128, KD * 4096], BF16)
        ofm = b.sb([128, KD * TT])
        ps_o = (b.psum([128, TT]), b.psum([128, TT]))

        @block.gpsimd
        def _(g):
            for c in range(KD):
                ldg.inc(g.dma_start(out=wout[:, c * D:(c + 1) * D],
                                    in_=woutt[c * 128:(c + 1) * 128, :]), 16)
            for c in range(KD):
                ldg.inc(g.dma_start(out=y_sb[:, c * 4096:(c + 1) * 4096],
                                    in_=ytile[c]), 16)

        zbuf = b.sb([128, 4096], BF16)
        zl = b.ch("zl")
        for c in range(KD):
            @block.sync
            def _(sync, c=c):
                gg.wt(sync)
                zl.wt(sync)
                zl.inc(sync.dma_start(out=zbuf[:], in_=zscr[c]), 16)

            @block.scalar
            def _(scalar, c=c):
                zl.wt(scalar)
                ldg.wt(scalar)
                gg.inc(scalar.activation(zbuf[:], zbuf[:], AF.Silu))

            @block.vector
            def _(vector, c=c):
                gg.wt(vector)
                gg.inc(vector.tensor_tensor(y_sb[:, c * 4096:(c + 1) * 4096],
                                            y_sb[:, c * 4096:(c + 1) * 4096],
                                            zbuf[:], ALU.mult))

        for t in range(NT):
            skel.load_rows(t, war_chs=(skel.ad,) if t else (mm, gg))
            for o in range(KD):
                pp = ps_o[o % 2]

                @block.tensor
                def _(tensor, o=o, pp=pp, t=t):
                    if o < 2 and t == 0:
                        gg.wt(tensor)
                        ldg.wt(tensor)
                    elif o < 2:
                        ev.wt(tensor)
                    else:
                        ev.wt(tensor, ev.n - 1)
                    for c in range(KD):
                        ins = tensor.matmul(pp[:], wout[:, c * D + o * 128: c * D + (o + 1) * 128],
                                            y_sb[:, c * 4096 + t * TT: c * 4096 + (t + 1) * TT],
                                            start=(c == 0), stop=(c == KD - 1))
                    mm.inc(ins)

                @block.vector
                def _(vector, o=o, pp=pp):
                    mm.wt(vector)
                    ev.inc(vector.tensor_copy(ofm[:, o * TT:(o + 1) * TT], pp[:]))

            skel.epilogue(t, ofm, ev)
        skel.finish()
    return b


def l2a_tails_from_parts(parts):
    tails = np.zeros((2048, D), np.float32)
    for r in range(NCORES):
        tails[r * 256:r * 256 + 128] = parts[r][1920:2048]
        tails[r * 256 + 128:r * 256 + 256] = parts[r][2048 + 1920:4096]
    return tails


def inmaps_l2attn_a(P, resid_full):
    parts = shard_rows(resid_full)
    tails = l2a_tails_from_parts(parts)
    base = {
        "wintile": P["L2_attn_W_in"],
        "wxp": P["L2_attn_W_xproj"],
        "convw": np.ascontiguousarray(P["L2_attn_conv_w"].reshape(8, 128, 1, 4)[:, :, 0, :]),
        "convb": P["L2_attn_conv_b"].reshape(8, 128),
        "identf": np.eye(128, dtype=np.float32),
        "tails": tails,
    }
    ims = []
    for c in range(NCORES):
        sm = np.zeros((128, 8), np.float32)
        if c > 0:
            sm[:, c - 1] = 1.0
        ims.append(dict(base, resid=parts[c], selmask=sm))
    return ims


def reshard_l2_u(uscr_parts):
    outs = []
    for j in range(NCORES):
        outs.append(np.ascontiguousarray(
            np.stack([uscr_parts[r][j] for r in range(NCORES)], 0)))
    return outs


def assemble_l2_bcd(bcd_parts):
    return np.ascontiguousarray(np.concatenate(bcd_parts, 0))


def reshard_l2_y(y_parts):
    import ml_dtypes
    Y = np.concatenate([y_parts[j] for j in range(NCORES)], 0)
    outs = []
    for c in range(NCORES):
        t = np.zeros((8, 128, 4096), dtype=Y.dtype)
        for ch in range(8):
            t[ch][:, 0:2048] = Y[ch * 128:(ch + 1) * 128, 0, c * 2048:(c + 1) * 2048]
            t[ch][:, 2048:4096] = Y[ch * 128:(ch + 1) * 128, 1, c * 2048:(c + 1) * 2048]
        outs.append(np.ascontiguousarray(t))
    return outs


def build_l0attn_a(b=None):
    if b is None:
        b = B()
    b.prefix = "l0attn_a_"
    nc = b.nc
    resid = b.din("resid", [2 * HB, D])
    wq = b.din("wq", [D, 512]); wk = b.din("wk", [D, 512])
    wv = b.din("wv", [D, 512]); wsp = b.din("wsp", [D, 512])
    wau = b.din("wau", [D, 64]); wad = b.din("wad", [64, 512])
    wbeta = b.din("wbeta", [D, 8])
    wu1 = b.din("wu1", [D, 64]); wu2 = b.din("wu2", [64, D])
    ginw = b.din("ginw", [8, 128])
    convw = b.din("convw", [8, 128, 4])
    convb = b.din("convb", [8, 128])
    bau = b.din("bau", [64, 1]); bad = b.din("bad", [4, 128])
    bbeta = b.din("bbeta", [8, 1])
    bu1 = b.din("bu1", [64, 1]); bu2 = b.din("bu2", [8, 128])
    identf = b.din("identf", [128, 128])
    qs = b.dout("qs", [8, 128, 2048]); ks = b.dout("ks", [8, 128, 2048])
    vs = b.dout("vs", [8, 128, 2048]); ds = b.dout("ds", [8, 128, 2048])
    aas = b.dout("aas", [8, 128, 2048])
    bbs = b.dout("bbs", [16, 2048])
    gates = b.dout("gates", [8, 128, 4096], BF16)

    with b.seg():
        block = b.block_open()
        skel = Skel(b, block, resid, None)
        ldg = b.ch("ldg"); lds = b.ch("lds"); mm = b.ch("mm"); ev = b.ch("ev")
        cv = b.ch("cv"); pj = b.ch("pj"); ust = b.ch("ust"); ga = b.ch("ga")
        skel.consts(identf)

        wqb = b.sb([128, KD * 512], BF16); wkb = b.sb([128, KD * 512], BF16)
        wvb = b.sb([128, KD * 512], BF16); wspb = b.sb([128, KD * 512], BF16)
        waub = b.sb([128, KD * 64], BF16); wadb = b.sb([64, 512], BF16)
        wbetab = b.sb([128, KD * 8], BF16)
        wu1b = b.sb([128, KD * 64], BF16); wu2b = b.sb([64, D], BF16)
        gin = b.sb([128, 8])
        cwv = b.sb([128, 8 * 4]); cbv = b.sb([128, 8])
        baus = b.sb([64, 1]); bads = b.sb([128, 4])
        bbetas = b.sb([8, 1]); bu1s = b.sb([64, 1]); bu2s = b.sb([128, 8])

        @block.gpsimd
        def _(g):
            for c in range(KD):
                ldg.inc(g.dma_start(out=wqb[:, c * 512:(c + 1) * 512], in_=wq[c * 128:(c + 1) * 128, :]), 16)
                ldg.inc(g.dma_start(out=wkb[:, c * 512:(c + 1) * 512], in_=wk[c * 128:(c + 1) * 128, :]), 16)
                ldg.inc(g.dma_start(out=wvb[:, c * 512:(c + 1) * 512], in_=wv[c * 128:(c + 1) * 128, :]), 16)
                ldg.inc(g.dma_start(out=wspb[:, c * 512:(c + 1) * 512], in_=wsp[c * 128:(c + 1) * 128, :]), 16)
                ldg.inc(g.dma_start(out=waub[:, c * 64:(c + 1) * 64], in_=wau[c * 128:(c + 1) * 128, :]), 16)
                ldg.inc(g.dma_start(out=wbetab[:, c * 8:(c + 1) * 8], in_=wbeta[c * 128:(c + 1) * 128, :]), 16)
                ldg.inc(g.dma_start(out=wu1b[:, c * 64:(c + 1) * 64], in_=wu1[c * 128:(c + 1) * 128, :]), 16)
            ldg.inc(g.dma_start(out=wadb[:], in_=wad), 16)
            ldg.inc(g.dma_start(out=wu2b[:], in_=wu2), 16)

        @block.sync
        def _(sync):
            lds.wt(sync)
            lds.inc(sync.dma_start(out=cwv.rearrange("p (c j) -> p c j", j=4),
                                   in_=convw.rearrange("c p j -> p c j")), 16)
            lds.inc(sync.dma_start(out=baus[:], in_=bau), 16)
            lds.inc(sync.dma_start(out=bbetas[:], in_=bbeta), 16)
            lds.inc(sync.dma_start(out=bu1s[:], in_=bu1), 16)

        eps6 = b.sb([128, 1])

        @block.vector
        def _(vector):
            skel.aux.inc(vector.memset(eps6[:], 1e-6))

        emit_vec_load(b, block, gin, ginw, 1024, skel, skel.ps_tf[0], lds)
        emit_vec_load(b, block, cbv, convb, 1024, skel, skel.ps_tf[0], lds)
        emit_vec_load(b, block, bads, bad, 512, skel, skel.ps_tf[0], lds)
        emit_vec_load(b, block, bu2s, bu2, 1024, skel, skel.ps_tf[0], lds)

        hm = b.sb([128, KD * 2 * HB], BF16)
        xsub = b.sb([128, KD * 128], BF16)
        evg = b.sb([128, 8 * 128], BF16)
        xr1 = b.sb([128, D])
        hnb = b.sb([128, D], BF16)
        au_sb = b.sb([64, 512], BF16)
        gu_sb = b.sb([64, 512], BF16)
        bb_sb = b.sb([8, 4096])
        ps_mm = (b.psum([128, 512]), b.psum([128, 512]))

        for s in range(34):
            h_, k = divmod(s, 17)

            @block.sync
            def _(sync, s=s, h_=h_, k=k):
                if s == 0:
                    mm.wt(sync); ldg.wt(sync); lds.wt(sync)
                else:
                    skel.ln.wt(sync)
                rows = resid[(h_ * HB + k * 128):(h_ * HB + (k + 1) * 128), :]
                skel.ldx.inc(sync.dma_start(out=xr1[:], in_=rows), 16)

            if s >= 1:
                @block.vector
                def _(vector):
                    skel.tr.wt(vector)
                    skel.ln.inc(vector.engine_nop())
            emit_ln_subtile(b, block, xr1[:], skel.xnb[:], skel.sq,
                            skel.stat, skel.ldx, skel.ln, eps_ap=skel.epsb[:])

            @block.vector
            def _(vector):
                skel.ln.wt(vector)
                vector.tensor_scalar(skel.stat[:, 6:7], skel.stat[:, 4:5], 1e-5, None, ALU.add)
                skel.ln.inc(vector.engine_nop())
                skel.ln.wt(vector)
                skel.ln.inc(vector.reciprocal(skel.stat[:, 7:8], skel.stat[:, 6:7]))
                skel.ln.wt(vector)
                skel.ln.inc(vector.tensor_tensor(skel.stat[:, 8:9], skel.stat[:, 4:5],
                                                 skel.stat[:, 7:8], ALU.mult))

            @block.scalar
            def _(scalar):
                skel.ln.wt(scalar)
                skel.ln.inc(scalar.activation(skel.stat[:, 9:10], skel.stat[:, 8:9], AF.Ln, bias=eps6[:, 0:1]))
                skel.ln.wt(scalar)
                skel.ln.inc(scalar.activation(skel.stat[:, 9:10], skel.stat[:, 9:10],
                                              AF.Exp, scale=-0.5))

            @block.vector
            def _(vector):
                skel.ln.wt(vector)
                skel.ln.inc(vector.tensor_scalar(hnb[:], skel.xnb[:], skel.stat[:, 9:10],
                                                 None, ALU.mult))

            if k >= 1:
                @block.vector
                def _(vector):
                    ga.wt(vector)
                    skel.fmc.inc(vector.engine_nop())

                def dstx(c):
                    return xsub[:, c * 128:(c + 1) * 128]
                emit_tm_to_fm(b, block, skel.xnb[:], dstx, skel.identb[:], skel.ps_t,
                              skel.tr, skel.fmc, ready_ch=skel.ln)

                @block.tensor
                def _(tensor, s=s):
                    skel.fmc.wt(tensor)
                    ga.wt(tensor)
                    ev.wt(tensor)
                    for c in range(KD):
                        ins = tensor.matmul(ps_mm[0][0:64, 0:128], wu1b[:, c * 64:(c + 1) * 64],
                                            xsub[:, c * 128:(c + 1) * 128],
                                            start=(c == 0), stop=(c == KD - 1))
                    mm.inc(ins)

                @block.scalar
                def _(scalar):
                    mm.wt(scalar)
                    ga.inc(scalar.activation(gu_sb[:, 0:128], ps_mm[0][0:64, 0:128],
                                             AF.Silu, bias=bu1s[:, 0:1]))

                for o in range(KD):
                    @block.tensor
                    def _(tensor, o=o):
                        ga.wt(tensor)
                        mm.inc(tensor.matmul(ps_mm[1][:, 0:128], wu2b[:, o * 128:(o + 1) * 128],
                                             gu_sb[:, 0:128], start=True, stop=True))

                    @block.scalar
                    def _(scalar, o=o):
                        mm.wt(scalar)
                        ust.wt(scalar)
                        ga.inc(scalar.activation(evg[:, o * 128:(o + 1) * 128],
                                                 ps_mm[1][:, 0:128], AF.Sigmoid,
                                                 bias=bu2s[:, o:o + 1]))

                @block.gpsimd
                def _(g, h_=h_, k=k):
                    ga.wt(g)
                    col = h_ * 2048 + (k - 1) * 128
                    ust.inc(g.dma_start(
                        out=gates[:, :, col:col + 128].rearrange("c p q -> p c q"),
                        in_=evg.rearrange("p (c q) -> p c q", q=128)), 16)

            for c in range(KD):
                pt = skel.ps_t[c % 2]

                @block.tensor
                def _(tensor, c=c, pt=pt):
                    if c == 0:
                        skel.ln.wt(tensor)
                        if skel.fmc.n > 0:
                            skel.fmc.wt(tensor)
                    elif c >= 2:
                        skel.fmc.wt(tensor, skel.fmc.n - 1)
                    skel.tr.inc(tensor.transpose(pt[:], hnb[:, c * 128:(c + 1) * 128],
                                                 skel.identb[:]))

                @block.vector
                def _(vector, c=c, s=s, h_=h_, k=k, pt=pt):
                    skel.tr.wt(vector)
                    dst = hm[:, c * 2 * HB + h_ * HB + k * 128: c * 2 * HB + h_ * HB + (k + 1) * 128]
                    skel.fmc.inc(vector.tensor_scalar(dst, pt[:], gin[:, c:c + 1],
                                                      None, ALU.mult))

        conv_scope = ExitStack()
        cacc = conv_scope.enter_context(nc.sbuf_tensor("cacc", [128, 2048], F32))
        cacc2 = conv_scope.enter_context(nc.sbuf_tensor("cacc2", [128, 2048], F32))
        for c in range(KD):
            for h_ in range(2):
                base = c * 2 * HB + h_ * HB + 128

                @block.vector
                def _(vector, c=c, h_=h_, base=base):
                    skel.fmc.wt(vector)
                    cv.wt(vector)
                    cv.inc(vector.tensor_scalar(cacc[:], hm[:, base - 3: base + 2045],
                                                cwv[:, c * 4:c * 4 + 1], None, ALU.mult))
                    cv.wt(vector)
                    cv.inc(vector.scalar_tensor_tensor(cacc2[:], hm[:, base - 2: base + 2046],
                                                       cwv[:, c * 4 + 1:c * 4 + 2], cacc[:],
                                                       ALU.mult, ALU.add))
                    cv.wt(vector)
                    cv.inc(vector.scalar_tensor_tensor(cacc[:], hm[:, base - 1: base + 2047],
                                                       cwv[:, c * 4 + 2:c * 4 + 3], cacc2[:],
                                                       ALU.mult, ALU.add))
                    cv.wt(vector)
                    cv.inc(vector.scalar_tensor_tensor(cacc2[:], hm[:, base: base + 2048],
                                                       cwv[:, c * 4 + 3:c * 4 + 4], cacc[:],
                                                       ALU.mult, ALU.add))

                @block.scalar
                def _(scalar, c=c, h_=h_, base=base):
                    cv.wt(scalar)
                    cv.inc(scalar.activation(hm[:, base: base + 2048],
                                             cacc2[:], AF.Silu, bias=cbv[:, c:c + 1]))

        conv_scope.close()
        evq = b.sb([128, 4 * 512])
        eva = b.sb([128, 512])
        evg = b.sb([128, 512], BF16)

        def hslice(c, tau):
            hh = tau // 4
            return hm[:, c * 2 * HB + hh * HB + 128 + (tau % 4) * 512:
                      c * 2 * HB + hh * HB + 128 + (tau % 4) * 512 + 512]

        for tau in range(8):
            for (wt_, dst, nm) in ((wqb, qs, "q"), (wkb, ks, "k"), (wvb, vs, "v"), (wspb, ds, "d")):
                for o in range(4):
                    pp = ps_mm[(tau * 4 + o) % 2]

                    @block.tensor
                    def _(tensor, tau=tau, o=o, wt_=wt_, pp=pp):
                        cv.wt(tensor)
                        ga.wt(tensor)
                        ev.wt(tensor, max(0, ev.n - 1))
                        for c in range(KD):
                            ins = tensor.matmul(pp[:], wt_[:, c * 512 + o * 128: c * 512 + (o + 1) * 128],
                                                hslice(c, tau), start=(c == 0), stop=(c == KD - 1))
                        mm.inc(ins)

                    @block.vector
                    def _(vector, tau=tau, o=o, dst=dst, pp=pp):
                        mm.wt(vector)
                        ust.wt(vector)
                        ev.inc(vector.tensor_copy(evq[:, (o % 4) * 512:(o % 4 + 1) * 512], pp[:]))

                    @block.gpsimd
                    def _(g, tau=tau, o=o, dst=dst):
                        ev.wt(g)
                        shard = (tau // 4) * 4 + o
                        ust.inc(g.dma_start(out=dst[shard, :, (tau % 4) * 512:(tau % 4 + 1) * 512],
                                            in_=evq[:, (o % 4) * 512:(o % 4 + 1) * 512]), 16)

            @block.tensor
            def _(tensor, tau=tau):
                cv.wt(tensor)
                ga.wt(tensor)
                ev.wt(tensor)
                for c in range(KD):
                    ins = tensor.matmul(ps_mm[0][0:64, :], waub[:, c * 64:(c + 1) * 64],
                                        hslice(c, tau), start=(c == 0), stop=(c == KD - 1))
                mm.inc(ins)

            @block.scalar
            def _(scalar, tau=tau):
                mm.wt(scalar)
                ga.inc(scalar.activation(au_sb[:], ps_mm[0][0:64, :], AF.Silu, bias=baus[:, 0:1]))

            for o in range(4):
                @block.tensor
                def _(tensor, tau=tau, o=o):
                    ga.wt(tensor)
                    mm.inc(tensor.matmul(ps_mm[1][:, 0:512], wadb[:, o * 128:(o + 1) * 128],
                                         au_sb[:], start=True, stop=True))

                @block.vector
                def _(vector, tau=tau, o=o):
                    mm.wt(vector)
                    ust.wt(vector)
                    ga.inc(vector.tensor_scalar(eva[:], ps_mm[1][:, 0:512],
                                                bads[:, o:o + 1], None, ALU.add))

                @block.gpsimd
                def _(g, tau=tau, o=o):
                    ga.wt(g)
                    shard = (tau // 4) * 4 + o
                    ust.inc(g.dma_start(out=aas[shard, :, (tau % 4) * 512:(tau % 4 + 1) * 512],
                                        in_=eva[:]), 16)

            @block.tensor
            def _(tensor, tau=tau):
                cv.wt(tensor)
                ga.wt(tensor)
                ev.wt(tensor)
                for c in range(KD):
                    ins = tensor.matmul(ps_mm[0][0:8, :], wbetab[:, c * 8:(c + 1) * 8],
                                        hslice(c, tau), start=(c == 0), stop=(c == KD - 1))
                mm.inc(ins)

            @block.vector
            def _(vector, tau=tau):
                mm.wt(vector)
                ga.inc(vector.tensor_scalar(bb_sb[:, tau * 512:(tau + 1) * 512],
                                            ps_mm[0][0:8, :], bbetas[:, 0:1], None, ALU.add))

        @block.sync
        def _(sync):
            ga.wt(sync)
            skel.sto.inc(sync.dma_start(
                out=bbs.rearrange("(h i) t -> i h t", h=2),
                in_=bb_sb.rearrange("i (h t) -> i h t", h=2)), 16)
        skel.finish()

        @block.gpsimd
        def _(g):
            ust.wt(g)
    return b


def build_l0attn_b(b=None):
    if b is None:
        b = B()
    b.prefix = "l0attn_b_"
    nc = b.nc
    q_in = b.din("q_in", [8, 128, 2048]).rearrange("r p t -> p r t")
    k_in = b.din("k_in", [8, 128, 2048]).rearrange("r p t -> p r t")
    v_in = b.din("v_in", [8, 128, 2048]).rearrange("r p t -> p r t")
    d_in = b.din("d_in", [8, 128, 2048]).rearrange("r p t -> p r t")
    aa_in = b.din("aa_in", [8, 128, 2048]).rearrange("r p t -> p r t")
    bb_in = b.din("bb_in", [16, 2048]).rearrange("(r i) t -> r i t", i=2)
    asp_col = b.din("asp_col", [128, 1])
    bsppat = b.din("bsppat", [128, 2])
    pat2 = b.din("pat2", [128, 2])
    pat2t = b.din("pat2t", [2, 128])
    masku = b.din("masku", [128, 64])
    maskl = b.din("maskl", [128, 64])
    maskui = b.din("maskui", [128, 64])
    identf = b.din("identf", [128, 128])
    ys = b.dout("ys", [128, 16384])
    betad = b.dtmp("betad", [128, 16384], BF16)

    T = 16384
    W = 256
    BL = 2048
    NB = T // BL
    CB = W + BL
    with b.seg():
        block = b.block_open()
        ld = b.ch("ld"); ldr = b.ch("ldr"); mm = b.ch("mm")
        vv = b.ch("vv"); sc = b.ch("sc"); yy = b.ch("yy"); sto = b.ch("sto")
        bst = b.ch("bst"); ldv = b.ch("ldv"); ldvg = b.ch("ldvg"); wyc = b.ch("wyc")
        ldw2 = b.ch("ldw2")

        lam = b.sb([128, T])
        aspc = b.sb([128, 1]); bspp = b.sb([128, 2])
        p2 = b.sb([128, 2]); p2t = b.sb([2, 128])
        msku = b.sb([128, 64]); mskui = b.sb([128, 64]); mskl = b.sb([128, 64])
        idf = b.sb([128, 128])
        identb2 = b.sb([128, 128], BF16)
        ones64 = b.sb([128, 64])
        rowt = b.sb([2, 512]); rowt2 = b.sb([2, 512])
        bchunk = b.sb([128, 512], BF16)

        @block.sync
        def _(sync):
            ld.inc(sync.dma_start(out=aspc[:], in_=asp_col), 16)
            ld.inc(sync.dma_start(out=bspp[:], in_=bsppat), 16)
            ld.inc(sync.dma_start(out=msku[:], in_=masku), 16)
            ld.inc(sync.dma_start(out=mskl[:], in_=maskl), 16)
            ld.inc(sync.dma_start(out=mskui[:], in_=maskui), 16)
            ld.inc(sync.dma_start(out=idf[:], in_=identf), 16)
            ld.inc(sync.dma_start(out=p2[:], in_=pat2), 16)
            ld.inc(sync.dma_start(out=p2t[:], in_=pat2t), 16)

        p2b = b.sb([128, 2], BF16)
        bsppb = b.sb([128, 2], BF16)

        @block.vector
        def _(vector):
            ld.wt(vector)
            vv.inc(vector.memset(ones64[:], 1.0))
            vv.wt(vector)
            vv.inc(vector.tensor_copy(identb2[:], idf[:]))
            vv.wt(vector)
            vv.inc(vector.tensor_copy(p2b[:], p2[:]))
            vv.wt(vector)
            vv.inc(vector.tensor_copy(bsppb[:], bspp[:]))

        ps_scope = ExitStack()
        ps_a = ps_scope.enter_context(nc.psum_tensor("ps_a", [128, 512], F32))
        ps_b = ps_scope.enter_context(nc.psum_tensor("ps_b", [128, 512], F32))
        ps_c = ps_scope.enter_context(nc.psum_tensor("ps_c", [128, 512], F32))
        ps_d = ps_scope.enter_context(nc.psum_tensor("ps_d", [128, 512], F32))

        with ExitStack() as s1:
            drw = s1.enter_context(nc.sbuf_tensor("drw", [128, NB * CB], F32))
            spk = s1.enter_context(nc.sbuf_tensor("spk", [128, T], BF16))
            mem = s1.enter_context(nc.sbuf_tensor("mem", [128, 16], F32))
            spkpad = s1.enter_context(nc.sbuf_tensor("spkpad", [128, 8], BF16))
            abuf = s1.enter_context(nc.sbuf_tensor("abuf", [128, 2048], F32))
            sps = s1.enter_context(nc.sbuf_tensor("sps", [128, 512], F32))
            bbg = s1.enter_context(nc.sbuf_tensor("bbg", [2, 2048], F32))

            @block.sync
            def _(sync):
                for j in range(NB):
                    if j == 0:
                        ldw2.inc(sync.dma_start(out=drw[:, W:CB],
                                              in_=d_in[:, 0]), 16)
                    else:
                        ldw2.inc(sync.dma_start(out=drw[:, j * CB:j * CB + W],
                                              in_=d_in[:, j - 1, BL - W:BL]), 16)
                        ldw2.inc(sync.dma_start(out=drw[:, j * CB + W:(j + 1) * CB],
                                              in_=d_in[:, j]), 16)

            @block.vector
            def _(vector):
                vv.inc(vector.memset(drw[:, 0:W], 0.0))
                vv.inc(vector.memset(mem[:], 0.0))

            drw3 = drw.rearrange("p (j c) -> p j c", c=CB)
            spk3 = spk.rearrange("p (j c) -> p j c", c=BL)

            @block.vector
            def _(vector):
                ld.wt(vector)
                ldw2.wt(vector)
                vv.wt(vector)
                m0, m1 = mem[:, 0:8], mem[:, 8:16]
                for tau in range(CB):
                    vv.inc(vector.scalar_tensor_tensor(m1, m0, 0.9, drw3[:, :, tau],
                                                       ALU.mult, ALU.add))
                    vv.wt(vector)
                    dst = spk3[:, :, tau - W] if tau >= W else spkpad[:, 0:8]
                    vv.inc(vector.tensor_scalar(dst, m1, 0.5, None, ALU.is_gt))
                    vv.wt(vector)
                    vv.inc(vector.scalar_tensor_tensor(m0, dst, -0.5, m1, ALU.mult, ALU.add))
                    vv.wt(vector)

            for g8 in range(T // 2048):
                @block.sync
                def _(sync, g8=g8):
                    vv.wt(sync)
                    ldr.wt(sync)
                    ldr.inc(sync.dma_start(out=abuf[:], in_=aa_in[:, g8]), 16)
                    ldr.inc(sync.dma_start(out=bbg[:], in_=bb_in[g8]), 16)

                for q4 in range(4):
                    blk = g8 * 4 + q4
                    cs = slice(blk * 512, (blk + 1) * 512)
                    csl = slice(q4 * 512, (q4 + 1) * 512)

                    @block.tensor
                    def _(tensor, cs=cs):
                        ldr.wt(tensor)
                        vv.wt(tensor)
                        sc.wt(tensor)
                        mm.inc(tensor.matmul(ps_a[0:2, :], p2b[:], spk[:, cs], start=True, stop=True))
                        mm.inc(tensor.matmul(ps_c[0:2, :], bsppb[:], spk[:, cs], start=True, stop=True))

                    @block.vector
                    def _(vector, csl=csl):
                        mm.wt(vector)
                        vv.inc(vector.tensor_scalar(rowt[:], ps_a[0:2, :], 0.0, None, ALU.is_gt))
                        vv.wt(vector)
                        vv.inc(vector.tensor_tensor(rowt2[:], ps_c[0:2, :], bbg[:, csl], ALU.add))

                    @block.scalar
                    def _(scalar):
                        vv.wt(scalar)
                        sc.inc(scalar.activation(rowt2[:], rowt2[:], AF.Sigmoid))

                    @block.vector
                    def _(vector):
                        sc.wt(vector)
                        vv.inc(vector.tensor_tensor(rowt2[:], rowt2[:], rowt[:], ALU.mult))

                    @block.tensor
                    def _(tensor):
                        vv.wt(tensor)
                        mm.inc(tensor.matmul(ps_b[:], p2t[:], rowt[:], start=True, stop=True))
                        mm.inc(tensor.matmul(ps_d[:], p2t[:], rowt2[:], start=True, stop=True))

                    @block.vector
                    def _(vector, cs=cs, csl=csl):
                        vv.wt(vector)
                        vv.inc(vector.scalar_tensor_tensor(lam[:, cs], spk[:, cs], aspc[:, 0:1],
                                                           abuf[:, csl], ALU.mult, ALU.add))

                    @block.scalar
                    def _(scalar, cs=cs):
                        vv.wt(scalar)
                        sc.inc(scalar.activation(sps[:], lam[:, cs], AF.Abs))
                        sc.wt(scalar)
                        sc.inc(scalar.activation(sps[:], sps[:], AF.Exp, scale=-1.0))
                        sc.wt(scalar)
                        sc.inc(scalar.activation(sps[:], sps[:], AF.Ln, bias=1.0))

                    @block.vector
                    def _(vector, cs=cs):
                        sc.wt(vector)
                        vv.inc(vector.tensor_scalar(lam[:, cs], lam[:, cs], 0.0, None, ALU.min))
                        vv.wt(vector)
                        vv.inc(vector.tensor_tensor(lam[:, cs], lam[:, cs], sps[:], ALU.subtract))
                        vv.wt(vector)
                        mm.wt(vector)
                        vv.inc(vector.tensor_tensor(lam[:, cs], lam[:, cs], ps_b[:], ALU.mult))
                        vv.wt(vector)
                        bst.wt(vector)
                        vv.inc(vector.tensor_copy(bchunk[:], ps_d[:]))

                    @block.gpsimd
                    def _(g, cs=cs):
                        vv.wt(g)
                        bst.inc(g.dma_start(out=betad[:, cs], in_=bchunk[:]), 16)

        qn = b.sb([128, T], BF16)
        kn = b.sb([128, T], BF16)
        qld = b.sb([128, 512])
        qsq = b.sb([128, 512])

        for (src_ap, dstn) in ((q_in, qn), (k_in, kn)):
            for blk in range(T // 512):
                cs = slice(blk * 512, (blk + 1) * 512)

                @block.sync
                def _(sync, src_ap=src_ap, blk=blk):
                    vv.wt(sync)
                    ldr.wt(sync)
                    ldr.inc(sync.dma_start(out=qld[:], in_=src_ap[:, blk // 4, (blk % 4) * 512:(blk % 4 + 1) * 512]), 16)

                @block.vector
                def _(vector):
                    ldr.wt(vector)
                    vv.wt(vector)
                    vv.inc(vector.tensor_tensor(qsq[:], qld[:], qld[:], ALU.mult))

                @block.tensor
                def _(tensor):
                    vv.wt(tensor)
                    mm.wt(tensor)
                    mm.inc(tensor.matmul(ps_a[0:2, :], p2[:], qsq[:], start=True, stop=True))

                @block.scalar
                def _(scalar):
                    mm.wt(scalar)
                    sc.inc(scalar.activation(rowt[:], ps_a[0:2, :], AF.Sqrt))

                @block.vector
                def _(vector):
                    sc.wt(vector)
                    vv.inc(vector.tensor_scalar(rowt[:], rowt[:], 1e-6, None, ALU.add))
                    vv.wt(vector)
                    vv.inc(vector.reciprocal(rowt2[:], rowt[:]))

                @block.tensor
                def _(tensor):
                    vv.wt(tensor)
                    mm.inc(tensor.matmul(ps_b[:], p2t[:], rowt2[:], start=True, stop=True))

                @block.vector
                def _(vector, dstn=dstn, cs=cs):
                    mm.wt(vector)
                    vv.inc(vector.tensor_tensor(dstn[:, cs], qld[:], ps_b[:], ALU.mult))

        ps_scope.close()
        vb = (b.sb([128, 2048], BF16), b.sb([128, 2048], BF16))
        bbuf = (b.sb([128, 2048], BF16), b.sb([128, 2048], BF16))
        ystage = (b.sb([128, 2048]), b.sb([128, 2048]))
        lamc = b.sb([128, 64])
        colst = b.sb([128, 8])
        fe1 = b.sb([128, 64]); fe2 = b.sb([128, 64])
        fgam = b.sb([128, 64]); fgrat = b.sb([128, 64])
        ktb = b.sb([128, 64], BF16); k2b = b.sb([128, 64], BF16)
        qtb = b.sb([128, 64], BF16); q2b = b.sb([128, 64], BF16)
        bkb = b.sb([128, 64]); wbb = b.sb([128, 64], BF16); w2b_ = b.sb([128, 64], BF16)
        Lsb = b.sb([128, 64], BF16); LTsb = b.sb([128, 64], BF16)
        Psb = b.sb([128, 64], BF16); PTsb = b.sb([128, 64], BF16)
        Pnb = b.sb([128, 64], BF16); PnTb = b.sb([128, 64], BF16)
        MTsb = b.sb([128, 64], BF16)
        Uf = b.sb([128, 64]); Ub = b.sb([128, 64], BF16)
        Sf = b.sb([128, 64]); Sb = b.sb([128, 64], BF16)
        Vtm = b.sb([128, 64], BF16)
        Ysc = b.sb([128, 64])
        ps_w = (b.psum([128, 64]), b.psum([128, 64]), b.psum([128, 64]), b.psum([128, 64]))
        ps_wt = b.psum([128, 64], BF16)

        @block.vector
        def _(vector):
            vv.wt(vector)
            wyc.inc(vector.memset(Sf[:], 0.0))
            wyc.wt(vector)
            wyc.inc(vector.tensor_copy(Sb[:], Sf[:]))

        NCH = T // 64
        for c in range(NCH):
            g2 = c // 32
            if c % 32 == 0:
                @block.sync
                def _(sync, g2=g2, c=c):
                    ldv.wt(sync)
                    if c > 0:
                        yy.wt(sync)
                        wyc.wt(sync)
                    bst.wt(sync)
                    ldv.inc(sync.dma_start(out=bbuf[g2 % 2][:], in_=betad[:, g2 * 2048:(g2 + 1) * 2048]), 16)

                @block.gpsimd
                def _(g, g2=g2, c=c):
                    ldvg.wt(g)
                    if c > 0:
                        wyc.wt(g)
                    ldvg.inc(g.dma_start(out=vb[g2 % 2][:], in_=v_in[:, g2]), 16)

                @block.vector
                def _(vector, c=c):
                    if c > 0:
                        sto.wt(vector)
                    yy.inc(vector.engine_nop())

            co = (c % 32) * 64
            vchunk = vb[g2 % 2][:, co:co + 64]
            bchunk2 = bbuf[g2 % 2][:, co:co + 64]
            cs = slice(c * 64, (c + 1) * 64)

            @block.vector
            def _(vector, cs=cs):
                vv.wt(vector)
                wyc.wt(vector)
                sc.wt(vector)
                wyc.inc(vector.tensor_tensor_scan(lamc[:], ones64[:], lam[:, cs],
                                                  0.0, ALU.mult, ALU.add))
                wyc.wt(vector)
                wyc.inc(vector.tensor_scalar(colst[:, 0:1], lamc[:, 63:64], 0.5, None, ALU.mult))
                wyc.wt(vector)
                wyc.inc(vector.tensor_scalar(colst[:, 1:2], colst[:, 0:1], -1.0, None, ALU.mult))

            @block.scalar
            def _(scalar):
                wyc.wt(scalar)
                sc.inc(scalar.activation(fe1[:], lamc[:], AF.Exp, bias=colst[:, 1:2]))
                sc.wt(scalar)
                sc.inc(scalar.activation(fe2[:], lamc[:], AF.Exp, scale=-1.0, bias=colst[:, 0:1]))
                sc.wt(scalar)
                sc.inc(scalar.activation(fgam[:], lamc[:], AF.Exp))
                sc.wt(scalar)
                sc.inc(scalar.activation(fgrat[:], lamc[:], AF.Exp, scale=-1.0,
                                         bias=lamc[:, 63:64]))
                sc.wt(scalar)
                sc.inc(scalar.activation(colst[:, 2:3], lamc[:, 63:64], AF.Exp))

            @block.vector
            def _(vector, cs=cs, bchunk2=bchunk2):
                sc.wt(vector)
                ldv.wt(vector)
                ldvg.wt(vector)
                wyc.inc(vector.tensor_tensor(ktb[:], kn[:, cs], fe1[:], ALU.mult))
                wyc.wt(vector)
                wyc.inc(vector.tensor_tensor(k2b[:], kn[:, cs], fgam[:], ALU.mult))
                wyc.wt(vector)
                wyc.inc(vector.tensor_tensor(qtb[:], qn[:, cs], fe1[:], ALU.mult))
                wyc.wt(vector)
                wyc.inc(vector.tensor_tensor(q2b[:], qn[:, cs], fgam[:], ALU.mult))
                wyc.wt(vector)
                wyc.inc(vector.tensor_tensor(bkb[:], kn[:, cs], bchunk2, ALU.mult))
                wyc.wt(vector)
                wyc.inc(vector.tensor_tensor(wbb[:], bkb[:], fe2[:], ALU.mult))
                wyc.wt(vector)
                wyc.inc(vector.tensor_tensor(w2b_[:], bkb[:], fgrat[:], ALU.mult))

            for p in range(2):
                r0 = p * 64
                rs = slice(r0, r0 + 64)

                def MM(tensor, out_ps, lhsT, rhs, start=True, stop=True):
                    return tensor.matmul(out_ps, lhsT, rhs, start=start, stop=stop)

                @block.tensor
                def _(tensor, p=p, rs=rs, vchunk=vchunk):
                    wyc.wt(tensor)
                    yy.wt(tensor)
                    mm.inc(tensor.transpose(ps_wt[0:64, :], vchunk[rs, :],
                                            identb2[rs, p * 64:(p + 1) * 64]))

                @block.vector
                def _(vector, p=p, rs=rs):
                    mm.wt(vector)
                    wyc.inc(vector.tensor_copy(Vtm[rs, :], ps_wt[0:64, :]))

                @block.tensor
                def _(tensor, p=p, rs=rs):
                    wyc.wt(tensor)
                    mm.inc(MM(tensor, ps_w[0][rs, :], ktb[rs, :], wbb[rs, :]))
                    mm.inc(MM(tensor, ps_w[1][rs, :], wbb[rs, :], ktb[rs, :]))

                @block.vector
                def _(vector, p=p, rs=rs):
                    mm.wt(vector)
                    wyc.inc(vector.tensor_tensor(Lsb[rs, :], ps_w[0][rs, :],
                                                 mskl[rs, :], ALU.mult))
                    wyc.wt(vector)
                    wyc.inc(vector.tensor_tensor(LTsb[rs, :], ps_w[1][rs, :],
                                                 msku[rs, :], ALU.mult))

                @block.tensor
                def _(tensor, p=p, rs=rs):
                    wyc.wt(tensor)
                    mm.inc(MM(tensor, ps_w[2][rs, :], k2b[rs, :], Sb[rs, :]))

                @block.vector
                def _(vector, p=p, rs=rs):
                    mm.wt(vector)
                    wyc.inc(vector.tensor_tensor(Uf[rs, :], Vtm[rs, :], ps_w[2][rs, :],
                                                 ALU.subtract))
                    wyc.wt(vector)
                    wyc.inc(vector.tensor_copy(Ub[rs, :], Uf[rs, :]))

                @block.tensor
                def _(tensor, p=p, rs=rs):
                    wyc.wt(tensor)
                    mm.inc(MM(tensor, ps_w[3][rs, :], LTsb[rs, :], Ub[rs, :]))

                @block.vector
                def _(vector, p=p, rs=rs):
                    mm.wt(vector)
                    wyc.inc(vector.tensor_tensor(Uf[rs, :], Uf[rs, :], ps_w[3][rs, :],
                                                 ALU.subtract))
                    wyc.wt(vector)
                    wyc.inc(vector.tensor_copy(Ub[rs, :], Uf[rs, :]))
                    wyc.wt(vector)
                    wyc.inc(vector.tensor_copy(Psb[rs, :], Lsb[rs, :]))
                    wyc.wt(vector)
                    wyc.inc(vector.tensor_copy(PTsb[rs, :], LTsb[rs, :]))

                for lev in range(5):
                    @block.tensor
                    def _(tensor, p=p, rs=rs):
                        wyc.wt(tensor)
                        mm.inc(MM(tensor, ps_w[0][rs, :], PTsb[rs, :], Psb[rs, :]))
                        mm.inc(MM(tensor, ps_w[1][rs, :], Psb[rs, :], PTsb[rs, :]))

                    @block.vector
                    def _(vector, p=p, rs=rs):
                        mm.wt(vector)
                        wyc.inc(vector.tensor_copy(Pnb[rs, :], ps_w[0][rs, :]))
                        wyc.wt(vector)
                        wyc.inc(vector.tensor_copy(PnTb[rs, :], ps_w[1][rs, :]))

                    @block.tensor
                    def _(tensor, p=p, rs=rs):
                        wyc.wt(tensor)
                        mm.inc(MM(tensor, ps_w[2][rs, :], PnTb[rs, :], Ub[rs, :]))

                    @block.vector
                    def _(vector, p=p, rs=rs):
                        mm.wt(vector)
                        wyc.inc(vector.tensor_tensor(Uf[rs, :], Uf[rs, :], ps_w[2][rs, :],
                                                     ALU.add))
                        wyc.wt(vector)
                        wyc.inc(vector.tensor_copy(Ub[rs, :], Uf[rs, :]))
                        wyc.wt(vector)
                        wyc.inc(vector.tensor_copy(Psb[rs, :], Pnb[rs, :]))
                        wyc.wt(vector)
                        wyc.inc(vector.tensor_copy(PTsb[rs, :], PnTb[rs, :]))

                @block.tensor
                def _(tensor, p=p, rs=rs):
                    wyc.wt(tensor)
                    mm.inc(MM(tensor, ps_w[0][rs, :], wbb[rs, :], qtb[rs, :]))

                @block.vector
                def _(vector, p=p, rs=rs):
                    mm.wt(vector)
                    wyc.inc(vector.tensor_tensor(MTsb[rs, :], ps_w[0][rs, :],
                                                 mskui[rs, :], ALU.mult))

                @block.tensor
                def _(tensor, p=p, rs=rs):
                    wyc.wt(tensor)
                    MM(tensor, ps_w[1][rs, :], MTsb[rs, :], Ub[rs, :], start=True, stop=False)
                    mm.inc(MM(tensor, ps_w[1][rs, :], q2b[rs, :], Sb[rs, :], start=False, stop=True))

                @block.vector
                def _(vector, p=p, rs=rs):
                    mm.wt(vector)
                    yy.inc(vector.tensor_copy(Ysc[rs, :], ps_w[1][rs, :]))

                @block.tensor
                def _(tensor, p=p, rs=rs):
                    yy.wt(tensor)
                    mm.inc(tensor.transpose(ps_w[2][0:64, :], Ysc[rs, :],
                                            idf[rs, p * 64:(p + 1) * 64]))

                @block.vector
                def _(vector, p=p, rs=rs, g2=g2, co=co):
                    mm.wt(vector)
                    yy.inc(vector.tensor_copy(ystage[g2 % 2][rs, co:co + 64], ps_w[2][0:64, :]))

                @block.tensor
                def _(tensor, p=p, rs=rs):
                    wyc.wt(tensor)
                    mm.inc(tensor.transpose(ps_wt[0:64, :], w2b_[rs, :],
                                            identb2[rs, p * 64:(p + 1) * 64]))

                @block.vector
                def _(vector, p=p, rs=rs):
                    mm.wt(vector)
                    wyc.inc(vector.tensor_copy(Pnb[rs, :], ps_wt[0:64, :]))

                @block.tensor
                def _(tensor, p=p, rs=rs):
                    wyc.wt(tensor)
                    mm.inc(MM(tensor, ps_w[3][rs, :], Pnb[rs, :], Ub[rs, :]))

                @block.vector
                def _(vector, p=p, rs=rs):
                    mm.wt(vector)
                    wyc.inc(vector.scalar_tensor_tensor(Sf[rs, :], Sf[rs, :], colst[rs, 2:3],
                                                        ps_w[3][rs, :], ALU.mult, ALU.add))
                    wyc.wt(vector)
                    wyc.inc(vector.tensor_copy(Sb[rs, :], Sf[rs, :]))

            if c % 32 == 31:
                @block.sync
                def _(sync, g2=g2):
                    yy.wt(sync)
                    sto.inc(sync.dma_start(out=ys[:, g2 * 2048:(g2 + 1) * 2048],
                                           in_=ystage[g2 % 2][:]), 16)

        @block.sync
        def _(sync):
            sto.wt(sync)

        @block.gpsimd
        def _(g):
            bst.wt(g)
    return b


def build_l0attn_c(b=None):
    if b is None:
        b = B()
    b.prefix = "l0attn_c_"
    nc = b.nc
    resid = b.din("resid", [ROWS, D])
    ytile = b.din("ytile", [8, 128, 2048])
    gates = b.din("gates", [8, 128, 4096], BF16)
    wo = b.din("wo", [512, D])
    hnwv = b.din("hnwv", [4, 128])
    ffw = b.din("ffw", [8, 128])
    hpatr = b.din("hpatr", [4, 128, 8])
    hpatb = b.din("hpatb", [4, 8, 128])
    wf1t = b.din("wf1t", [8, 32, 128, 128])
    wf3t = b.din("wf3t", [8, 32, 128, 128])
    wf2t = b.din("wf2t", [32, 8, 128, 128])
    identf = b.din("identf", [128, 128])
    out = b.dout("out", [ROWS, D])
    wf1b = b.dtmp("wf1b", [8, 32, 128, 128], BF16)
    wf3b = b.dtmp("wf3b", [8, 32, 128, 128], BF16)
    wf2b = b.dtmp("wf2b", [32, 8, 128, 128], BF16)

    EPS2 = 1e-6
    with b.seg():
        block = b.block_open()
        skel = Skel(b, block, resid, out)
        ldg = b.ch("ldg"); lds = b.ch("lds"); mm = b.ch("mm"); ev = b.ch("ev")
        wp = b.ch("wp"); yl = b.ch("yl"); hv = b.ch("hv"); rz = b.ch("rz")
        chs = {"wldA": b.ch("wldA"), "wldB": b.ch("wldB"),
               "mm": b.ch("mmf"), "ev": b.ch("evf")}
        skel.consts(identf)

        @block.gpsimd
        def _(g):
            for (s_, d_) in ((wf1t, wf1b), (wf3t, wf3b), (wf2t, wf2b)):
                wp.inc(g.dma_start(out=d_.rearrange("a c p q -> (a c p) q"),
                                   in_=s_.rearrange("a c p q -> (a c p) q")), 16)

        wob = b.sb([128, 4 * D], BF16)
        hnw = b.sb([128, 4])
        ffg = b.sb([128, 8])
        eps2 = b.sb([128, 1])
        hpr = b.sb([128, 4 * 8])
        hpb = b.sb([8, 4 * 128])

        @block.gpsimd
        def _(g):
            for c in range(4):
                ldg.inc(g.dma_start(out=wob[:, c * D:(c + 1) * D],
                                    in_=wo[c * 128:(c + 1) * 128, :]), 16)
            ldg.inc(g.dma_start(out=hpr.rearrange("p (c x) -> p c x", x=8),
                                in_=hpatr.rearrange("c p x -> p c x")), 16)
            ldg.inc(g.dma_start(out=hpb.rearrange("p (c x) -> p c x", x=128),
                                in_=hpatb.rearrange("c p x -> p c x")), 16)

        emit_vec_load(b, block, hnw, hnwv, 512, skel, skel.ps_tf[0], lds)
        emit_vec_load(b, block, ffg, ffw, 1024, skel, skel.ps_tf[0], lds)

        @block.vector
        def _(vector):
            skel.aux.inc(vector.memset(eps2[:], EPS2))

        @block.vector
        def _(vector):
            ldg.wt(vector)
            lds.wt(vector)
            skel.aux.wt(vector)
            for c in range(4):
                yl.inc(vector.tensor_scalar(wob[:, c * D:(c + 1) * D],
                                            wob[:, c * D:(c + 1) * D],
                                            hnw[:, c:c + 1], None, ALU.mult))

        yfm = b.sb([128, 4 * TT])
        ysq = b.sb([128, 4 * TT])
        rrow = b.sb([8, TT])
        ynf = b.sb([128, 4 * TT], BF16)
        gbuf = b.sb([128, 8 * TT], BF16)
        y2 = b.sb([128, KD * TT])
        zsq = b.sb([128, TT])
        zrow = b.sb([1, TT])
        zfm = b.sb([128, KD * TT], BF16)
        hsw = b.sb([128, 32 * TT], BF16)
        ofm = b.sb([128, KD * TT])
        wbufs = (b.sb([128, 32 * 128], BF16), b.sb([128, 32 * 128], BF16))
        psums = (b.psum([128, TT]), b.psum([128, TT]))
        ps_h = b.psum([128, TT])
        ps_r = b.psum([1, TT])

        for t in range(NT):
            skel.load_rows(t, war_chs=(skel.ad,) if t else (chs["mm"], wp))
            skel.ln_fmize(t, mm_war_ch=chs["mm"])

            @block.sync
            def _(sync, t=t):
                yl.wt(sync)
                if t >= 1:
                    mm.wt(sync)
                    chs["mm"].wt(sync)
                    hv.wt(sync)
                for c in range(8):
                    yl.inc(sync.dma_start(out=gbuf[:, c * TT:(c + 1) * TT],
                                          in_=gates[c, :, t * TT:(t + 1) * TT]), 16)
                for c in range(4):
                    bq, colq = divmod(t * TT, 2048)
                    yl.inc(sync.dma_start(out=yfm[:, c * TT:(c + 1) * TT],
                                          in_=ytile[bq * 4 + c, :, colq:colq + TT]), 16)

            @block.scalar
            def _(scalar, t=t):
                yl.wt(scalar)
                rz.wt(scalar)
                for c in range(4):
                    rz.inc(scalar.activation(ysq[:, c * TT:(c + 1) * TT],
                                             yfm[:, c * TT:(c + 1) * TT], AF.Square))

            @block.tensor
            def _(tensor, t=t):
                rz.wt(tensor)
                if t >= 1:
                    hv.wt(tensor)
                for c in range(4):
                    ins = tensor.matmul(ps_h[0:8, :], hpr[:, c * 8:(c + 1) * 8],
                                        ysq[:, c * TT:(c + 1) * TT],
                                        start=(c == 0), stop=(c == 3))
                mm.inc(ins)

            @block.scalar
            def _(scalar):
                mm.wt(scalar)
                rz.inc(scalar.activation(rrow[:], ps_h[0:8, :], AF.Ln, bias=eps2[0:8, 0:1],
                                  scale=1.0 / 64))
                rz.wt(scalar)
                rz.inc(scalar.activation(rrow[:], rrow[:], AF.Exp, scale=-0.5))

            for c in range(4):
                @block.tensor
                def _(tensor, c=c):
                    rz.wt(tensor)
                    hv.wt(tensor)
                    mm.inc(tensor.matmul(ps_h[:], hpb[:, c * 128:(c + 1) * 128],
                                         rrow[:], start=True, stop=True))

                @block.vector
                def _(vector, c=c):
                    mm.wt(vector)
                    hv.inc(vector.tensor_tensor(ynf[:, c * TT:(c + 1) * TT],
                                                yfm[:, c * TT:(c + 1) * TT],
                                                ps_h[:], ALU.mult))

            for o in range(KD):
                pp = psums[o % 2]

                @block.tensor
                def _(tensor, o=o, pp=pp, t=t):
                    hv.wt(tensor)
                    if o < 2:
                        if t >= 1 or True:
                            ev.wt(tensor)
                    else:
                        ev.wt(tensor, ev.n - 1)
                    for c in range(4):
                        ins = tensor.matmul(pp[:], wob[:, c * D + o * 128: c * D + (o + 1) * 128],
                                            ynf[:, c * TT:(c + 1) * TT],
                                            start=(c == 0), stop=(c == 3))
                    mm.inc(ins)

                @block.vector
                def _(vector, o=o, pp=pp):
                    mm.wt(vector)
                    ev.inc(vector.tensor_tensor(y2[:, o * TT:(o + 1) * TT], pp[:],
                                                gbuf[:, o * TT:(o + 1) * TT], ALU.mult))
                    ev.wt(vector)
                    ev.inc(vector.tensor_tensor(y2[:, o * TT:(o + 1) * TT],
                                                y2[:, o * TT:(o + 1) * TT],
                                                skel.xfm[:, o * TT:(o + 1) * TT], ALU.add))

            for o in range(KD):
                @block.scalar
                def _(scalar, o=o):
                    ev.wt(scalar)
                    rz.wt(scalar)
                    rz.inc(scalar.activation(zsq[:], y2[:, o * TT:(o + 1) * TT], AF.Square))

                @block.tensor
                def _(tensor, o=o):
                    rz.wt(tensor)
                    mm.inc(tensor.matmul(ps_r[0:1, :], skel.ones128[:], zsq[:],
                                         start=(o == 0), stop=(o == KD - 1)))

            @block.scalar
            def _(scalar):
                mm.wt(scalar)
                rz.inc(scalar.activation(zrow[:], ps_r[0:1, :], AF.Ln, bias=eps2[0:1, 0:1],
                                  scale=1.0 / 1024))
                rz.wt(scalar)
                rz.inc(scalar.activation(zrow[:], zrow[:], AF.Exp, scale=-0.5))

            @block.tensor
            def _(tensor):
                rz.wt(tensor)
                hv.wt(tensor)
                mm.inc(tensor.matmul(ps_h[:], skel.ones_row[:], zrow[:],
                                     start=True, stop=True))

            for o in range(KD):
                @block.vector
                def _(vector, o=o):
                    mm.wt(vector)
                    hv.wt(vector)
                    hv.inc(vector.tensor_tensor(zsq[:], y2[:, o * TT:(o + 1) * TT],
                                                ps_h[:], ALU.mult))
                    hv.wt(vector)
                    hv.inc(vector.tensor_scalar(zfm[:, o * TT:(o + 1) * TT], zsq[:],
                                                ffg[:, o:o + 1], None, ALU.mult))

            def ev_f1(f, pst, t=t):
                @block.scalar
                def _(scalar, f=f, pst=pst):
                    chs["mm"].wt(scalar)
                    chs["ev"].inc(scalar.activation(hsw[:, f * TT:(f + 1) * TT], pst[:],
                                                    AF.Silu))
            emit_stream_matmul(b, block, skel, src_fm=zfm, src_tt=TT, kc=8, fc=32,
                               wtiles=wf1b, wbufs=wbufs, psums=psums, chs=chs,
                               evac=ev_f1, t=t, first_tile=(t == 0), wait_src_ch=hv,
                               wready_ch=wp)

            def ev_f3(f, pst, t=t):
                @block.vector
                def _(vector, f=f, pst=pst):
                    chs["mm"].wt(vector)
                    chs["ev"].inc(vector.scalar_tensor_tensor(
                        hsw[:, f * TT:(f + 1) * TT], pst[:], 1.0,
                        hsw[:, f * TT:(f + 1) * TT], ALU.mult, ALU.mult))
            emit_stream_matmul(b, block, skel, src_fm=zfm, src_tt=TT, kc=8, fc=32,
                               wtiles=wf3b, wbufs=wbufs, psums=psums, chs=chs,
                               evac=ev_f3, t=t, first_tile=False, wait_src_ch=chs["ev"])

            def ev_f2(f, pst, t=t):
                @block.vector
                def _(vector, f=f, pst=pst):
                    chs["mm"].wt(vector)
                    chs["ev"].inc(vector.tensor_tensor(ofm[:, f * TT:(f + 1) * TT], pst[:],
                                                       y2[:, f * TT:(f + 1) * TT], ALU.add))
            emit_stream_matmul(b, block, skel, src_fm=hsw, src_tt=TT, kc=32, fc=8,
                               wtiles=wf2b, wbufs=wbufs, psums=psums, chs=chs,
                               evac=ev_f2, t=t, first_tile=False, wait_src_ch=chs["ev"])

            skel.epilogue(t, ofm, chs["ev"])
        skel.finish()
    return b


def l0_reshard_qkv(parts_dict, key):
    outs = []
    for j in range(NCORES):
        outs.append(np.ascontiguousarray(
            np.stack([parts_dict[r][key][j] for r in range(NCORES)], 0)))
    return outs


def l0_reshard_bb(parts):
    outs = []
    for j in range(NCORES):
        outs.append(np.ascontiguousarray(
            np.concatenate([parts[r][2 * j:2 * j + 2] for r in range(NCORES)], 0)))
    return outs


def l0_reshard_y(y_parts):
    outs = []
    for c in range(NCORES):
        outs.append(np.ascontiguousarray(np.stack(
            [y_parts[r][:, c * 2048:(c + 1) * 2048] for r in range(NCORES)], 0)))
    return outs


def inmaps_l0attn_a(P, resid_full):
    a = resid_full.reshape(2, 16384, D)
    parts = []
    for c in range(NCORES):
        seg = []
        for bb in range(2):
            lo = c * 2048 - 128
            halo = np.zeros((128, D), np.float32) if lo < 0 else a[bb, lo:c * 2048]
            seg.append(np.concatenate([halo, a[bb, c * 2048:(c + 1) * 2048]], 0))
        parts.append(np.ascontiguousarray(np.concatenate(seg, 0)))
    g = lambda n: P['L0_attn_' + n]
    base = {
        "wq": g('Wq'), "wk": g('Wk'), "wv": g('Wv'), "wsp": g('Wspike'),
        "wau": g('Wau'), "wad": g('Wad'), "wbeta": g('Wbeta'),
        "wu1": g('Wu1'), "wu2": g('Wu2'),
        "ginw": g('norm_in_w').reshape(8, 128),
        "convw": g('conv_w').reshape(8, 128, 1, 4)[:, :, 0, :],
        "convb": g('conv_b').reshape(8, 128),
        "bau": g('bau').reshape(64, 1), "bad": g('bad').reshape(4, 128),
        "bbeta": g('bbeta').reshape(8, 1),
        "bu1": g('bu1').reshape(64, 1), "bu2": g('bu2').reshape(8, 128),
        "identf": np.eye(128, dtype=np.float32),
    }
    return [dict(base, resid=parts[c]) for c in range(NCORES)]


def inmaps_l0attn_b(P, aparts):
    qres = {k: l0_reshard_qkv(aparts, k) for k in ("qs", "ks", "vs", "ds", "aas")}
    bbr = l0_reshard_bb([aparts[c]["bbs"] for c in range(NCORES)])
    asp = P['L0_attn_alpha_spike'][0]
    bsp = P['L0_attn_beta_spike'][0]
    masku = np.tile(np.triu(np.ones((64, 64), np.float32), 1), (2, 1))
    maskl = np.tile(np.tril(np.ones((64, 64), np.float32), -1), (2, 1))
    maskui = np.tile(np.triu(np.ones((64, 64), np.float32), 0), (2, 1))
    pat2 = np.zeros((128, 2), np.float32)
    pat2[0:64, 0] = 1; pat2[64:128, 1] = 1
    pat2t = np.zeros((2, 128), np.float32)
    pat2t[0, 0:64] = 1; pat2t[1, 64:128] = 1
    ims = []
    for j in range(NCORES):
        p0, p1 = 2 * j, 2 * j + 1
        aspc = np.concatenate([asp[p0 % 8], asp[p1 % 8]]).reshape(128, 1)
        bsppat = np.zeros((128, 2), np.float32)
        bsppat[0:64, 0] = bsp[p0 % 8]
        bsppat[64:128, 1] = bsp[p1 % 8]
        ims.append({
            "q_in": qres["qs"][j], "k_in": qres["ks"][j], "v_in": qres["vs"][j],
            "d_in": qres["ds"][j], "aa_in": qres["aas"][j], "bb_in": bbr[j],
            "asp_col": np.ascontiguousarray(aspc), "bsppat": bsppat,
            "pat2": pat2, "pat2t": pat2t,
            "masku": masku, "maskl": maskl, "maskui": maskui,
            "identf": np.eye(128, dtype=np.float32),
        })
    return ims


def inmaps_l0attn_c(P, resid_full, yparts, gates_parts):
    parts = shard_rows(resid_full)
    g = lambda n: P['L0_attn_' + n]
    hpatr = np.zeros((4, 128, 8), np.float32)
    hpatb = np.zeros((4, 8, 128), np.float32)
    for c in range(4):
        for sub in range(2):
            h = 2 * c + sub
            hpatr[c, sub * 64:(sub + 1) * 64, h] = 1
            hpatb[c, h, sub * 64:(sub + 1) * 64] = 1
    base = {
        "wo": g('Wo'),
        "hnwv": g('head_norm_w').reshape(4, 128),
        "ffw": g('ff_norm_w').reshape(8, 128),
        "hpatr": hpatr, "hpatb": hpatb,
        "wf1t": tiles4(g('Wff1'), 8, 32),
        "wf3t": tiles4(g('Wff3'), 8, 32),
        "wf2t": tiles4(g('Wff2'), 32, 8),
        "identf": np.eye(128, dtype=np.float32),
    }
    yt = l0_reshard_y(yparts)
    return [dict(base, resid=parts[c], ytile=yt[c], gates=gates_parts[c])
            for c in range(NCORES)]



import os as _os
FF = 4096
DBG = False
SKIP_PREP = False
SKIP_MAIN = False
PH_FLAG = 9
PH = 9
def build_l1ffn_k(b=None):
    if b is None:
        b = B()
    b.prefix = "l1ffn_k_"
    nc = b.nc
    resid = b.din("resid", [ROWS, D])
    w1t = b.din("w1t", [D, FF])
    w2t = b.din("w2t", [FF, D])
    b1 = b.din("b1", [32, 128])
    b2 = b.din("b2", [8, 128])
    identf = b.din("identf", [128, 128])
    out = b.dout("out", [ROWS, D])
    dbg_w1 = b.dout("dbg_w1", [128, FF], BF16)
    dbg_xn = b.dout("dbg_xn", [128, D], BF16)
    dbg_xfm = b.dout("dbg_xfm", [128, 512], BF16)
    dbg_hfm = b.dout("dbg_hfm", [128, 512], BF16)
    dbg_ofm = b.dout("dbg_ofm", [128, 512])
    dbg_rs = b.dout("dbg_rs", [128, 1])
    dbg_asum = b.dout("dbg_asum", [128, 40])

    with b.seg():
        block = b.block_open()
        ld = b.ch("ld"); wp = b.ch("wp"); mm1 = b.ch("mm1"); ev1 = b.ch("ev1")
        ldw = b.ch("ldw"); ldx = b.ch("ldx"); dbg = b.ch("dbg")
        mm2 = b.ch("mm2"); ev2 = b.ch("ev2"); tr = b.ch("tr"); ad = b.ch("ad")
        sto = b.ch("sto"); ln = b.ch("ln"); fmc = b.ch("fmc"); aux = b.ch("aux")

        w1b = b.sb([128, KD * FF], BF16)
        w2b = b.sb([128, KF * D], BF16)
        b1s = b.sb([128, 32])
        b2s = b.sb([128, 8])
        identb = b.sb([128, 128], BF16)
        idf = b.sb([128, 128])
        ones128 = b.sb([128, 1])
        ones_row = b.sb([1, 128])
        rs = b.sb([128, 1])
        rsn = b.sb([128, 1])
        epsb = b.sb([128, 1])

        @block.sync
        def _(sync):
            ld.inc(sync.dma_start(out=idf[:], in_=identf), 16)

        @block.vector
        def _(vector):
            ld.wt(vector)
            aux.inc(vector.tensor_copy(identb[:], idf[:]))
            aux.inc(vector.memset(ones128[:], 1.0))
            aux.inc(vector.memset(ones_row[:], 1.0))
            aux.inc(vector.memset(epsb[:], 1e-5))

        @block.scalar
        def _(scalar):
            aux.wt(scalar)
            ln.inc(scalar.nop())

        @block.vector
        def _(vector):
            aux.wt(vector)
            ln.inc(vector.engine_nop())

        if SKIP_PREP:
            @block.vector
            def _(vector):
                wp.inc(vector.memset(w1b[:], 1.0))
                wp.inc(vector.memset(w2b[:], 1.0))
                wp.inc(vector.memset(b1s[:], 0.0))
                wp.inc(vector.memset(b2s[:], 0.0))
        elif True:
          with ExitStack() as prep:
            wld = prep.enter_context(nc.sbuf_tensor("wld", [128, FF], F32))
            wldi = prep.enter_context(nc.sbuf_tensor("wldi", [128, FF], I32))
            asum = prep.enter_context(nc.sbuf_tensor("asum", [128, 40], F32))
            scl = prep.enter_context(nc.sbuf_tensor("scl", [1, 4], F32))
            ps_s = prep.enter_context(nc.psum_tensor("ps_s", [128, 8], F32))
            ps_bias = prep.enter_context(nc.psum_tensor("ps_bias", [128, 40], F32))

            for (wt_ap, nchunk, ncols, wres, rs_dst, nelem) in (
                    (w1t, KD, FF, w1b, rs, D * FF),
                    (w2t, KF, D, w2b, rs, FF * D)):
                for c in range(nchunk):
                    @block.sync
                    def _(sync, c=c):
                        wp.wt(sync)
                        ldw.wt(sync)
                        ldw.inc(sync.dma_start(out=wld[:, 0:ncols],
                                              in_=wt_ap[c * 128:(c + 1) * 128, :]), 16)

                    @block.scalar
                    def _(scalar, c=c):
                        ldw.wt(scalar)
                        wp.inc(scalar.activation(wldi[:, 0:ncols].bitcast(F32), wld[:, 0:ncols],
                                                 AF.Abs, accum_out=asum[:, c:c + 1]))
                @block.vector
                def _(vector):
                    wp.wt(vector)
                    wp.inc(vector.reduce_sum(asum[:, 38:39], asum[:, 0:nchunk], axis=AX.X))

                @block.tensor
                def _(tensor):
                    wp.wt(tensor)
                    mm1.inc(tensor.matmul(ps_s[0:1, 0:1], asum[:, 38:39], ones128[:],
                                          start=True, stop=True))

                @block.vector
                def _(vector):
                    mm1.wt(vector)
                    wp.inc(vector.tensor_scalar(scl[0:1, 0:1], ps_s[0:1, 0:1], 1.0 / nelem, 1e-8,
                                         ALU.mult, ALU.max))
                    wp.wt(vector)
                    wp.inc(vector.tensor_scalar(scl[0:1, 1:2], scl[0:1, 0:1], 0.5, None, ALU.mult))

                @block.tensor
                def _(tensor):
                    wp.wt(tensor)
                    mm1.inc(tensor.matmul(ps_s[:, 1:2], ones_row[:],
                                          scl[0:1, 1:2], start=True, stop=True))

                @block.vector
                def _(vector):
                    mm1.wt(vector)
                    wp.inc(vector.tensor_copy(rs_dst[:], ps_s[:, 1:2]))
                    wp.wt(vector)
                    wp.inc(vector.tensor_scalar(rsn[:], rs_dst[:], -1.0, None, ALU.mult))

                for c in range(nchunk):
                    @block.sync
                    def _(sync, c=c):
                        wp.wt(sync)
                        ldw.wt(sync)
                        ldw.inc(sync.dma_start(out=wld[:, 0:ncols],
                                              in_=wt_ap[c * 128:(c + 1) * 128, :]), 16)

                    @block.vector
                    def _(vector, c=c):
                        ldw.wt(vector)
                        wp.wt(vector)
                        wp.inc(vector.tensor_scalar(wldi[:, 0:ncols].bitcast(F32), wld[:, 0:ncols],
                                                    rsn[:, 0:1], None, ALU.is_lt))
                        wp.wt(vector)
                        wp.inc(vector.scalar_tensor_tensor(wres[:, c * ncols:(c + 1) * ncols],
                                                           wld[:, 0:ncols], rs_dst[:, 0:1],
                                                           wldi[:, 0:ncols].bitcast(F32),
                                                           ALU.is_gt, ALU.subtract))

            @block.sync
            def _(sync):
                wp.wt(sync)
                ldw.wt(sync)
                ldw.inc(sync.dma_start(out=wld[0:32, 0:128], in_=b1), 16)
                ldw.inc(sync.dma_start(out=wld[0:8, 128:256], in_=b2), 16)

            @block.tensor
            def _(tensor):
                ldw.wt(tensor)
                mm1.inc(tensor.transpose(ps_bias[:, 0:32], wld[0:32, 0:128], idf[0:32, 0:32]))
                mm1.inc(tensor.transpose(ps_bias[:, 32:40], wld[0:8, 128:256], idf[0:8, 0:8]))

            @block.vector
            def _(vector):
                mm1.wt(vector)
                wp.inc(vector.tensor_copy(b1s[:], ps_bias[:, 0:32]))
                wp.inc(vector.tensor_copy(b2s[:], ps_bias[:, 32:40]))

        xr4 = b.sb([128, 4 * D])
        sq = b.sb([128, D])
        stat = b.sb([128, 16])
        xnb = b.sb([128, D], BF16)
        xfm = b.sb([128, KD * TT], BF16)
        hfm = b.sb([128, KF * TT], BF16)
        ofm = b.sb([128, 8 * TT])
        ps_t = (b.psum([128, 128], BF16), b.psum([128, 128], BF16))
        ps_tf = b.psum([128, 128])
        ps_tf2 = b.psum([128, 128])
        ps_a = b.psum([128, TT])
        ps_b = b.psum([128, TT])
        ps_c = b.psum([128, TT])
        ps_d = b.psum([128, TT])

        for t in range(NT if not SKIP_MAIN else 0):
            rows = resid[t * TT:(t + 1) * TT, :].rearrange("(s p) d -> p s d", p=128)
            orow = out[t * TT:(t + 1) * TT, :].rearrange("(s p) d -> p s d", p=128)

            @block.sync
            def _(sync, t=t, rows=rows):
                if t == 0:
                    mm1.wt(sync)
                    wp.wt(sync)
                else:
                    sto.wt(sync)
                ldx.inc(sync.dma_start(out=xr4.rearrange("p (s d) -> p s d", s=4), in_=rows), 16)

            for s in range(4):
                if s == 0 and t >= 1:
                    @block.vector
                    def _(vector, t=t):
                        mm1.wt(vector)
                        if t == 1 and DBG:
                            dbg.wt(vector)
                        ln.inc(vector.engine_nop())
                emit_ln_subtile(b, block, xr4[:, s * D:(s + 1) * D], xnb[:], sq, stat, ldx, ln, eps_ap=epsb[:])
                if PH < 2:
                    continue
                emit_tm_to_fm(b, block, xnb[:],
                              lambda c, s=s: xfm[:, c * TT + s * 128: c * TT + (s + 1) * 128],
                              identb[:], ps_t, tr, fmc, ready_ch=ln)
                @block.vector
                def _(vector):
                    tr.wt(vector)
                    ln.inc(vector.engine_nop())

            if t == 0 and DBG:
                @block.gpsimd
                def _(g):
                    fmc.wt(g)
                    wp.wt(g)
                    dbg.inc(g.dma_start(out=dbg_w1, in_=w1b[:, 0:FF]), 16)
                    dbg.inc(g.dma_start(out=dbg_rs, in_=rs[:]), 16)
                    dbg.inc(g.dma_start(out=dbg_xn, in_=xnb[:]), 16)
                    dbg.inc(g.dma_start(out=dbg_xfm, in_=xfm[:, 0:512]), 16)

            for f in range(KF if PH >= 3 else 0):
                pst = ps_a if f % 2 == 0 else ps_b

                @block.tensor
                def _(tensor, f=f, pst=pst, t=t):
                    if f < 2:
                        fmc.wt(tensor)
                        wp.wt(tensor)
                        if t >= 1:
                            ev1.wt(tensor)
                    else:
                        ev1.wt(tensor, ev1.n - 1)
                    for c in range(KD):
                        ins = tensor.matmul(pst[:], w1b[:, c * FF + f * 128: c * FF + (f + 1) * 128],
                                            xfm[:, c * TT:(c + 1) * TT],
                                            start=(c == 0), stop=(c == KD - 1))
                    mm1.inc(ins)

                @block.scalar
                def _(scalar, f=f, pst=pst, t=t):
                    if f == 0 and t >= 1:
                        mm2.wt(scalar)
                    mm1.wt(scalar)
                    ev1.inc(scalar.activation(hfm[:, f * TT:(f + 1) * TT], pst[:],
                                              AF.Gelu, bias=b1s[:, f:f + 1]))

            if t == 0 and DBG:
                @block.gpsimd
                def _(g):
                    ev1.wt(g)
                    dbg.inc(g.dma_start(out=dbg_hfm, in_=hfm[:, 0:512]), 16)

            for o in range(KD if PH >= 4 else 0):
                pst = ps_c if o % 2 == 0 else ps_d

                @block.tensor
                def _(tensor, o=o, pst=pst, t=t):
                    if o < 2:
                        ev1.wt(tensor)
                        if t >= 1:
                            ev2.wt(tensor)
                    else:
                        ev2.wt(tensor, ev2.n - 1)
                    for f in range(KF):
                        ins = tensor.matmul(pst[:], w2b[:, f * D + o * 128: f * D + (o + 1) * 128],
                                            hfm[:, f * TT:(f + 1) * TT],
                                            start=(f == 0), stop=(f == KF - 1))
                    mm2.inc(ins)

                @block.vector
                def _(vector, o=o, pst=pst, t=t):
                    if o == 0 and t >= 1:
                        tr.wt(vector)
                    mm2.wt(vector)
                    ev2.inc(vector.tensor_scalar(ofm[:, o * TT:(o + 1) * TT], pst[:],
                                                 b2s[:, o:o + 1], None, ALU.add))

            if t == 0 and DBG:
                @block.gpsimd
                def _(g):
                    ev2.wt(g)
                    dbg.inc(g.dma_start(out=dbg_ofm, in_=ofm[:, 0:512]), 16)

            for o in range(KD if PH >= 5 else 0):
                for s in range(4):
                    i = o * 4 + s
                    pstf = ps_tf if i % 2 == 0 else ps_tf2

                    @block.tensor
                    def _(tensor, o=o, s=s, i=i, pstf=pstf, t=t):
                        if i == 0:
                            ev2.wt(tensor)
                            if t >= 1:
                                ad.wt(tensor)
                        elif i >= 2:
                            ad.wt(tensor, ad.n - 1)
                        tr.inc(tensor.transpose(pstf[:],
                                               ofm[:, o * TT + s * 128: o * TT + (s + 1) * 128],
                                               idf[:]))

                    @block.vector
                    def _(vector, o=o, s=s, pstf=pstf):
                        tr.wt(vector)
                        dst = xr4[:, s * D + o * 128: s * D + (o + 1) * 128]
                        ad.inc(vector.tensor_tensor(dst, dst, pstf[:], ALU.add))

            @block.sync
            def _(sync, t=t, orow=orow):
                ad.wt(sync)
                sto.inc(sync.dma_start(out=orow, in_=xr4.rearrange("p (s d) -> p s d", s=4)), 16)

        if SKIP_MAIN:
            @block.sync
            def _(sync):
                wp.wt(sync)
                sto.inc(sync.dma_start(out=out[0:128, :], in_=w1b[:, 0:2048].bitcast(F32)), 16)
        @block.sync
        def _(sync):
            sto.wt(sync)

        if DBG:
            @block.gpsimd
            def _(g):
                dbg.wt(g)
    return b




def inmaps_l1ffn_k(P, resid_full):
    parts = shard_rows(resid_full)
    base = {
        "w1t": np.ascontiguousarray(P["L1_ffn_W1"].T),
        "w2t": np.ascontiguousarray(P["L1_ffn_W2"].T),
        "b1": P["L1_ffn_b1"].reshape(32, 128),
        "b2": P["L1_ffn_b2"].reshape(8, 128),
        "identf": np.eye(128, dtype=np.float32),
    }
    return [dict(base, resid=parts[c]) for c in range(NCORES)]


_BUILD_CACHE = {}


def _get(name, fn):
    if name not in _BUILD_CACHE:
        _BUILD_CACHE[name] = fn()
    return _BUILD_CACHE[name]


def _run(b, im):
    res = run_bass_kernel_spmd(b.nc, im, core_ids=list(range(NCORES)))
    return res.results


def kernel(**inputs):
    P = inputs
    resid = np.ascontiguousarray(P["L0_attn_x"].reshape(32768, D).astype(np.float32))

    outsA = ["qs", "ks", "vs", "ds", "aas", "bbs", "gates"]
    r = _run(_get("l0a", build_l0attn_a), inmaps_l0attn_a(P, resid))
    aparts = [{k: r[c][k] for k in outsA} for c in range(NCORES)]
    r = _run(_get("l0b", build_l0attn_b), inmaps_l0attn_b(P, aparts))
    yparts = [r[j]["ys"] for j in range(NCORES)]
    gates_parts = [aparts[c]["gates"] for c in range(NCORES)]
    r = _run(_get("l0c", build_l0attn_c), inmaps_l0attn_c(P, resid, yparts, gates_parts))
    resid = np.concatenate([r[c]["out"] for c in range(NCORES)], 0)
    del aparts, yparts, gates_parts

    def from_parts(rr):
        return np.concatenate([rr[c]["out"] for c in range(NCORES)], 0)

    def sharded(xcat):
        return [xcat[c * ROWS:(c + 1) * ROWS] for c in range(NCORES)]

    def im_with_resid(base_ims, xcat):
        sh = sharded(xcat)
        for c in range(NCORES):
            base_ims[c]["resid"] = np.ascontiguousarray(sh[c])
        return base_ims

    ims = inmaps_l0ffn(P, np.zeros((32768, D), np.float32))
    r = _run(_get("l0f", build_l0ffn), im_with_resid(ims, resid))
    resid = from_parts(r)

    ims = inmaps_l1attn(P, np.zeros((32768, D), np.float32))
    r = _run(_get("l1a", build_l1attn), im_with_resid(ims, resid))
    resid = from_parts(r)

    ims = inmaps_l1ffn_k(P, np.zeros((32768, D), np.float32))
    r = _run(_get("l1f", build_l1ffn_k), im_with_resid(ims, resid))
    resid = from_parts(r)

    imA = inmaps_l2attn_a_fromcat(P, resid)
    r = _run(_get("l2a", build_l2attn_a), imA)
    uparts = [r[c]["uscr"] for c in range(NCORES)]
    zparts = [r[c]["zscr"] for c in range(NCORES)]
    bcd = assemble_l2_bcd([r[c]["bcd_out"] for c in range(NCORES)])
    uin = reshard_l2_u(uparts)
    logA = P["L2_attn_log_A"]; dparam = P["L2_attn_D_param"]
    imB = [{"u_in": uin[j], "bcd": bcd, "logA": np.ascontiguousarray(logA[j * 128:(j + 1) * 128]),
            "dparam": np.ascontiguousarray(dparam[j * 128:(j + 1) * 128].reshape(128, 1))}
           for j in range(NCORES)]
    r = _run(_get("l2b", build_l2attn_b), imB)
    yt = reshard_l2_y([r[j]["y_out"] for j in range(NCORES)])
    sh = sharded(resid)
    imC = [{"resid": np.ascontiguousarray(sh[c]), "ytile": yt[c], "zscr": zparts[c],
            "woutt": np.ascontiguousarray(P["L2_attn_W_out"]),
            "identf": np.eye(128, dtype=np.float32)} for c in range(NCORES)]
    r = _run(_get("l2c", build_l2attn_c), imC)
    resid = from_parts(r)

    ims = inmaps_l2ffn(P, np.zeros((32768, D), np.float32))
    r = _run(_get("l2f", build_l2ffn), im_with_resid(ims, resid))
    resid = from_parts(r)

    sh = sharded(resid)
    imA = inmaps_l3attn_a(P, np.zeros((32768, D), np.float32))
    for c in range(NCORES):
        imA[c]["resid"] = np.ascontiguousarray(sh[c])
    r = _run(_get("l3a", build_l3attn_a), imA)
    av_all = np.ascontiguousarray(np.stack([r[c]["av_out"] for c in range(NCORES)], 0))
    attnT_parts = [r[c]["attnT"] for c in range(NCORES)]
    imB = inmaps_l3attn_b(P, np.zeros((32768, D), np.float32), av_all, attnT_parts)
    for c in range(NCORES):
        imB[c]["resid"] = np.ascontiguousarray(sh[c])
    r = _run(_get("l3b", build_l3attn_b), imB)
    resid = from_parts(r)

    ims = inmaps_l3ffn(P, np.zeros((32768, D), np.float32))
    r = _run(_get("l3f", build_l3ffn), im_with_resid(ims, resid))
    resid = from_parts(r)

    b0 = np.concatenate([resid[c * ROWS:(c * ROWS) + 2048] for c in range(NCORES)], 0)
    b1 = np.concatenate([resid[c * ROWS + 2048:(c + 1) * ROWS] for c in range(NCORES)], 0)
    return np.ascontiguousarray(np.stack([b0, b1], 0))


def inmaps_l2attn_a_fromcat(P, resid_cat):
    parts = [resid_cat[c * ROWS:(c + 1) * ROWS] for c in range(NCORES)]
    tails = l2a_tails_from_parts(parts)
    base = {
        "wintile": P["L2_attn_W_in"],
        "wxp": P["L2_attn_W_xproj"],
        "convw": np.ascontiguousarray(P["L2_attn_conv_w"].reshape(8, 128, 1, 4)[:, :, 0, :]),
        "convb": P["L2_attn_conv_b"].reshape(8, 128),
        "identf": np.eye(128, dtype=np.float32),
        "tails": tails,
    }
    ims = []
    for c in range(NCORES):
        sm = np.zeros((128, 8), np.float32)
        if c > 0:
            sm[:, c - 1] = 1.0
        ims.append(dict(base, resid=np.ascontiguousarray(parts[c]), selmask=sm))
    return ims
